# version 08
# baseline (speedup 1.0000x reference)
import math
import contextlib
import numpy as np
import ml_dtypes
import concourse.bass as bass
import concourse.mybir as mybir
from concourse.bass_utils import run_bass_kernel_spmd

F32 = mybir.dt.float32
BF16 = mybir.dt.bfloat16
F8 = mybir.dt.float8e4
AF = mybir.ActivationFunctionType
ALU = mybir.AluOpType
AXX = mybir.AxisListType.X

N_CORES = 8
HID = 2048
INTER = 6144
NQH = 16
NKV = 8
HD = 128
WINDOW = 1024
C_SCALE = 1.0 / math.sqrt(HD)
LN_EPS = 1e-5
RMS_EPS = 1e-6
MASK_NEG = -240.0
NEG_BIG = -3.0e38

ENGS = ("sync", "scalar", "vector", "tensor", "gpsimd")


class Sched:

    def __init__(self):
        self.steps = {e: [] for e in ENGS}
        self.ctr = {}
        self.names = []

    def sem(self, name):
        if name not in self.ctr:
            self.ctr[name] = 0
            self.names.append(name)
        return name

    def val(self, name):
        self.sem(name)
        return self.ctr[name]

    def add(self, eng, fn, waits=(), inc=None, inc_n=1, tag=None):
        w = tuple((self.sem(s), v) for (s, v) in waits if v > 0)
        if inc is not None:
            self.sem(inc)
            self.ctr[inc] += inc_n
        self.steps[eng].append((w, fn, inc, inc_n, tag))

    def validate(self):
        for e in ENGS:
            for (waits, _f, _i, _n, _t) in self.steps[e]:
                for (s, v) in waits:
                    assert v <= self.ctr[s], f"unsatisfiable wait {s}>={v} (max {self.ctr[s]})"

    def emit(self, eng_name, eng, sems):
        for (waits, fn, inc, inc_n, tag) in self.steps[eng_name]:
            for (s, v) in waits:
                eng.wait_ge(sems[s], v)
            inst = fn(eng)
            if tag and inst is not None:
                inst.annotate(tag)
            if inc is not None:
                assert inst is not None, f"step with inc={inc} returned None"
                inst.then_inc(sems[inc], inc_n)


def build_program(S_CORE):
    S_TOT = N_CORES * S_CORE
    NTT = S_CORE // 128
    NKT = S_TOT // 128
    NGRP = NKT // 4
    NU = NQH * NTT
    NCH = 4
    GPC = NGRP // NCH
    assert NGRP % NCH == 0 and NGRP >= 4
    NOC = INTER // 128
    NT5 = S_CORE // 512
    NQO = HID // 512
    NKO = (NKV * HD) // 512
    assert S_CORE % 512 == 0

    nc = bass.Bass("TRN2", target_bir_lowering=False, debug=False)
    ap = lambda t: t.ap()
    x_in = ap(nc.dram_tensor("x", [S_CORE, HID], F32, kind="ExternalInput"))
    wq_in = ap(nc.dram_tensor("wq", [HID, HID], F32, kind="ExternalInput"))
    wk_in = ap(nc.dram_tensor("wk", [NKV * HD, HID], F32, kind="ExternalInput"))
    wv_in = ap(nc.dram_tensor("wv", [NKV * HD, HID], F32, kind="ExternalInput"))
    wo_in = ap(nc.dram_tensor("wo", [HID, HID], F32, kind="ExternalInput"))
    gate_in = ap(nc.dram_tensor("gate_w", [HID, INTER], F32, kind="ExternalInput"))
    up_in = ap(nc.dram_tensor("up_w", [HID, INTER], F32, kind="ExternalInput"))
    down_in = ap(nc.dram_tensor("down_w", [INTER, HID], F32, kind="ExternalInput"))
    ident_f = ap(nc.dram_tensor("ident_f", [128, 128], F32, kind="ExternalInput"))
    ident_b = ap(nc.dram_tensor("ident_b", [128, 128], BF16, kind="ExternalInput"))
    ones_r = ap(nc.dram_tensor("ones_r", [1, 128], F32, kind="ExternalInput"))
    qg_in = ap(nc.dram_tensor("qgain", [128, 128], F32, kind="ExternalInput"))
    kg_in = ap(nc.dram_tensor("kgain", [128, 128], F32, kind="ExternalInput"))
    gs_in = ap(nc.dram_tensor("gscale", [128, NOC], F32, kind="ExternalInput"))
    us_in = ap(nc.dram_tensor("uscale", [128, NOC], F32, kind="ExternalInput"))
    ds_in = ap(nc.dram_tensor("dscale", [128, HID], F32, kind="ExternalInput"))
    mask_in = ap(nc.dram_tensor("mask", [128, NTT * 1024], F8, kind="ExternalInput"))
    eps_in = ap(nc.dram_tensor("epsv", [128, 2], F32, kind="ExternalInput"))
    out_d = ap(nc.dram_tensor("out", [S_CORE, HID], F32, kind="ExternalOutput"))
    qsc = ap(nc.dram_tensor("qscratch", [NQH * HD, S_CORE], BF16))
    AGC = max(S_CORE, NKV * HD)
    AGR = NKV * HD + S_CORE
    ag_i = ap(nc.dram_tensor("ag_in", [AGR, AGC], BF16))
    ag_o = ap(nc.dram_tensor("ag_out", [N_CORES * AGR, AGC], BF16))
    uTd = ap(nc.dram_tensor("uTd", [NQH * HD, S_CORE], BF16))

    S = Sched()
    es = contextlib.ExitStack()
    sbE = lambda name, shape, dt: es.enter_context(nc.sbuf_tensor(name, shape, dt))

    with es:
        idf = sbE("idf", [128, 128], F32)
        idb = sbE("idb", [128, 128], BF16)
        onr = sbE("onr", [1, 128], F32)
        qg = sbE("qg", [128, 128], F32)
        kg = sbE("kg", [128, 128], F32)
        gs = sbE("gs", [128, NOC], F32)
        us = sbE("us", [128, NOC], F32)
        mask_sb = sbE("mask_sb", [128, NTT * 1024], F8)
        epsv = sbE("epsv_sb", [128, 2], F32)
        stat = sbE("stat", [128, 96], F32)
        mred = sbE("mred", [1, 2], F32)
        negm = sbE("negm", [128, 2], F32)
        recs = sbE("recs", [128, 4], F32)
        uhat = sbE("uhat", [128, 2 * 256], BF16)
        mpos = sbE("mpos", [128, 2], F32)
        asub = sbE("asub", [128, 2], F32)
        alpha = sbE("alpha", [128, 2], F32)
        maxv = sbE("maxv", [128, 2 * 16], F32)
        ffn_sb = sbE("ffn_sb", [128, NTT * HID], BF16)
        quad = es.enter_context(nc.psum_tensor("quad", [128, 2048], F32))
        pair = es.enter_context(nc.psum_tensor("pair", [128, 1024], F32))
        b6 = es.enter_context(nc.psum_tensor("b6", [128, 512], F32))
        b7 = es.enter_context(nc.psum_tensor("b7", [128, 512], F32))

        b6b = b6[:].bitcast(BF16)
        b7b = b7[:].bitcast(BF16)
        pairb = pair[:].bitcast(BF16)

        def bankq(i):
            return quad[:, i * 512:(i + 1) * 512]

        for dst, src in [(idf[:], ident_f), (idb[:], ident_b), (onr[:], ones_r),
                         (qg[:], qg_in), (kg[:], kg_in), (gs[:], gs_in),
                         (us[:], us_in), (mask_sb[:], mask_in), (epsv[:], eps_in)]:
            S.add("sync", lambda e, d=dst, s=src: e.dma_start(out=d, in_=s),
                  inc="c_ld", inc_n=16)

        es12 = contextlib.ExitStack()
        sb12 = lambda name, shape, dt: es12.enter_context(nc.sbuf_tensor(name, shape, dt))
        x_db = sb12("x_db", [128, 2 * HID], F32)
        xn_tm = sb12("xn_tm", [128, HID], BF16)
        xn_fm = sb12("xn_fm", [128, 16 * S_CORE], BF16)
        wnat = sb12("wnat", [128, 2 * HID], BF16)
        wT = sb12("wT", [128, 16 * 512], BF16)
        qraw = sb12("qraw", [128, 2 * 512], F32)
        g8 = sb12("g8", [128, 2 * HID], F8)
        u8 = sb12("u8", [128, 2 * HID], F8)
        d8 = sb12("d8", [128, 4 * 512], F8)
        silu2 = sb12("silu2", [128, 2 * 512], BF16)
        fuse = sb12("fuse", [128, NOC * 512], BF16)
        ds_bc = sb12("ds_bc", [128, HID], F32)
        zbuf = sb12("zbuf", [128, 512], BF16)
        stg4 = sb12("stg4", [128, 4 * 512], BF16)

        S.add("sync", lambda e: e.dma_start(out=ds_bc[:], in_=ds_in), inc="c_ld", inc_n=16)
        CLD = S.val("c_ld")

        AGC = max(S_CORE, NKV * HD)
        if AGC > S_CORE:
            S.add("vector", lambda e: e.memset(zbuf[:], 0.0), inc="p_zm")
            for rb in range(NKV * HD // 128):
                S.add("sync",
                      lambda e, rb=rb: e.dma_start(
                          out=ag_i[rb * 128:(rb + 1) * 128, S_CORE:AGC], in_=zbuf[:, 0:AGC - S_CORE]),
                      waits=[("p_zm", 1)], inc="p_zp", inc_n=16)
            ZP_W = [("p_zp", S.val("p_zp"))]
        else:
            ZP_W = []

        xr = x_in.rearrange("(a p) n -> a p n", p=128)
        xn_c = xn_fm[:, :].rearrange("p (c t) -> p c t", c=16)

        for tt in range(NTT):
            par = tt % 2
            sl = lambda k, w=1, par=par: stat[:, par * 48 + k: par * 48 + k + w]
            xs = x_db[:, par * HID:(par + 1) * HID]
            S.add("sync", lambda e, xs=xs, tt=tt: e.dma_start(out=xs, in_=xr[tt]),
                  waits=[("c_ld", CLD), ("p_ln", tt - 1)], inc=f"p_x{par}", inc_n=16)
            S.add("scalar",
                  lambda e, xs=xs, d=sl(0): e.activation(out=xn_tm[:], in_=xs,
                                                         func=AF.Identity, accum_out=d),
                  waits=[(f"p_x{par}", 16 * (tt // 2 + 1)), ("p_tT", tt * 4)])
            S.add("scalar",
                  lambda e, xs=xs, d=sl(1): e.activation(out=fuse[:, 0:HID].bitcast(BF16) if False else fuse[:, 0:HID], in_=xs,
                                                         func=AF.Square, accum_out=d))
            S.add("scalar", lambda e: e.drain(), inc="p_st")
            PST = S.val("p_st")

            def ln_stats(e, sl=sl):
                e.tensor_scalar(out=sl(2), in0=sl(0), scalar1=1.0 / HID, scalar2=0.0,
                                op0=ALU.mult, op1=ALU.add)
                e.drain()
                e.tensor_tensor(out=sl(4), in0=sl(2), in1=sl(2), op=ALU.mult)
                e.drain()
                return e.scalar_tensor_tensor(out=sl(5), in0=sl(1), scalar=1.0 / HID,
                                              in1=sl(4), op0=ALU.mult, op1=ALU.subtract)
            S.add("vector", ln_stats, waits=[("p_st", PST)], inc="p_ve")
            S.add("scalar", lambda e, sl=sl: e.activation(out=sl(6), in_=sl(5), func=AF.Sqrt,
                                                          bias=epsv[:, 0:1]),
                  waits=[("p_ve", S.val("p_ve"))])
            S.add("scalar", lambda e: e.drain(), inc="p_sq")
            PSQ = S.val("p_sq")

            def ln_apply(e, xs=xs, sl=sl):
                e.reciprocal(sl(7), sl(6))
                e.drain()
                return e.tensor_scalar(out=xn_tm[:], in0=xs, scalar1=sl(2), scalar2=sl(7),
                                       op0=ALU.subtract, op1=ALU.mult)
            S.add("vector", ln_apply, waits=[("p_sq", PSQ), ("p_tT", tt * 4)], inc="p_ln")
            PLN = S.val("p_ln")
            for qtr in range(4):
                for j in range(4):
                    c = qtr * 4 + j
                    w = []
                    if j == 0:
                        w = [("p_xnT", tt * 4 + qtr)]
                        if qtr == 0:
                            w.append(("p_ln", PLN))
                    S.add("tensor",
                          lambda e, c=c, j=j: e.transpose(b6b[:, j * 128:(j + 1) * 128],
                                                          xn_tm[:, c * 128:(c + 1) * 128],
                                                          idb[:]),
                          waits=w, inc=("p_tT" if j == 3 else None))
                S.add("vector",
                      lambda e, tt=tt, qtr=qtr: e.tensor_copy(
                          xn_c[:, qtr * 4:(qtr + 1) * 4, tt * 128:(tt + 1) * 128],
                          b6b[:, 0:512].rearrange("p (c t) -> p c t", c=4)),
                      waits=[("p_tT", S.val("p_tT"))], inc="p_xnT")
        XND = S.val("p_xnT")

        o_specs = ([("q", i) for i in range(NQO)] + [("k", i) for i in range(NKO)]
                   + [("v", i) for i in range(NKO)])
        w_dram = {"q": wq_in, "k": wk_in, "v": wv_in}
        n_stg = 0
        n_qk = 0

        for oi, (kind, osl) in enumerate(o_specs):
            wd = w_dram[kind].rearrange("(a p) n -> a p n", p=128)
            for osub in range(4):
                kpc = oi * 4 + osub
                pw = kpc % 2
                dst = wnat[:, pw * HID:(pw + 1) * HID]
                S.add("gpsimd", lambda e, d=dst, s=wd[osl * 4 + osub]: e.dma_start(out=d, in_=s),
                      waits=[("p_wT", (kpc - 1) * 16), ("c_ld", CLD)],
                      inc=f"p_wn{pw}", inc_n=16)
                PWN = S.val(f"p_wn{pw}")
                for ic in range(16):
                    trn = kpc * 16 + ic
                    off = (trn % 2) * 1024 + ((trn // 2) % 4) * 128
                    S.add("tensor",
                          lambda e, d=dst, ic=ic, off=off: e.transpose(
                              pairb[:, off:off + 128],
                              d[:, ic * 128:(ic + 1) * 128], idb[:]),
                          waits=([(f"p_wn{pw}", PWN)] if ic == 0 else []) +
                                [("p_wTe", trn - 1)],
                          inc="p_wT")
                    S.add("vector",
                          lambda e, ic=ic, osub=osub, off=off: e.tensor_copy(
                              wT[:, ic * 512 + osub * 128: ic * 512 + (osub + 1) * 128],
                              pairb[:, off:off + 128]),
                          waits=[("p_wT", S.val("p_wT"))], inc="p_wTe")
            WTE = S.val("p_wTe")

            for tt in range(NTT):
                bank = tt % 4
                for ic in range(16):
                    w = []
                    if ic == 0:
                        w = [("p_xnT", XND), ("p_wTe", WTE),
                             ("p_pev", S.val("p_pev") - 3)]
                    S.add("tensor",
                          lambda e, bank=bank, ic=ic, tt=tt: e.matmul(
                              bankq(bank), xn_c[:, ic, tt * 128:(tt + 1) * 128],
                              wT[:, ic * 512:(ic + 1) * 512],
                              start=(ic == 0), stop=(ic == 15)),
                          waits=w, inc=("p_mmP" if ic == 15 else None))
                MMP = S.val("p_mmP")
                slot = n_stg % 4
                nuse = n_stg // 4
                n_stg += 1
                stg_s = stg4[:, slot * 512:(slot + 1) * 512]
                if kind == "v":
                    S.add("vector",
                          lambda e, bank=bank, d=stg_s: e.tensor_copy(d, bankq(bank)),
                          waits=[("p_mmP", MMP), (f"p_stg{slot}", 16 * nuse)],
                          inc="p_pev")
                    S.add("scalar",
                          lambda e, s=stg_s, tt=tt, osl=osl: e.dma_start(
                              out=ag_i[NKV * HD + tt * 128: NKV * HD + (tt + 1) * 128,
                                       osl * 512:(osl + 1) * 512], in_=s),
                          waits=[("p_pev", S.val("p_pev"))], inc=f"p_stg{slot}", inc_n=16)
                else:
                    n_qk += 1
                    parq = tt % 2
                    slp = lambda k, w=1, parq=parq: stat[:, parq * 48 + 16 + k: parq * 48 + 16 + k + w]
                    qr = qraw[:, parq * 512:(parq + 1) * 512]
                    S.add("vector",
                          lambda e, bank=bank, qr=qr: e.tensor_copy(qr, bankq(bank)),
                          waits=[("p_mmP", MMP), ("p_rms", n_qk - 2)],
                          inc="p_pev")
                    PEV = S.val("p_pev")
                    for h4 in range(4):
                        S.add("scalar",
                              lambda e, qr=qr, h4=h4, d=slp(h4): e.activation(
                                  out=xn_tm[:, h4 * 128:(h4 + 1) * 128],
                                  in_=qr[:, h4 * 128:(h4 + 1) * 128],
                                  func=AF.Square, accum_out=d),
                              waits=([("p_pev", PEV), ("p_tT", NTT * 4)] if h4 == 0 else []))
                    S.add("scalar", lambda e: e.drain(), inc="p_ss")
                    S.add("vector",
                          lambda e, slp=slp: e.tensor_scalar(
                              out=slp(4, 4), in0=slp(0, 4), scalar1=1.0 / HD,
                              scalar2=0.0, op0=ALU.mult, op1=ALU.add),
                          waits=[("p_ss", S.val("p_ss"))], inc="p_rv")
                    S.add("scalar",
                          lambda e, slp=slp: e.activation(out=slp(8, 4), in_=slp(4, 4),
                                                          func=AF.Sqrt, bias=epsv[:, 1:2]),
                          waits=[("p_rv", S.val("p_rv"))])
                    S.add("scalar", lambda e: e.drain(), inc="p_sq2")

                    def rms_nr(e, slp=slp):
                        e.reciprocal(slp(28, 4), slp(8, 4))
                        return e.drain()
                    S.add("vector", rms_nr, waits=[("p_sq2", S.val("p_sq2"))], inc="p_nr")
                    gt = qg if kind == "q" else kg
                    for h4 in range(4):
                        w = [("p_qT", S.val("p_qT"))] if h4 == 0 else []
                        S.add("vector",
                              lambda e, qr=qr, h4=h4, gt=gt, slp=slp: e.scalar_tensor_tensor(
                                  out=silu2[:, h4 * 128:(h4 + 1) * 128],
                                  in0=qr[:, h4 * 128:(h4 + 1) * 128],
                                  scalar=slp(28 + h4), in1=gt[:],
                                  op0=ALU.mult, op1=ALU.mult),
                              waits=w, inc=("p_rms" if h4 == 3 else None))
                    PRMS = S.val("p_rms")
                    for h4 in range(4):
                        S.add("tensor",
                              lambda e, h4=h4: e.transpose(
                                  b6b[:, h4 * 128:(h4 + 1) * 128],
                                  silu2[:, h4 * 128:(h4 + 1) * 128], idb[:]),
                              waits=([("p_rms", PRMS), ("p_qTe", S.val("p_qTe"))]
                                     if h4 == 0 else []),
                              inc=("p_qT" if h4 == 3 else None))
                    S.add("vector",
                          lambda e, d=stg_s: e.tensor_copy(d, b6b[:, 0:512]),
                          waits=[("p_qT", S.val("p_qT")), (f"p_stg{slot}", 16 * nuse)],
                          inc="p_qTe")
                    if kind == "q":
                        dstq = qsc.rearrange("(h p) t -> p h t", p=128)[
                            :, osl * 4:osl * 4 + 4, tt * 128:(tt + 1) * 128]
                    else:
                        dstq = ag_i[0:NKV * HD, 0:S_CORE].rearrange("(h p) t -> p h t", p=128)[
                            :, osl * 4:osl * 4 + 4, tt * 128:(tt + 1) * 128]
                    S.add("scalar",
                          lambda e, s=stg_s, d=dstq: e.dma_start(
                              out=d, in_=s.rearrange("p (h t) -> p h t", h=4)),
                          waits=[("p_qTe", S.val("p_qTe"))], inc=f"p_stg{slot}", inc_n=16)
        STG_W = [(f"p_stg{i}", S.val(f"p_stg{i}")) for i in range(4)]
        PEV_ALL = S.val("p_pev")
        QTE_ALL = S.val("p_qTe")

        S.add("gpsimd",
              lambda e: e.collective_compute("AllGather", ALU.bypass,
                                             replica_groups=[list(range(N_CORES))],
                                             ins=[ag_i.opt()], outs=[ag_o.opt()]),
              waits=STG_W + ZP_W, inc="p_ag")

        gcol = gate_in.rearrange("(c p) n -> p c n", p=128)
        ucol = up_in.rearrange("(c p) n -> p c n", p=128)
        dcol = down_in.rearrange("(c p) n -> p c n", p=128)
        NOCP = NOC // 2
        ALLB = [quad[:, i * 512:(i + 1) * 512] for i in range(4)] + \
               [pair[:, 0:512], pair[:, 512:1024], b6[:], b7[:]]

        for oph in range(2):
            for ocr in range(NOCP):
                oc = oph * NOCP + ocr
                j = oc
                pw = j % 2
                gds = g8[:, pw * HID:(pw + 1) * HID]
                uds = u8[:, pw * HID:(pw + 1) * HID]
                S.add("gpsimd",
                      lambda e, d=gds, oc=oc: e.dma_start(
                          out=d.rearrange("p (c n) -> p c n", c=16),
                          in_=gcol[:, :, oc * 128:(oc + 1) * 128]),
                      waits=[("p_gmm", (j - 2) * NT5 + NT5)], inc=f"p_g8{pw}", inc_n=16)
                S.add("gpsimd",
                      lambda e, d=uds, oc=oc: e.dma_start(
                          out=d.rearrange("p (c n) -> p c n", c=16),
                          in_=ucol[:, :, oc * 128:(oc + 1) * 128]),
                      waits=[("p_umm", (j - 2) * NT5 + NT5)], inc=f"p_u8{pw}", inc_n=16)
                PG8 = S.val(f"p_g8{pw}")
                PU8 = S.val(f"p_u8{pw}")
                for t5 in range(NT5):
                    j2 = j * NT5 + t5
                    pw2 = j2 % 2
                    gp = pair[:, 0:512] if pw2 == 0 else b6[:]
                    upp = pair[:, 512:1024] if pw2 == 0 else b7[:]
                    for ic in range(16):
                        w = []
                        if ic == 0:
                            w = [(f"p_g8{pw}", PG8), ("p_xnT", XND), ("p_si", j2 - 1)]
                            if j2 <= 1:
                                w += [("p_pev", PEV_ALL), ("p_qTe", QTE_ALL)]
                            if oph == 1 and ocr == 0 and t5 <= 1:
                                w += [("p_dne", oph * 16 * NT5)]
                        S.add("tensor",
                              lambda e, gp=gp, gds=gds, ic=ic, t5=t5: e.matmul(
                                  gp, gds[:, ic * 128:(ic + 1) * 128],
                                  xn_c[:, ic, t5 * 512:(t5 + 1) * 512],
                                  start=(ic == 0), stop=(ic == 15)),
                              waits=w, inc=("p_gmm" if ic == 15 else None))
                    for ic in range(16):
                        w = []
                        if ic == 0:
                            w = [("p_fu", j2 - 1), (f"p_u8{pw}", PU8)]
                            if oph == 1 and ocr == 0 and t5 <= 1:
                                w += [("p_dne", oph * 16 * NT5)]
                        S.add("tensor",
                              lambda e, upp=upp, uds=uds, ic=ic, t5=t5: e.matmul(
                                  upp, uds[:, ic * 128:(ic + 1) * 128],
                                  xn_c[:, ic, t5 * 512:(t5 + 1) * 512],
                                  start=(ic == 0), stop=(ic == 15)),
                              waits=w, inc=("p_umm" if ic == 15 else None))
                    ps = pw2
                    S.add("scalar",
                          lambda e, gp=gp, oc=oc, ps=ps: e.activation(
                              out=silu2[:, ps * 512:(ps + 1) * 512], in_=gp, func=AF.Silu,
                              scale=gs[:, oc:oc + 1]),
                          waits=[("p_gmm", S.val("p_gmm")), ("p_fu", j2 - 1)], inc="p_si")
                    fsl = (ocr * NT5 + t5)
                    S.add("vector",
                          lambda e, upp=upp, oc=oc, ps=ps, fsl=fsl: e.scalar_tensor_tensor(
                              out=fuse[:, fsl * 512:(fsl + 1) * 512], in0=upp,
                              scalar=us[:, oc:oc + 1], in1=silu2[:, ps * 512:(ps + 1) * 512],
                              op0=ALU.mult, op1=ALU.mult),
                          waits=[("p_umm", S.val("p_umm")), ("p_si", S.val("p_si"))] +
                                ([("p_dnm", oph * 4 * NOCP if ocr == 0 and t5 == 0 else 0)]
                                 if oph >= 1 else []),
                          inc="p_fu")
            FUD = S.val("p_fu")
            for hs in range(4):
                for ocr in range(NOCP):
                    oc = oph * NOCP + ocr
                    jd = (oph * 4 + hs) * NOCP + ocr
                    sd = jd % 4
                    dds = d8[:, sd * 512:(sd + 1) * 512]
                    S.add("gpsimd",
                          lambda e, d=dds, oc=oc, hs=hs: e.dma_start(
                              out=d, in_=dcol[:, oc, hs * 512:(hs + 1) * 512]),
                          waits=[("p_dnm", jd - 3), ("p_fu", FUD if ocr == 0 else 0)],
                          inc=f"p_d8{sd}", inc_n=16)
                    PD8 = S.val(f"p_d8{sd}")
                    for t5 in range(NT5):
                        for ts in range(4):
                            bk = t5 * 4 + ts
                            fsl = (ocr * NT5 + t5)
                            w = []
                            if ts == 0 and t5 == 0:
                                w = [(f"p_d8{sd}", PD8), ("p_fu", FUD)]
                                if ocr == 0:
                                    w.append(("p_dne", (oph * 4 + hs) * 4 * NT5))
                            S.add("tensor",
                                  lambda e, bk=bk, fsl=fsl, ts=ts, dds=dds, ocr=ocr: e.matmul(
                                      ALLB[bk], fuse[:, fsl * 512 + ts * 128: fsl * 512 + (ts + 1) * 128],
                                      dds, start=(ocr == 0), stop=(ocr == NOCP - 1)),
                                  waits=w, inc=("p_dnm" if (ts == 3 and t5 == NT5 - 1) else None),
                                  tag=f"dn{oph}.{hs}.{ocr}.{t5}.{ts}")
                DNM = S.val("p_dnm")
                for t5 in range(NT5):
                    for ts in range(4):
                        bk = t5 * 4 + ts
                        tt = t5 * 4 + ts
                        if oph == 0:
                            S.add("vector",
                                  lambda e, bk=bk, tt=tt, hs=hs: e.tensor_tensor(
                                      out=ffn_sb[:, tt * HID + hs * 512: tt * HID + (hs + 1) * 512],
                                      in0=ALLB[bk], in1=ds_bc[:, hs * 512:(hs + 1) * 512],
                                      op=ALU.mult),
                                  waits=[("p_dnm", DNM)], inc="p_dne")
                        else:
                            def ev2(e, bk=bk, tt=tt, hs=hs):
                                qtmp = qraw[:, (bk % 2) * 512:(bk % 2) * 512 + 512]
                                e.tensor_tensor(out=qtmp,
                                                in0=ALLB[bk], in1=ds_bc[:, hs * 512:(hs + 1) * 512],
                                                op=ALU.mult)
                                e.drain()
                                return e.tensor_tensor(
                                    out=ffn_sb[:, tt * HID + hs * 512: tt * HID + (hs + 1) * 512],
                                    in0=qtmp,
                                    in1=ffn_sb[:, tt * HID + hs * 512: tt * HID + (hs + 1) * 512],
                                    op=ALU.add)
                            S.add("vector", ev2, waits=[("p_dnm", DNM)], inc="p_dne")
        GMM_ALL = S.val("p_gmm")
        UMM_ALL = S.val("p_umm")
        DNE_ALL = S.val("p_dne")
        DNM_ALL = S.val("p_dnm")
        FU_ALL = S.val("p_fu")
        SI_ALL = S.val("p_si")

        es12.close()
        es3 = contextlib.ExitStack()
        sb3 = lambda name, shape, dt: es3.enter_context(nc.sbuf_tensor(name, shape, dt))
        sc_sb = sb3("sc_sb", [128, 2 * (NKT // 4) * 256], F32)
        p_sb = sb3("p_sb", [128, 2 * (NKT // 4) * 256], BF16)
        usb = sb3("usb", [128, 2 * 512], F32)
        EVSEM = {}
        kbuf = sb3("kbuf", [128, 2 * NKT * 128], BF16)
        vbuf = sb3("vbuf", [128, 2 * NKT * 129], BF16)
        qbuf = sb3("qbuf", [128, 2 * S_CORE], BF16)
        ustg = sb3("ustg", [128, 2 * 128], BF16)

        agv = ag_o.rearrange("(r f p) t -> r f p t", r=N_CORES, p=128)
        vview = vbuf[:, :].rearrange("p (b k) -> p b k", k=129)
        NQT = NTT // 2
        NSTG = 4
        NKS = NKT // NSTG
        NGE = NKS // 2
        NST = NQH * NQT * NSTG
        NCH2 = 2
        GPC2 = NGE // NCH2
        DNM_ALL = S.val("p_dnm")
        DNE_ALL = S.val("p_dne")
        FU_ALL = S.val("p_fu")
        GMM_ALL = S.val("p_gmm")
        UMM_ALL = S.val("p_umm")

        def grp_eng(st_idx, ge):
            u, st = divmod(st_idx, NSTG)
            masked = (st == NSTG - 1) and (ge >= NGE - 4)
            if masked:
                return "vector_mask"
            return "scalar" if ge % 2 == 1 else "vector"

        S.add("vector", lambda e: e.memset(vview[:, :, 128:129], 1.0),
              waits=[("p_dne", DNE_ALL)], inc="p_on")

        def load_kv(h):
            pk = h % 2
            kd = kbuf[:, pk * NKT * 128:(pk + 1) * NKT * 128]
            w = [("p_ag", 1), ("p_dnm", DNM_ALL), ("p_on", 1)]
            if h >= 2:
                w.append(("p_sc2", 2 * (h - 1) * NTT // 2 * NSTG * NGE))
                w.append(("p_ue", NSTG * (2 * (h - 1) * (NTT // 2))))
            S.add("sync",
                  lambda e, kd=kd, h=h: e.dma_start(
                      out=kd.rearrange("p (r t) -> p r t", r=N_CORES),
                      in_=agv[:, h, :, 0:S_CORE].transpose([1, 0, 2])),
                  waits=w, inc=f"p_kv{pk}", inc_n=16)
            vdst = vbuf[:, :].rearrange("p (b2 r j k) -> p b2 r j k",
                                        b2=2, r=N_CORES, j=NTT)[:, pk, :, :, 0:128]
            for r in range(N_CORES):
                S.add("sync",
                      lambda e, h=h, vdst=vdst, r=r: e.dma_start(
                          out=vdst[:, r],
                          in_=agv[r, NKV:NKV + NTT, :, h * 128:(h + 1) * 128]
                              .transpose([1, 0, 2])),
                      inc=f"p_kv{pk}", inc_n=16)

        def load_q(qh):
            pq = qh % 2
            w = STG_W + [("p_ln", NTT), ("p_dnm", DNM_ALL)]
            if qh >= 2:
                w.append(("p_sc2", (qh - 1) * NTT // 2 * NSTG * NGE))
            S.add("sync",
                  lambda e, qh=qh, pq=pq: e.dma_start(
                      out=qbuf[:, pq * S_CORE:(pq + 1) * S_CORE],
                      in_=qsc[qh * 128:(qh + 1) * 128, :]),
                  waits=w, inc=f"p_q{pq}", inc_n=16)

        load_kv(0)
        load_kv(1)
        load_q(0)
        load_q(1)

        def emit_av_mm(st_idx):
            u, st = divmod(st_idx, NSTG)
            qh, q2 = divmod(u, NQT)
            hk = qh // 2
            pk = hk % 2
            pu = u % 2
            avb = pair[:, pu * 512:(pu + 1) * 512]
            QSO = (0, 136)
            for qs in range(2):
                for ch in range(NCH2):
                    for jj in range(GPC2 * 2):
                        kt_rel = ch * GPC2 * 2 + jj
                        kt = st * NKS + kt_rel
                        w = []
                        if jj == 0 and qs == 0:
                            w = [("p_ex", st_idx * NCH2 + ch + 1)]
                            if ch == 0:
                                w.append(("p_mv2", min(st_idx + 2, NST)))
                            if ch == 0:
                                if st == 0:
                                    w.append(("p_fu", FU_ALL))
                                    w.append(("p_ue", NSTG * (u - 1)))
                                else:
                                    w.append(("p_ue", NSTG * u + st))
                        spb = (st_idx % 2) * (NKT // 4) * 256
                        S.add("tensor",
                              lambda e, kt=kt, kt_rel=kt_rel, qs=qs, pk=pk, avb=avb, spb=spb: e.matmul(
                                  avb[:, QSO[qs]: QSO[qs] + 129],
                                  p_sb[:, spb + kt_rel * 256 + qs * 128: spb + kt_rel * 256 + (qs + 1) * 128],
                                  vview[:, pk * NKT + kt, :],
                                  start=(kt_rel == 0), stop=(kt_rel == NKS - 1)),
                              waits=w,
                              inc=("p_av" if (jj == GPC2 * 2 - 1 and qs == 1
                                              and ch == NCH2 - 1) else None),
                              tag=f"av{st_idx}.{qs}.{kt_rel}")

        def emit_av_vec(st_idx):
            u, st = divmod(st_idx, NSTG)
            qh, q2 = divmod(u, NQT)
            pu = u % 2
            avb = pair[:, pu * 512:(pu + 1) * 512]
            if st == 0:
                def park(e, pu=pu):
                    e.tensor_copy(usb[:, pu * 512 + 0: pu * 512 + 129], avb[:, 0:129])
                    return e.tensor_copy(usb[:, pu * 512 + 256: pu * 512 + 385], avb[:, 136:265])
                S.add("vector", park,
                      waits=[("p_av", st_idx + 1), ("p_ue", NSTG * u)], inc="p_ue")
                return
            last = (st == NSTG - 1)

            def comb(e, pu=pu, last=last):
                QSO2 = (0, 136)
                for qs in range(2):
                    e.scalar_tensor_tensor(
                        out=usb[:, pu * 512 + qs * 256: pu * 512 + qs * 256 + 129],
                        in0=usb[:, pu * 512 + qs * 256: pu * 512 + qs * 256 + 129],
                        scalar=alpha[:, pu:pu + 1],
                        in1=avb[:, QSO2[qs]: QSO2[qs] + 129],
                        op0=ALU.mult, op1=ALU.add)
                if not last:
                    return e.drain()
                e.drain()
                e.reciprocal(recs[:, pu * 2:pu * 2 + 1], usb[:, pu * 512 + 128: pu * 512 + 129])
                e.reciprocal(recs[:, pu * 2 + 1:pu * 2 + 2], usb[:, pu * 512 + 384: pu * 512 + 385])
                e.drain()
                e.tensor_scalar(out=uhat[:, pu * 256: pu * 256 + 128],
                                in0=usb[:, pu * 512: pu * 512 + 128],
                                scalar1=recs[:, pu * 2:pu * 2 + 1], scalar2=0.0,
                                op0=ALU.mult, op1=ALU.add)
                return e.tensor_scalar(out=uhat[:, pu * 256 + 128: pu * 256 + 256],
                                       in0=usb[:, pu * 512 + 256: pu * 512 + 384],
                                       scalar1=recs[:, pu * 2 + 1:pu * 2 + 2], scalar2=0.0,
                                       op0=ALU.mult, op1=ALU.add)
            S.add("vector", comb,
                  waits=[("p_av", st_idx + 1), ("p_al", 3 * u + st),
                         ("p_uT", 2 * (u - 1) + 2)], inc="p_ue", tag=f"comb{st_idx}")
            if not last:
                return
            for qs in range(2):
                n_u = 2 * u + qs
                up_ = n_u % 2
                ub = b6b if up_ == 0 else b7b
                S.add("tensor",
                      lambda e, ub=ub, pu=pu, qs=qs: e.transpose(
                          ub[:, 0:128], uhat[:, pu * 256 + qs * 128: pu * 256 + (qs + 1) * 128],
                          idb[:]),
                      waits=[("p_ue", S.val("p_ue")), ("p_uf", 2 * (u - 1) + qs + 1)],
                      inc="p_uT")
                S.add("vector",
                      lambda e, ub=ub, up_=up_: e.tensor_copy(
                          ustg[:, up_ * 128:(up_ + 1) * 128], ub[:, 0:128]),
                      waits=[("p_uT", S.val("p_uT")),
                             (f"p_us{up_}", 16 * (n_u // 2))], inc="p_uf")
                dst = uTd.rearrange("(c p) t -> p c t", p=128)[
                    :, qh, q2 * 256 + qs * 128: q2 * 256 + (qs + 1) * 128]
                S.add("scalar",
                      lambda e, dst=dst, up_=up_: e.dma_start(
                          out=dst, in_=ustg[:, up_ * 128:(up_ + 1) * 128]),
                      waits=[("p_uf", S.val("p_uf"))], inc=f"p_us{up_}", inc_n=16)

        def emit_mred(sti):
            mslot = sti % 2
            mxo = mslot * 512 + 384
            S.add("vector",
                  lambda e, mslot=mslot, mxo=mxo: e.tensor_reduce(
                      mred[:, mslot:mslot + 1], pair[0:1, mxo:mxo + 128], AXX, ALU.max),
                  waits=[("p_mxT", sti + 1), ("p_bc", sti - 1)], inc="p_mv2")

        def emit_bc(sti):
            mslot = sti % 2
            bco = mslot * 512 + 272
            S.add("tensor",
                  lambda e, mslot=mslot, bco=bco: e.matmul(
                      pair[:, bco:bco + 1], onr[:], mred[:, mslot:mslot + 1],
                      start=True, stop=True),
                  waits=[("p_mv2", sti + 1), ("p_ng", sti - 1), ("p_ue", sti)], inc="p_bc")

        def emit_mxT(sti):
            mslot = sti % 2
            mxo = mslot * 512 + 384
            S.add("tensor",
                  lambda e, mslot=mslot, mxo=mxo: e.transpose(
                      pair[0:1, mxo:mxo + 128],
                      stat[:, 90 + mslot:91 + mslot], idf[:]),
                  waits=[("p_mv1", sti + 1), ("p_mv2", sti - 1)], inc="p_mxT")

        def emit_ngf(sti):
            u, st = divmod(sti, NSTG)
            mslot = sti % 2
            bco = mslot * 512 + 272
            pu = u % 2
            ngw = [("p_bc", sti + 1)]
            if sti >= 2:
                ngw.append(("p_ex", (sti - 1) * NCH2))
            if st >= 1 and (3 * u + st - 1) > 0:
                ngw.append(("p_al", 3 * u + st - 1))

            def ngf(e, mslot=mslot, bco=bco, st=st, pu=pu):
                if st == 0:
                    e.tensor_scalar(out=negm[:, mslot:mslot + 1], in0=pair[:, bco:bco + 1],
                                    scalar1=-1.0, scalar2=0.0, op0=ALU.mult, op1=ALU.add)
                    return e.tensor_copy(mpos[:, pu:pu + 1], pair[:, bco:bco + 1])
                e.tensor_tensor(out=stat[:, 94:95], in0=pair[:, bco:bco + 1],
                                in1=mpos[:, pu:pu + 1], op=ALU.max)
                e.drain()
                e.tensor_scalar(out=negm[:, mslot:mslot + 1], in0=stat[:, 94:95],
                                scalar1=-1.0, scalar2=0.0, op0=ALU.mult, op1=ALU.add)
                e.tensor_tensor(out=asub[:, pu:pu + 1], in0=mpos[:, pu:pu + 1],
                                in1=stat[:, 94:95], op=ALU.subtract)
                e.drain()
                return e.tensor_copy(mpos[:, pu:pu + 1], stat[:, 94:95])
            S.add("vector", ngf, waits=ngw, inc="p_ng")

        def emit_alpha_exp(sti):
            u, st = divmod(sti, NSTG)
            mslot = sti % 2
            NG = sti + 1
            if st >= 1:
                aw = [("p_ng", NG), ("p_ue", NSTG * u + st)]
                S.add("scalar",
                      lambda e, pu=u % 2: e.activation(out=alpha[:, pu:pu + 1],
                                                       in_=asub[:, pu:pu + 1], func=AF.Exp),
                      waits=aw, inc="p_al")
            for ch in range(NCH2):
                w = [("p_ng", NG)]
                best = {}
                for gg in range(ch * GPC2, (ch + 1) * GPC2):
                    nm, vv = EVSEM[(sti, gg)]
                    best[nm] = max(best.get(nm, 0), vv)
                w += list(best.items())
                if sti >= 2:
                    w.append(("p_av", sti - 1))
                spb = (sti % 2) * (NKT // 4) * 256
                ssc2 = (sti % 2) * (NKT // 4) * 256
                S.add("scalar",
                      lambda e, ch=ch, mslot=mslot, spb=spb, ssc2=ssc2: e.activation(
                          out=p_sb[:, spb + ch * GPC2 * 512: spb + (ch + 1) * GPC2 * 512],
                          in_=sc_sb[:, ssc2 + ch * GPC2 * 512: ssc2 + (ch + 1) * GPC2 * 512],
                          func=AF.Exp, bias=negm[:, mslot:mslot + 1], scale=1.0),
                      waits=w, inc="p_ex")

        for it in range(NST + 2):
            if 1 <= it <= NST:
                emit_mred(it - 1)
            if it < NST:
                sti = it
                u, st = divmod(sti, NSTG)
                qh, q2 = divmod(u, NQT)
                hk = qh // 2
                pk = hk % 2
                pq = qh % 2
                mslot = sti % 2
                if st == 0 and q2 == 0:
                    if qh + 2 < NQH:
                        load_q(qh + 2)
                    if qh % 2 == 0 and hk + 2 < NKV:
                        load_kv(hk + 2)
                nvec = 0
                for ge in range(NGE):
                    for j in range(2):
                        kt_rel = ge * 2 + j
                        kt = st * NKS + kt_rel
                        slot = kt_rel % 8
                        w = []
                        if j == 0:
                            if ge >= 4 or sti > 0:
                                psti, pg = (sti, ge - 4) if ge >= 4 else (sti - 1, NGE + ge - 4)
                                w.append(EVSEM[(psti, pg)])
                            w += [(f"p_kv{pk}", 16 * (1 + N_CORES) * (hk // 2 + 1)),
                                  (f"p_q{pq}", 16 * (qh // 2 + 1))]
                            if sti == 0 and ge == 0:
                                w.append(("p_dne", DNE_ALL))
                        S.add("tensor",
                              lambda e, slot=slot, kt=kt, pk=pk, pq=pq, q2=q2: e.matmul(
                                  quad[:, slot * 256:(slot + 1) * 256],
                                  kbuf[:, pk * NKT * 128 + kt * 128: pk * NKT * 128 + (kt + 1) * 128],
                                  qbuf[:, pq * S_CORE + q2 * 256: pq * S_CORE + (q2 + 1) * 256],
                                  start=True, stop=True),
                              waits=w, inc=("p_sc2" if j == 1 else None))
                    SC2 = S.val("p_sc2")
                    eng = grp_eng(sti, ge)
                    bank = (2 * ge % 8) // 2
                    ssc = (sti % 2) * (NKT // 4) * 256
                    ew = [("p_sc2", SC2), ("p_ex", (sti - 2) * NCH2 + ge // GPC2 + 1)]
                    if sti == 0 and ge == 0:
                        ew += [("p_gmm", GMM_ALL), ("p_umm", UMM_ALL)]
                    if eng == "vector":
                        S.add("vector",
                              lambda e, bank=bank, ge=ge, mslot=mslot, nv=nvec, ssc=ssc: e.tensor_scalar(
                                  out=sc_sb[:, ssc + ge * 512: ssc + (ge + 1) * 512],
                                  in0=quad[:, bank * 512:(bank + 1) * 512],
                                  scalar1=float(C_SCALE), scalar2=NEG_BIG,
                                  op0=ALU.mult, op1=ALU.max,
                                  accum_out=maxv[:, mslot * 16 + nv: mslot * 16 + nv + 1]),
                              waits=ew, inc="p_evv")
                        EVSEM[(sti, ge)] = ("p_evv", S.val("p_evv"))
                        nvec += 1
                    elif eng == "vector_mask":
                        mg = ge - (NGE - 4)
                        S.add("vector",
                              lambda e, bank=bank, ge=ge, mg=mg, q2=q2, ssc=ssc: e.scalar_tensor_tensor(
                                  out=sc_sb[:, ssc + ge * 512: ssc + (ge + 1) * 512],
                                  in0=quad[:, bank * 512:(bank + 1) * 512],
                                  scalar=float(C_SCALE),
                                  in1=mask_sb[:, q2 * 2048 + mg * 512: q2 * 2048 + (mg + 1) * 512],
                                  op0=ALU.mult, op1=ALU.add),
                              waits=ew, inc="p_evv")
                        EVSEM[(sti, ge)] = ("p_evv", S.val("p_evv"))
                    else:
                        S.add("scalar",
                              lambda e, bank=bank, ge=ge, ssc=ssc: e.mul(
                                  sc_sb[:, ssc + ge * 512: ssc + (ge + 1) * 512],
                                  quad[:, bank * 512:(bank + 1) * 512], float(C_SCALE)),
                              waits=ew, inc="p_evs")
                        EVSEM[(sti, ge)] = ("p_evs", S.val("p_evs"))
                NV = nvec
                assert NV > 0 or st > 0

                def mx_red(e, mslot=mslot, NV=NV, st=st, pu=u % 2):
                    e.drain()
                    if st == 0:
                        return e.tensor_reduce(stat[:, 90 + mslot:91 + mslot],
                                               maxv[:, mslot * 16: mslot * 16 + NV],
                                               AXX, ALU.max)
                    if NV == 0:
                        return e.tensor_scalar(out=stat[:, 90 + mslot:91 + mslot],
                                               in0=epsv[:, 0:1], scalar1=0.0,
                                               scalar2=NEG_BIG, op0=ALU.mult, op1=ALU.add)
                    return e.tensor_reduce(stat[:, 90 + mslot:91 + mslot],
                                           maxv[:, mslot * 16: mslot * 16 + NV],
                                           AXX, ALU.max)
                mrw = []
                if sti >= 2:
                    mrw.append(("p_mxT", sti - 1))
                S.add("vector", mx_red, waits=mrw, inc="p_mv1")
            if 2 <= it <= NST + 1:
                emit_av_mm(it - 2)
                emit_av_vec(it - 2)
            if 1 <= it <= NST:
                emit_bc(it - 1)
            if it < NST:
                emit_mxT(it)
            if 1 <= it <= NST:
                emit_ngf(it - 1)
                emit_alpha_exp(it - 1)
        EX_ALL = S.val("p_ex")
        AVC_ALL = S.val("p_av")
        UF_ALL = S.val("p_uf")
        SCG_ALL = S.val("p_sc2")
        EV_ALL = S.val("p_evv")
        EVS_ALL = S.val("p_evs")
        UE_ALL = S.val("p_ue")

        US_W = [(f"p_us{i}", S.val(f"p_us{i}")) for i in range(2)]
        UE_ALL2 = S.val("p_ue")
        es3.close()
        es4 = contextlib.ExitStack()
        sb4 = lambda name, shape, dt: es4.enter_context(nc.sbuf_tensor(name, shape, dt))
        uT4 = sb4("uT4", [128, 2 * 2048], BF16)
        woT = sb4("woT", [128, 16 * 512], BF16)
        wonat = sb4("wonat", [128, 2 * HID], BF16)
        xrl = sb4("xrl", [128, 4 * 512], F32)
        osb = sb4("osb", [128, 4 * 512], F32)
        tmp4 = sb4("tmp4", [128, 4 * 512], F32)

        wor = wo_in.rearrange("(a p) n -> a p n", p=128)
        n_x = 0
        for hs in range(4):
            for osub in range(4):
                kpc = hs * 4 + osub
                pw = kpc % 2
                dst = wonat[:, pw * HID:(pw + 1) * HID]
                S.add("gpsimd",
                      lambda e, d=dst, kpc=kpc: e.dma_start(out=d, in_=wor[kpc]),
                      waits=[("p_ex", EX_ALL), ("p_av", AVC_ALL), ("p_ue", UE_ALL2),
                             ("p_sc2", SCG_ALL), ("p_woT", (kpc - 1) * 16)] + US_W,
                      inc=f"p_won{pw}", inc_n=16)
                PWON = S.val(f"p_won{pw}")
                for ic in range(16):
                    trn = kpc * 16 + ic
                    off = (trn % 2) * 1024 + ((trn // 2) % 4) * 128
                    S.add("tensor",
                          lambda e, d=dst, ic=ic, off=off: e.transpose(
                              pairb[:, off:off + 128],
                              d[:, ic * 128:(ic + 1) * 128], idb[:]),
                          waits=([(f"p_won{pw}", PWON), ("p_uf", UF_ALL),
                                  ("p_ue", UE_ALL), ("p_evs", EVS_ALL)] if ic == 0 else []) +
                                [("p_woTe", trn - 1)],
                          inc="p_woT")
                    S.add("vector",
                          lambda e, ic=ic, osub=osub, off=off: e.tensor_copy(
                              woT[:, ic * 512 + osub * 128: ic * 512 + (osub + 1) * 128],
                              pairb[:, off:off + 128]),
                          waits=[("p_woT", S.val("p_woT"))], inc="p_woTe")
            WOTE = S.val("p_woTe")
            for tt in range(NTT):
                bank = tt % 4
                slot = n_x % 4
                nuse = n_x // 4
                up4 = n_x % 2
                n_x += 1
                S.add("sync",
                      lambda e, slot=slot, tt=tt, hs=hs: e.dma_start(
                          out=xrl[:, slot * 512:(slot + 1) * 512],
                          in_=xr[tt][:, hs * 512:(hs + 1) * 512]),
                      waits=[(f"p_out{slot}", 16 * nuse), ("p_ex", EX_ALL),
                             ("p_ue", UE_ALL2)] + US_W,
                      inc=f"p_xr{slot}", inc_n=16)
                S.add("scalar",
                      lambda e, up4=up4, tt=tt: e.dma_start(
                          out=uT4[:, up4 * 2048:(up4 + 1) * 2048].rearrange(
                              "p (c t) -> p c t", c=16),
                          in_=uTd.rearrange("(c p) t -> p c t", p=128)[
                              :, :, tt * 128:(tt + 1) * 128]),
                      waits=US_W + [("p_wom", S.val("p_wom") - 1)],
                      inc=f"p_ut4{up4}", inc_n=16)
                PUT4 = S.val(f"p_ut4{up4}")
                for ic in range(16):
                    w = []
                    if ic == 0:
                        w = [("p_woTe", WOTE), ("p_cmb", S.val("p_cmb") - 3),
                             ("p_evv", EV_ALL), ("p_evs", EVS_ALL), ("p_uf", UF_ALL),
                             (f"p_ut4{up4}", PUT4)]
                    S.add("tensor",
                          lambda e, bank=bank, ic=ic, up4=up4: e.matmul(
                              bankq(bank),
                              uT4[:, up4 * 2048 + ic * 128: up4 * 2048 + (ic + 1) * 128],
                              woT[:, ic * 512:(ic + 1) * 512],
                              start=(ic == 0), stop=(ic == 15)),
                          waits=w, inc=("p_wom" if ic == 15 else None))
                WOM = S.val("p_wom")

                def combine(e, bank=bank, slot=slot, tt=tt, hs=hs):
                    e.tensor_tensor(out=tmp4[:, slot * 512:(slot + 1) * 512],
                                    in0=bankq(bank), in1=xrl[:, slot * 512:(slot + 1) * 512],
                                    op=ALU.add)
                    e.drain()
                    return e.tensor_tensor(
                        out=osb[:, slot * 512:(slot + 1) * 512],
                        in0=tmp4[:, slot * 512:(slot + 1) * 512],
                        in1=ffn_sb[:, tt * HID + hs * 512: tt * HID + (hs + 1) * 512],
                        op=ALU.add)
                S.add("vector", combine,
                      waits=[("p_wom", WOM), (f"p_xr{slot}", 16 * (nuse + 1)),
                             (f"p_out{slot}", 16 * nuse)],
                      inc="p_cmb")
                S.add("sync",
                      lambda e, slot=slot, tt=tt, hs=hs: e.dma_start(
                          out=out_d[tt * 128:(tt + 1) * 128, hs * 512:(hs + 1) * 512],
                          in_=osb[:, slot * 512:(slot + 1) * 512]),
                      waits=[("p_cmb", S.val("p_cmb"))], inc=f"p_out{slot}", inc_n=16)
        OUT_W = [(f"p_out{i}", S.val(f"p_out{i}")) for i in range(4)]
        S.add("sync", lambda e: e.nop(), waits=OUT_W)
        es4.close()

        S.validate()
        with contextlib.ExitStack() as es_sem:
            sems = {n: es_sem.enter_context(nc.semaphore(n)) for n in S.names}
            import os
            if os.environ.get("KDBG"):
                for n, h in sems.items():
                    print("SEMMAP", getattr(h, "index", getattr(h, "id", h)), n)
            with nc.Block() as block:
                block.sync(lambda e: S.emit("sync", e, sems))
                block.scalar(lambda e: S.emit("scalar", e, sems))
                block.vector(lambda e: S.emit("vector", e, sems))
                block.tensor(lambda e: S.emit("tensor", e, sems))
                block.gpsimd(lambda e: S.emit("gpsimd", e, sems))

    meta = dict(S_CORE=S_CORE)
    return nc, meta


def build_consts(S_CORE, qnw, knw, gsc, usc, dsc):
    S_TOT = N_CORES * S_CORE
    NTT = S_CORE // 128
    NKT = S_TOT // 128
    NOC = INTER // 128
    f32 = np.float32
    bf = ml_dtypes.bfloat16
    f8 = ml_dtypes.float8_e4m3fn
    c = {
        "ident_f": np.eye(128, dtype=f32),
        "ident_b": np.eye(128).astype(bf),
        "ones_r": np.ones((1, 128), f32),
        "qgain": np.ascontiguousarray(np.broadcast_to(np.asarray(qnw, f32), (128, 128))),
        "kgain": np.ascontiguousarray(np.broadcast_to(np.asarray(knw, f32), (128, 128))),
        "gscale": np.ascontiguousarray(np.asarray(gsc, f32).reshape(NOC, 128).T),
        "uscale": np.ascontiguousarray(np.asarray(usc, f32).reshape(NOC, 128).T),
        "dscale": np.ascontiguousarray(np.broadcast_to(np.asarray(dsc, f32), (128, HID))),
        "epsv": np.ascontiguousarray(np.broadcast_to(np.array([LN_EPS, RMS_EPS], f32), (128, 2))),
    }
    masks = []
    for core in range(N_CORES):
        m = np.zeros((128, NTT * 1024), f32)
        for qt in range(NTT):
            for gi in range(2):
                for j in range(4):
                    kt = NKT - 8 + gi * 4 + j
                    kgl = kt * 128 + np.arange(128)[:, None]
                    qgl = core * S_CORE + qt * 128 + np.arange(128)[None, :]
                    msk = (kgl >= S_TOT - WINDOW) & (kgl <= qgl)
                    m[:, qt * 1024 + gi * 512 + j * 128: qt * 1024 + gi * 512 + (j + 1) * 128] = \
                        np.where(msk, MASK_NEG, 0.0)
        masks.append(m.astype(f8))
    return c, masks


_CACHE = {}


def kernel(L0_attn_hidden_states, L0_attn_Wq, L0_attn_Wk, L0_attn_Wv, L0_attn_Wo,
           L0_attn_q_norm_w, L0_attn_k_norm_w, L0_ffn_gate_w, L0_ffn_up_w,
           L0_ffn_down_w, L0_ffn_gate_scale, L0_ffn_up_scale, L0_ffn_down_scale):
    x = np.asarray(L0_attn_hidden_states)
    B, L, H = x.shape
    S_CORE = L // N_CORES
    if S_CORE not in _CACHE:
        _CACHE[S_CORE] = build_program(S_CORE)
    nc, meta = _CACHE[S_CORE]
    consts, masks = build_consts(S_CORE, L0_attn_q_norm_w, L0_attn_k_norm_w,
                                 L0_ffn_gate_scale, L0_ffn_up_scale, L0_ffn_down_scale)
    shared = {
        "wq": np.ascontiguousarray(np.asarray(L0_attn_Wq, np.float32)),
        "wk": np.ascontiguousarray(np.asarray(L0_attn_Wk, np.float32)),
        "wv": np.ascontiguousarray(np.asarray(L0_attn_Wv, np.float32)),
        "wo": np.ascontiguousarray(np.asarray(L0_attn_Wo, np.float32)),
        "gate_w": np.ascontiguousarray(np.asarray(L0_ffn_gate_w, np.float32)),
        "up_w": np.ascontiguousarray(np.asarray(L0_ffn_up_w, np.float32)),
        "down_w": np.ascontiguousarray(np.asarray(L0_ffn_down_w, np.float32)),
    }
    shared.update(consts)
    xs = np.ascontiguousarray(x.reshape(L, H).astype(np.float32))
    in_maps = []
    for core in range(N_CORES):
        m = dict(shared)
        m["x"] = xs[core * S_CORE:(core + 1) * S_CORE]
        m["mask"] = masks[core]
        in_maps.append(m)
    res = run_bass_kernel_spmd(nc, in_maps, core_ids=list(range(N_CORES)))
    out = np.concatenate([res.results[c]["out"] for c in range(N_CORES)], axis=0)
    return out.reshape(B, L, H)


# version 09
# speedup vs baseline: 1.2047x; 1.2047x over previous
import math
import contextlib
import numpy as np
import ml_dtypes
import concourse.bass as bass
import concourse.mybir as mybir
from concourse.bass_utils import run_bass_kernel_spmd

F32 = mybir.dt.float32
BF16 = mybir.dt.bfloat16
F8 = mybir.dt.float8e4
AF = mybir.ActivationFunctionType
ALU = mybir.AluOpType
AXX = mybir.AxisListType.X

N_CORES = 8
HID = 2048
INTER = 6144
NQH = 16
NKV = 8
HD = 128
WINDOW = 1024
C_SCALE = 1.0 / math.sqrt(HD)
LN_EPS = 1e-5
RMS_EPS = 1e-6
MASK_NEG = -240.0
NEG_BIG = -3.0e38

ENGS = ("sync", "scalar", "vector", "tensor", "gpsimd")


class Sched:

    def __init__(self):
        self.steps = {e: [] for e in ENGS}
        self.ctr = {}
        self.names = []

    def sem(self, name):
        if name not in self.ctr:
            self.ctr[name] = 0
            self.names.append(name)
        return name

    def val(self, name):
        self.sem(name)
        return self.ctr[name]

    def add(self, eng, fn, waits=(), inc=None, inc_n=1, tag=None):
        w = tuple((self.sem(s), v) for (s, v) in waits if v > 0)
        if inc is not None:
            self.sem(inc)
            self.ctr[inc] += inc_n
        self.steps[eng].append((w, fn, inc, inc_n, tag))

    def validate(self):
        for e in ENGS:
            for (waits, _f, _i, _n, _t) in self.steps[e]:
                for (s, v) in waits:
                    assert v <= self.ctr[s], f"unsatisfiable wait {s}>={v} (max {self.ctr[s]})"

    def emit(self, eng_name, eng, sems):
        for (waits, fn, inc, inc_n, tag) in self.steps[eng_name]:
            for (s, v) in waits:
                eng.wait_ge(sems[s], v)
            inst = fn(eng)
            if tag and inst is not None:
                inst.annotate(tag)
            if inc is not None:
                assert inst is not None, f"step with inc={inc} returned None"
                inst.then_inc(sems[inc], inc_n)


def build_program(S_CORE):
    S_TOT = N_CORES * S_CORE
    NTT = S_CORE // 128
    NKT = S_TOT // 128
    NGRP = NKT // 4
    NU = NQH * NTT
    NCH = 4
    GPC = NGRP // NCH
    assert NGRP % NCH == 0 and NGRP >= 4
    NOC = INTER // 128
    NT5 = S_CORE // 512
    NQO = HID // 512
    NKO = (NKV * HD) // 512
    assert S_CORE % 512 == 0

    nc = bass.Bass("TRN2", target_bir_lowering=False, debug=False)
    ap = lambda t: t.ap()
    x_in = ap(nc.dram_tensor("x", [S_CORE, HID], F32, kind="ExternalInput"))
    wq_in = ap(nc.dram_tensor("wq", [HID, HID], F32, kind="ExternalInput"))
    wk_in = ap(nc.dram_tensor("wk", [NKV * HD, HID], F32, kind="ExternalInput"))
    wv_in = ap(nc.dram_tensor("wv", [NKV * HD, HID], F32, kind="ExternalInput"))
    wo_in = ap(nc.dram_tensor("wo", [HID, HID], F32, kind="ExternalInput"))
    gate_in = ap(nc.dram_tensor("gate_w", [HID, INTER], F32, kind="ExternalInput"))
    up_in = ap(nc.dram_tensor("up_w", [HID, INTER], F32, kind="ExternalInput"))
    down_in = ap(nc.dram_tensor("down_w", [INTER, HID], F32, kind="ExternalInput"))
    ident_f = ap(nc.dram_tensor("ident_f", [128, 128], F32, kind="ExternalInput"))
    ident_b = ap(nc.dram_tensor("ident_b", [128, 128], BF16, kind="ExternalInput"))
    ones_r = ap(nc.dram_tensor("ones_r", [1, 128], F32, kind="ExternalInput"))
    qg_in = ap(nc.dram_tensor("qgain", [128, 128], F32, kind="ExternalInput"))
    kg_in = ap(nc.dram_tensor("kgain", [128, 128], F32, kind="ExternalInput"))
    gs_in = ap(nc.dram_tensor("gscale", [128, NOC], F32, kind="ExternalInput"))
    us_in = ap(nc.dram_tensor("uscale", [128, NOC], F32, kind="ExternalInput"))
    ds_in = ap(nc.dram_tensor("dscale", [128, HID], F32, kind="ExternalInput"))
    mask_in = ap(nc.dram_tensor("mask", [128, NTT * 1024], F8, kind="ExternalInput"))
    eps_in = ap(nc.dram_tensor("epsv", [128, 2], F32, kind="ExternalInput"))
    out_d = ap(nc.dram_tensor("out", [S_CORE, HID], F32, kind="ExternalOutput"))
    qsc = ap(nc.dram_tensor("qscratch", [NQH * HD, S_CORE], BF16))
    AGC = max(S_CORE, NKV * HD)
    AGR = NKV * HD + S_CORE
    ag_i = ap(nc.dram_tensor("ag_in", [AGR, AGC], BF16))
    ag_o = ap(nc.dram_tensor("ag_out", [N_CORES * AGR, AGC], BF16))
    uTd = ap(nc.dram_tensor("uTd", [NQH * HD, S_CORE], BF16))

    S = Sched()
    es = contextlib.ExitStack()
    sbE = lambda name, shape, dt: es.enter_context(nc.sbuf_tensor(name, shape, dt))

    with es:
        idf = sbE("idf", [128, 128], F32)
        idb = sbE("idb", [128, 128], BF16)
        onr = sbE("onr", [1, 128], F32)
        qg = sbE("qg", [128, 128], F32)
        kg = sbE("kg", [128, 128], F32)
        gs = sbE("gs", [128, NOC], F32)
        us = sbE("us", [128, NOC], F32)
        mask_sb = sbE("mask_sb", [128, NTT * 1024], F8)
        epsv = sbE("epsv_sb", [128, 2], F32)
        stat = sbE("stat", [128, 96], F32)
        mred = sbE("mred", [1, 2], F32)
        negm = sbE("negm", [128, 2], F32)
        recs = sbE("recs", [128, 4], F32)
        uhat = sbE("uhat", [128, 2 * 256], BF16)
        mpos = sbE("mpos", [128, 2], F32)
        asub = sbE("asub", [128, 2], F32)
        alpha = sbE("alpha", [128, 2], F32)
        maxv = sbE("maxv", [128, 2 * 16], F32)
        ffn_sb = sbE("ffn_sb", [128, NTT * HID], BF16)
        quad = es.enter_context(nc.psum_tensor("quad", [128, 2048], F32))
        pair = es.enter_context(nc.psum_tensor("pair", [128, 1024], F32))
        b6 = es.enter_context(nc.psum_tensor("b6", [128, 512], F32))
        b7 = es.enter_context(nc.psum_tensor("b7", [128, 512], F32))

        b6b = b6[:].bitcast(BF16)
        b7b = b7[:].bitcast(BF16)
        pairb = pair[:].bitcast(BF16)

        def bankq(i):
            return quad[:, i * 512:(i + 1) * 512]

        for dst, src in [(idf[:], ident_f), (idb[:], ident_b), (onr[:], ones_r),
                         (qg[:], qg_in), (kg[:], kg_in), (gs[:], gs_in),
                         (us[:], us_in), (mask_sb[:], mask_in), (epsv[:], eps_in)]:
            S.add("sync", lambda e, d=dst, s=src: e.dma_start(out=d, in_=s),
                  inc="c_ld", inc_n=16)

        es12 = contextlib.ExitStack()
        sb12 = lambda name, shape, dt: es12.enter_context(nc.sbuf_tensor(name, shape, dt))
        x_db = sb12("x_db", [128, 2 * HID], F32)
        xn_tm = sb12("xn_tm", [128, HID], BF16)
        xn_fm = sb12("xn_fm", [128, 16 * S_CORE], BF16)
        wnat = sb12("wnat", [128, 2 * HID], BF16)
        wT = sb12("wT", [128, 16 * 512], BF16)
        qraw = sb12("qraw", [128, 2 * 512], F32)
        g8 = sb12("g8", [128, 2 * HID], F8)
        u8 = sb12("u8", [128, 2 * HID], F8)
        d8 = sb12("d8", [128, 4 * 512], F8)
        silu2 = sb12("silu2", [128, 2 * 512], BF16)
        fuse = sb12("fuse", [128, NOC * 512], BF16)
        ds_bc = sb12("ds_bc", [128, HID], F32)
        zbuf = sb12("zbuf", [128, 512], BF16)
        stg4 = sb12("stg4", [128, 4 * 512], BF16)

        S.add("sync", lambda e: e.dma_start(out=ds_bc[:], in_=ds_in), inc="c_ld", inc_n=16)
        CLD = S.val("c_ld")

        AGC = max(S_CORE, NKV * HD)
        if AGC > S_CORE:
            S.add("vector", lambda e: e.memset(zbuf[:], 0.0), inc="p_zm")
            for rb in range(NKV * HD // 128):
                S.add("sync",
                      lambda e, rb=rb: e.dma_start(
                          out=ag_i[rb * 128:(rb + 1) * 128, S_CORE:AGC], in_=zbuf[:, 0:AGC - S_CORE]),
                      waits=[("p_zm", 1)], inc="p_zp", inc_n=16)
            ZP_W = [("p_zp", S.val("p_zp"))]
        else:
            ZP_W = []

        xr = x_in.rearrange("(a p) n -> a p n", p=128)
        xn_c = xn_fm[:, :].rearrange("p (c t) -> p c t", c=16)

        for tt in range(NTT):
            par = tt % 2
            sl = lambda k, w=1, par=par: stat[:, par * 48 + k: par * 48 + k + w]
            xs = x_db[:, par * HID:(par + 1) * HID]
            S.add("sync", lambda e, xs=xs, tt=tt: e.dma_start(out=xs, in_=xr[tt]),
                  waits=[("c_ld", CLD), ("p_ln", tt - 1)], inc=f"p_x{par}", inc_n=16)
            S.add("scalar",
                  lambda e, xs=xs, d=sl(0): e.activation(out=xn_tm[:], in_=xs,
                                                         func=AF.Identity, accum_out=d),
                  waits=[(f"p_x{par}", 16 * (tt // 2 + 1)), ("p_tT", tt * 4)])
            S.add("scalar",
                  lambda e, xs=xs, d=sl(1): e.activation(out=fuse[:, 0:HID].bitcast(BF16) if False else fuse[:, 0:HID], in_=xs,
                                                         func=AF.Square, accum_out=d))
            S.add("scalar", lambda e: e.drain(), inc="p_st")
            PST = S.val("p_st")

            def ln_stats(e, sl=sl):
                e.tensor_scalar(out=sl(2), in0=sl(0), scalar1=1.0 / HID, scalar2=0.0,
                                op0=ALU.mult, op1=ALU.add)
                e.drain()
                e.tensor_tensor(out=sl(4), in0=sl(2), in1=sl(2), op=ALU.mult)
                e.drain()
                return e.scalar_tensor_tensor(out=sl(5), in0=sl(1), scalar=1.0 / HID,
                                              in1=sl(4), op0=ALU.mult, op1=ALU.subtract)
            S.add("vector", ln_stats, waits=[("p_st", PST)], inc="p_ve")
            S.add("scalar", lambda e, sl=sl: e.activation(out=sl(6), in_=sl(5), func=AF.Sqrt,
                                                          bias=epsv[:, 0:1]),
                  waits=[("p_ve", S.val("p_ve"))])
            S.add("scalar", lambda e: e.drain(), inc="p_sq")
            PSQ = S.val("p_sq")

            def ln_apply(e, xs=xs, sl=sl):
                e.reciprocal(sl(7), sl(6))
                e.drain()
                return e.tensor_scalar(out=xn_tm[:], in0=xs, scalar1=sl(2), scalar2=sl(7),
                                       op0=ALU.subtract, op1=ALU.mult)
            S.add("vector", ln_apply, waits=[("p_sq", PSQ), ("p_tT", tt * 4)], inc="p_ln")
            PLN = S.val("p_ln")
            for qtr in range(4):
                for j in range(4):
                    c = qtr * 4 + j
                    w = []
                    if j == 0:
                        w = [("p_xnT", tt * 4 + qtr)]
                        if qtr == 0:
                            w.append(("p_ln", PLN))
                    S.add("tensor",
                          lambda e, c=c, j=j: e.transpose(b6b[:, j * 128:(j + 1) * 128],
                                                          xn_tm[:, c * 128:(c + 1) * 128],
                                                          idb[:]),
                          waits=w, inc=("p_tT" if j == 3 else None))
                S.add("vector",
                      lambda e, tt=tt, qtr=qtr: e.tensor_copy(
                          xn_c[:, qtr * 4:(qtr + 1) * 4, tt * 128:(tt + 1) * 128],
                          b6b[:, 0:512].rearrange("p (c t) -> p c t", c=4)),
                      waits=[("p_tT", S.val("p_tT"))], inc="p_xnT")
        XND = S.val("p_xnT")

        o_specs = ([("q", i) for i in range(NQO)] + [("k", i) for i in range(NKO)]
                   + [("v", i) for i in range(NKO)])
        w_dram = {"q": wq_in, "k": wk_in, "v": wv_in}
        n_stg = 0
        n_qk = 0

        for oi, (kind, osl) in enumerate(o_specs):
            wd = w_dram[kind].rearrange("(a p) n -> a p n", p=128)
            for osub in range(4):
                kpc = oi * 4 + osub
                pw = kpc % 2
                dst = wnat[:, pw * HID:(pw + 1) * HID]
                S.add("gpsimd", lambda e, d=dst, s=wd[osl * 4 + osub]: e.dma_start(out=d, in_=s),
                      waits=[("p_wT", (kpc - 1) * 16), ("c_ld", CLD)],
                      inc=f"p_wn{pw}", inc_n=16)
                PWN = S.val(f"p_wn{pw}")
                for ic in range(16):
                    trn = kpc * 16 + ic
                    off = (trn % 2) * 1024 + ((trn // 2) % 4) * 128
                    S.add("tensor",
                          lambda e, d=dst, ic=ic, off=off: e.transpose(
                              pairb[:, off:off + 128],
                              d[:, ic * 128:(ic + 1) * 128], idb[:]),
                          waits=([(f"p_wn{pw}", PWN)] if ic == 0 else []) +
                                [("p_wTe", trn - 1)],
                          inc="p_wT")
                    S.add("vector",
                          lambda e, ic=ic, osub=osub, off=off: e.tensor_copy(
                              wT[:, ic * 512 + osub * 128: ic * 512 + (osub + 1) * 128],
                              pairb[:, off:off + 128]),
                          waits=[("p_wT", S.val("p_wT"))], inc="p_wTe")
            WTE = S.val("p_wTe")

            for tt in range(NTT):
                bank = tt % 4
                for ic in range(16):
                    w = []
                    if ic == 0:
                        w = [("p_xnT", XND), ("p_wTe", WTE),
                             ("p_pev", S.val("p_pev") - 3)]
                    S.add("tensor",
                          lambda e, bank=bank, ic=ic, tt=tt: e.matmul(
                              bankq(bank), xn_c[:, ic, tt * 128:(tt + 1) * 128],
                              wT[:, ic * 512:(ic + 1) * 512],
                              start=(ic == 0), stop=(ic == 15)),
                          waits=w, inc=("p_mmP" if ic == 15 else None))
                MMP = S.val("p_mmP")
                slot = n_stg % 4
                nuse = n_stg // 4
                n_stg += 1
                stg_s = stg4[:, slot * 512:(slot + 1) * 512]
                if kind == "v":
                    S.add("vector",
                          lambda e, bank=bank, d=stg_s: e.tensor_copy(d, bankq(bank)),
                          waits=[("p_mmP", MMP), (f"p_stg{slot}", 16 * nuse)],
                          inc="p_pev")
                    S.add("scalar",
                          lambda e, s=stg_s, tt=tt, osl=osl: e.dma_start(
                              out=ag_i[NKV * HD + tt * 128: NKV * HD + (tt + 1) * 128,
                                       osl * 512:(osl + 1) * 512], in_=s),
                          waits=[("p_pev", S.val("p_pev"))], inc=f"p_stg{slot}", inc_n=16)
                else:
                    n_qk += 1
                    parq = tt % 2
                    slp = lambda k, w=1, parq=parq: stat[:, parq * 48 + 16 + k: parq * 48 + 16 + k + w]
                    qr = qraw[:, parq * 512:(parq + 1) * 512]
                    S.add("vector",
                          lambda e, bank=bank, qr=qr: e.tensor_copy(qr, bankq(bank)),
                          waits=[("p_mmP", MMP), ("p_rms", n_qk - 2)],
                          inc="p_pev")
                    PEV = S.val("p_pev")
                    for h4 in range(4):
                        S.add("scalar",
                              lambda e, qr=qr, h4=h4, d=slp(h4): e.activation(
                                  out=xn_tm[:, h4 * 128:(h4 + 1) * 128],
                                  in_=qr[:, h4 * 128:(h4 + 1) * 128],
                                  func=AF.Square, accum_out=d),
                              waits=([("p_pev", PEV), ("p_tT", NTT * 4)] if h4 == 0 else []))
                    S.add("scalar", lambda e: e.drain(), inc="p_ss")
                    S.add("vector",
                          lambda e, slp=slp: e.tensor_scalar(
                              out=slp(4, 4), in0=slp(0, 4), scalar1=1.0 / HD,
                              scalar2=0.0, op0=ALU.mult, op1=ALU.add),
                          waits=[("p_ss", S.val("p_ss"))], inc="p_rv")
                    S.add("scalar",
                          lambda e, slp=slp: e.activation(out=slp(8, 4), in_=slp(4, 4),
                                                          func=AF.Sqrt, bias=epsv[:, 1:2]),
                          waits=[("p_rv", S.val("p_rv"))])
                    S.add("scalar", lambda e: e.drain(), inc="p_sq2")

                    def rms_nr(e, slp=slp):
                        e.reciprocal(slp(28, 4), slp(8, 4))
                        return e.drain()
                    S.add("vector", rms_nr, waits=[("p_sq2", S.val("p_sq2"))], inc="p_nr")
                    gt = qg if kind == "q" else kg
                    for h4 in range(4):
                        w = [("p_qT", S.val("p_qT"))] if h4 == 0 else []
                        S.add("vector",
                              lambda e, qr=qr, h4=h4, gt=gt, slp=slp: e.scalar_tensor_tensor(
                                  out=silu2[:, h4 * 128:(h4 + 1) * 128],
                                  in0=qr[:, h4 * 128:(h4 + 1) * 128],
                                  scalar=slp(28 + h4), in1=gt[:],
                                  op0=ALU.mult, op1=ALU.mult),
                              waits=w, inc=("p_rms" if h4 == 3 else None))
                    PRMS = S.val("p_rms")
                    for h4 in range(4):
                        S.add("tensor",
                              lambda e, h4=h4: e.transpose(
                                  b6b[:, h4 * 128:(h4 + 1) * 128],
                                  silu2[:, h4 * 128:(h4 + 1) * 128], idb[:]),
                              waits=([("p_rms", PRMS), ("p_qTe", S.val("p_qTe"))]
                                     if h4 == 0 else []),
                              inc=("p_qT" if h4 == 3 else None))
                    S.add("vector",
                          lambda e, d=stg_s: e.tensor_copy(d, b6b[:, 0:512]),
                          waits=[("p_qT", S.val("p_qT")), (f"p_stg{slot}", 16 * nuse)],
                          inc="p_qTe")
                    if kind == "q":
                        dstq = qsc.rearrange("(h p) t -> p h t", p=128)[
                            :, osl * 4:osl * 4 + 4, tt * 128:(tt + 1) * 128]
                    else:
                        dstq = ag_i[0:NKV * HD, 0:S_CORE].rearrange("(h p) t -> p h t", p=128)[
                            :, osl * 4:osl * 4 + 4, tt * 128:(tt + 1) * 128]
                    S.add("scalar",
                          lambda e, s=stg_s, d=dstq: e.dma_start(
                              out=d, in_=s.rearrange("p (h t) -> p h t", h=4)),
                          waits=[("p_qTe", S.val("p_qTe"))], inc=f"p_stg{slot}", inc_n=16)
        STG_W = [(f"p_stg{i}", S.val(f"p_stg{i}")) for i in range(4)]
        PEV_ALL = S.val("p_pev")
        QTE_ALL = S.val("p_qTe")

        S.add("gpsimd",
              lambda e: e.collective_compute("AllGather", ALU.bypass,
                                             replica_groups=[list(range(N_CORES))],
                                             ins=[ag_i.opt()], outs=[ag_o.opt()]),
              waits=STG_W + ZP_W, inc="p_ag")

        gcol = gate_in.rearrange("(c p) n -> p c n", p=128)
        ucol = up_in.rearrange("(c p) n -> p c n", p=128)
        dcol = down_in.rearrange("(c p) n -> p c n", p=128)
        NOCP = NOC // 2
        ALLB = [quad[:, i * 512:(i + 1) * 512] for i in range(4)] + \
               [pair[:, 0:512], pair[:, 512:1024], b6[:], b7[:]]

        for oph in range(2):
            for ocr in range(NOCP):
                oc = oph * NOCP + ocr
                j = oc
                pw = j % 2
                gds = g8[:, pw * HID:(pw + 1) * HID]
                uds = u8[:, pw * HID:(pw + 1) * HID]
                S.add("gpsimd",
                      lambda e, d=gds, oc=oc: e.dma_start(
                          out=d.rearrange("p (c n) -> p c n", c=16),
                          in_=gcol[:, :, oc * 128:(oc + 1) * 128]),
                      waits=[("p_gmm", (j - 2) * NT5 + NT5)], inc=f"p_g8{pw}", inc_n=16)
                S.add("gpsimd",
                      lambda e, d=uds, oc=oc: e.dma_start(
                          out=d.rearrange("p (c n) -> p c n", c=16),
                          in_=ucol[:, :, oc * 128:(oc + 1) * 128]),
                      waits=[("p_umm", (j - 2) * NT5 + NT5)], inc=f"p_u8{pw}", inc_n=16)
                PG8 = S.val(f"p_g8{pw}")
                PU8 = S.val(f"p_u8{pw}")
                for t5 in range(NT5):
                    j2 = j * NT5 + t5
                    pw2 = j2 % 2
                    gp = pair[:, 0:512] if pw2 == 0 else b6[:]
                    upp = pair[:, 512:1024] if pw2 == 0 else b7[:]
                    for ic in range(16):
                        w = []
                        if ic == 0:
                            w = [(f"p_g8{pw}", PG8), ("p_xnT", XND), ("p_si", j2 - 1)]
                            if j2 <= 1:
                                w += [("p_pev", PEV_ALL), ("p_qTe", QTE_ALL)]
                            if oph == 1 and ocr == 0 and t5 <= 1:
                                w += [("p_dne", oph * 16 * NT5)]
                        S.add("tensor",
                              lambda e, gp=gp, gds=gds, ic=ic, t5=t5: e.matmul(
                                  gp, gds[:, ic * 128:(ic + 1) * 128],
                                  xn_c[:, ic, t5 * 512:(t5 + 1) * 512],
                                  start=(ic == 0), stop=(ic == 15)),
                              waits=w, inc=("p_gmm" if ic == 15 else None))
                    for ic in range(16):
                        w = []
                        if ic == 0:
                            w = [("p_fu", j2 - 1), (f"p_u8{pw}", PU8)]
                            if oph == 1 and ocr == 0 and t5 <= 1:
                                w += [("p_dne", oph * 16 * NT5)]
                        S.add("tensor",
                              lambda e, upp=upp, uds=uds, ic=ic, t5=t5: e.matmul(
                                  upp, uds[:, ic * 128:(ic + 1) * 128],
                                  xn_c[:, ic, t5 * 512:(t5 + 1) * 512],
                                  start=(ic == 0), stop=(ic == 15)),
                              waits=w, inc=("p_umm" if ic == 15 else None))
                    ps = pw2
                    S.add("scalar",
                          lambda e, gp=gp, oc=oc, ps=ps: e.activation(
                              out=silu2[:, ps * 512:(ps + 1) * 512], in_=gp, func=AF.Silu,
                              scale=gs[:, oc:oc + 1]),
                          waits=[("p_gmm", S.val("p_gmm")), ("p_fu", j2 - 1)], inc="p_si")
                    fsl = (ocr * NT5 + t5)
                    S.add("vector",
                          lambda e, upp=upp, oc=oc, ps=ps, fsl=fsl: e.scalar_tensor_tensor(
                              out=fuse[:, fsl * 512:(fsl + 1) * 512], in0=upp,
                              scalar=us[:, oc:oc + 1], in1=silu2[:, ps * 512:(ps + 1) * 512],
                              op0=ALU.mult, op1=ALU.mult),
                          waits=[("p_umm", S.val("p_umm")), ("p_si", S.val("p_si"))] +
                                ([("p_dnm", oph * 4 * NOCP if ocr == 0 and t5 == 0 else 0)]
                                 if oph >= 1 else []),
                          inc="p_fu")
            FUD = S.val("p_fu")
            for hs in range(4):
                for ocr in range(NOCP):
                    oc = oph * NOCP + ocr
                    jd = (oph * 4 + hs) * NOCP + ocr
                    sd = jd % 4
                    dds = d8[:, sd * 512:(sd + 1) * 512]
                    S.add("gpsimd",
                          lambda e, d=dds, oc=oc, hs=hs: e.dma_start(
                              out=d, in_=dcol[:, oc, hs * 512:(hs + 1) * 512]),
                          waits=[("p_dnm", jd - 3), ("p_fu", FUD if ocr == 0 else 0)],
                          inc=f"p_d8{sd}", inc_n=16)
                    PD8 = S.val(f"p_d8{sd}")
                    for t5 in range(NT5):
                        for ts in range(4):
                            bk = t5 * 4 + ts
                            fsl = (ocr * NT5 + t5)
                            w = []
                            if ts == 0 and t5 == 0:
                                w = [(f"p_d8{sd}", PD8), ("p_fu", FUD)]
                                if ocr == 0:
                                    w.append(("p_dne", (oph * 4 + hs) * 4 * NT5))
                            S.add("tensor",
                                  lambda e, bk=bk, fsl=fsl, ts=ts, dds=dds, ocr=ocr: e.matmul(
                                      ALLB[bk], fuse[:, fsl * 512 + ts * 128: fsl * 512 + (ts + 1) * 128],
                                      dds, start=(ocr == 0), stop=(ocr == NOCP - 1)),
                                  waits=w, inc=("p_dnm" if (ts == 3 and t5 == NT5 - 1) else None),
                                  tag=f"dn{oph}.{hs}.{ocr}.{t5}.{ts}")
                DNM = S.val("p_dnm")
                for t5 in range(NT5):
                    for ts in range(4):
                        bk = t5 * 4 + ts
                        tt = t5 * 4 + ts
                        if oph == 0:
                            S.add("vector",
                                  lambda e, bk=bk, tt=tt, hs=hs: e.tensor_tensor(
                                      out=ffn_sb[:, tt * HID + hs * 512: tt * HID + (hs + 1) * 512],
                                      in0=ALLB[bk], in1=ds_bc[:, hs * 512:(hs + 1) * 512],
                                      op=ALU.mult),
                                  waits=[("p_dnm", DNM)], inc="p_dne")
                        else:
                            def ev2(e, bk=bk, tt=tt, hs=hs):
                                qtmp = qraw[:, (bk % 2) * 512:(bk % 2) * 512 + 512]
                                e.tensor_tensor(out=qtmp,
                                                in0=ALLB[bk], in1=ds_bc[:, hs * 512:(hs + 1) * 512],
                                                op=ALU.mult)
                                e.drain()
                                return e.tensor_tensor(
                                    out=ffn_sb[:, tt * HID + hs * 512: tt * HID + (hs + 1) * 512],
                                    in0=qtmp,
                                    in1=ffn_sb[:, tt * HID + hs * 512: tt * HID + (hs + 1) * 512],
                                    op=ALU.add)
                            S.add("vector", ev2, waits=[("p_dnm", DNM)], inc="p_dne")
        GMM_ALL = S.val("p_gmm")
        UMM_ALL = S.val("p_umm")
        DNE_ALL = S.val("p_dne")
        DNM_ALL = S.val("p_dnm")
        FU_ALL = S.val("p_fu")
        SI_ALL = S.val("p_si")

        es12.close()
        es3 = contextlib.ExitStack()
        sb3 = lambda name, shape, dt: es3.enter_context(nc.sbuf_tensor(name, shape, dt))
        sc_sb = sb3("sc_sb", [128, 2 * (NKT // 4) * 256], F32)
        p_sb = sb3("p_sb", [128, 2 * (NKT // 4) * 256], BF16)
        usb = sb3("usb", [128, 2 * 512], F32)
        EVSEM = {}
        kbuf = sb3("kbuf", [128, 2 * NKT * 128], BF16)
        vbuf = sb3("vbuf", [128, 2 * NKT * 129], BF16)
        qbuf = sb3("qbuf", [128, 2 * S_CORE], BF16)
        ustg = sb3("ustg", [128, 2 * 128], BF16)

        agv = ag_o.rearrange("(r f p) t -> r f p t", r=N_CORES, p=128)
        vview = vbuf[:, :].rearrange("p (b k) -> p b k", k=129)
        NQT = NTT // 2
        NSTG = 4
        NKS = NKT // NSTG
        NGE = NKS // 2
        NST = NQH * NQT * NSTG
        NCH2 = 2
        GPC2 = NGE // NCH2
        DNM_ALL = S.val("p_dnm")
        DNE_ALL = S.val("p_dne")
        FU_ALL = S.val("p_fu")
        GMM_ALL = S.val("p_gmm")
        UMM_ALL = S.val("p_umm")

        def grp_eng(st_idx, ge):
            u, st = divmod(st_idx, NSTG)
            masked = (st == NSTG - 1) and (ge >= NGE - 4)
            if masked:
                return "vector_mask"
            return "scalar" if ge >= NGE // 2 else "vector"

        S.add("vector", lambda e: e.memset(vview[:, :, 128:129], 1.0),
              waits=[("p_dne", DNE_ALL)], inc="p_on")

        def load_kv(h):
            pk = h % 2
            kd = kbuf[:, pk * NKT * 128:(pk + 1) * NKT * 128]
            w = [("p_ag", 1), ("p_dnm", DNM_ALL), ("p_on", 1)]
            if h >= 2:
                w.append(("p_sc2", 2 * (h - 1) * NTT // 2 * NSTG * NGE))
                w.append(("p_ue", NSTG * (2 * (h - 1) * (NTT // 2))))
            S.add("sync",
                  lambda e, kd=kd, h=h: e.dma_start(
                      out=kd.rearrange("p (r t) -> p r t", r=N_CORES),
                      in_=agv[:, h, :, 0:S_CORE].transpose([1, 0, 2])),
                  waits=w, inc=f"p_kv{pk}", inc_n=16)
            vdst = vbuf[:, :].rearrange("p (b2 r j k) -> p b2 r j k",
                                        b2=2, r=N_CORES, j=NTT)[:, pk, :, :, 0:128]
            for r in range(N_CORES):
                S.add("sync",
                      lambda e, h=h, vdst=vdst, r=r: e.dma_start(
                          out=vdst[:, r],
                          in_=agv[r, NKV:NKV + NTT, :, h * 128:(h + 1) * 128]
                              .transpose([1, 0, 2])),
                      inc=f"p_kv{pk}", inc_n=16)

        def load_q(qh):
            pq = qh % 2
            w = STG_W + [("p_ln", NTT), ("p_dnm", DNM_ALL)]
            if qh >= 2:
                w.append(("p_sc2", (qh - 1) * NTT // 2 * NSTG * NGE))
            S.add("sync",
                  lambda e, qh=qh, pq=pq: e.dma_start(
                      out=qbuf[:, pq * S_CORE:(pq + 1) * S_CORE],
                      in_=qsc[qh * 128:(qh + 1) * 128, :]),
                  waits=w, inc=f"p_q{pq}", inc_n=16)

        load_kv(0)
        load_kv(1)
        load_q(0)
        load_q(1)

        def emit_av_mm(st_idx):
            u, st = divmod(st_idx, NSTG)
            qh, q2 = divmod(u, NQT)
            hk = qh // 2
            pk = hk % 2
            pu = u % 2
            avb = pair[:, pu * 512:(pu + 1) * 512]
            QSO = (0, 136)
            for qs in range(2):
                for ch in range(NCH2):
                    for jj in range(GPC2 * 2):
                        kt_rel = ch * GPC2 * 2 + jj
                        kt = st * NKS + kt_rel
                        w = []
                        if jj == 0 and qs == 0:
                            w = [("p_ex", st_idx * NCH2 + ch + 1)]
                            if ch == 0:
                                if st == 0:
                                    w.append(("p_fu", FU_ALL))
                                    w.append(("p_ue", NSTG * (u - 1)))
                                else:
                                    w.append(("p_ue", NSTG * u + st))
                        spb = (st_idx % 2) * (NKT // 4) * 256
                        S.add("tensor",
                              lambda e, kt=kt, kt_rel=kt_rel, qs=qs, pk=pk, avb=avb, spb=spb: e.matmul(
                                  avb[:, QSO[qs]: QSO[qs] + 129],
                                  p_sb[:, spb + kt_rel * 256 + qs * 128: spb + kt_rel * 256 + (qs + 1) * 128],
                                  vview[:, pk * NKT + kt, :],
                                  start=(kt_rel == 0), stop=(kt_rel == NKS - 1)),
                              waits=w,
                              inc=("p_av" if (jj == GPC2 * 2 - 1 and qs == 1
                                              and ch == NCH2 - 1) else None),
                              tag=f"av{st_idx}.{qs}.{kt_rel}")

        def emit_av_vec(st_idx):
            u, st = divmod(st_idx, NSTG)
            qh, q2 = divmod(u, NQT)
            pu = u % 2
            avb = pair[:, pu * 512:(pu + 1) * 512]
            if st == 0:
                def park(e, pu=pu):
                    e.tensor_copy(usb[:, pu * 512 + 0: pu * 512 + 129], avb[:, 0:129])
                    return e.tensor_copy(usb[:, pu * 512 + 256: pu * 512 + 385], avb[:, 136:265])
                S.add("vector", park,
                      waits=[("p_av", st_idx + 1), ("p_ue", NSTG * u)], inc="p_ue")
                return
            last = (st == NSTG - 1)

            def comb(e, pu=pu, last=last, aslot=st_idx % 2):
                QSO2 = (0, 136)
                for qs in range(2):
                    e.scalar_tensor_tensor(
                        out=usb[:, pu * 512 + qs * 256: pu * 512 + qs * 256 + 129],
                        in0=usb[:, pu * 512 + qs * 256: pu * 512 + qs * 256 + 129],
                        scalar=alpha[:, aslot:aslot + 1],
                        in1=avb[:, QSO2[qs]: QSO2[qs] + 129],
                        op0=ALU.mult, op1=ALU.add)
                if not last:
                    return e.drain()
                e.drain()
                e.reciprocal(recs[:, pu * 2:pu * 2 + 1], usb[:, pu * 512 + 128: pu * 512 + 129])
                e.reciprocal(recs[:, pu * 2 + 1:pu * 2 + 2], usb[:, pu * 512 + 384: pu * 512 + 385])
                e.drain()
                e.tensor_scalar(out=uhat[:, pu * 256: pu * 256 + 128],
                                in0=usb[:, pu * 512: pu * 512 + 128],
                                scalar1=recs[:, pu * 2:pu * 2 + 1], scalar2=0.0,
                                op0=ALU.mult, op1=ALU.add)
                return e.tensor_scalar(out=uhat[:, pu * 256 + 128: pu * 256 + 256],
                                       in0=usb[:, pu * 512 + 256: pu * 512 + 384],
                                       scalar1=recs[:, pu * 2 + 1:pu * 2 + 2], scalar2=0.0,
                                       op0=ALU.mult, op1=ALU.add)
            S.add("vector", comb,
                  waits=[("p_av", st_idx + 1), ("p_al", 3 * u + st),
                         ("p_uT", 2 * (u - 1) + 2)], inc="p_ue", tag=f"comb{st_idx}")
            if not last:
                return
            for qs in range(2):
                n_u = 2 * u + qs
                up_ = n_u % 2
                ub = b6b
                S.add("tensor",
                      lambda e, ub=ub, pu=pu, qs=qs: e.transpose(
                          ub[:, 0:128], uhat[:, pu * 256 + qs * 128: pu * 256 + (qs + 1) * 128],
                          idb[:]),
                      waits=[("p_ue", S.val("p_ue")), ("p_uf", 2 * u + qs)],
                      inc="p_uT")
                S.add("vector",
                      lambda e, ub=ub, up_=up_: e.tensor_copy(
                          ustg[:, up_ * 128:(up_ + 1) * 128], ub[:, 0:128]),
                      waits=[("p_uT", S.val("p_uT")),
                             (f"p_us{up_}", 16 * (n_u // 2))], inc="p_uf")
                dst = uTd.rearrange("(c p) t -> p c t", p=128)[
                    :, qh, q2 * 256 + qs * 128: q2 * 256 + (qs + 1) * 128]
                S.add("scalar",
                      lambda e, dst=dst, up_=up_: e.dma_start(
                          out=dst, in_=ustg[:, up_ * 128:(up_ + 1) * 128]),
                      waits=[("p_uf", S.val("p_uf"))], inc=f"p_us{up_}", inc_n=16)

        def emit_mred(sti):
            mslot = sti % 2
            S.add("vector",
                  lambda e, mslot=mslot: e.tensor_reduce(
                      mred[:, mslot:mslot + 1], b7[0:1, mslot * 256: mslot * 256 + 128],
                      AXX, ALU.max),
                  waits=[("p_mxT", sti + 1), ("p_bc", sti - 1)], inc="p_mv2")

        def emit_bc(sti):
            mslot = sti % 2
            S.add("tensor",
                  lambda e, mslot=mslot: e.matmul(
                      b7[:, mslot * 256 + 128: mslot * 256 + 129], onr[:],
                      mred[:, mslot:mslot + 1], start=True, stop=True),
                  waits=[("p_mv2", sti + 1), ("p_ng", sti - 1)], inc="p_bc")

        def emit_mxT(sti):
            mslot = sti % 2
            S.add("tensor",
                  lambda e, mslot=mslot: e.transpose(
                      b7[0:1, mslot * 256: mslot * 256 + 128],
                      stat[:, 90 + mslot:91 + mslot], idf[:]),
                  waits=[("p_mv1", sti + 1), ("p_mv2", sti - 1)], inc="p_mxT")

        def emit_ngf(sti):
            u, st = divmod(sti, NSTG)
            mslot = sti % 2
            pu = u % 2
            ngw = [("p_bc", sti + 1)]
            if sti >= 2:
                ngw.append(("p_ex", (sti - 1) * NCH2))
            if st >= 1:
                prev = sti - 2 if (sti - 2) % NSTG != 0 else sti - 4
                if prev >= 1:
                    pa = 3 * (prev // NSTG) + (prev % NSTG)
                    if pa > 0:
                        ngw.append(("p_al", pa))

            def ngf(e, mslot=mslot, st=st, pu=pu):
                bcv = b7[:, mslot * 256 + 128: mslot * 256 + 129]
                if st == 0:
                    e.tensor_scalar(out=negm[:, mslot:mslot + 1], in0=bcv,
                                    scalar1=-1.0, scalar2=0.0, op0=ALU.mult, op1=ALU.add)
                    return e.tensor_copy(mpos[:, pu:pu + 1], bcv)
                e.tensor_tensor(out=stat[:, 94:95], in0=bcv,
                                in1=mpos[:, pu:pu + 1], op=ALU.max)
                e.drain()
                e.tensor_scalar(out=negm[:, mslot:mslot + 1], in0=stat[:, 94:95],
                                scalar1=-1.0, scalar2=0.0, op0=ALU.mult, op1=ALU.add)
                e.tensor_tensor(out=asub[:, mslot:mslot + 1], in0=mpos[:, pu:pu + 1],
                                in1=stat[:, 94:95], op=ALU.subtract)
                e.drain()
                return e.tensor_copy(mpos[:, pu:pu + 1], stat[:, 94:95])
            S.add("vector", ngf, waits=ngw, inc="p_ng")

        def emit_alpha_exp(sti):
            u, st = divmod(sti, NSTG)
            mslot = sti % 2
            NG = sti + 1
            if st >= 1:
                aw = [("p_ng", NG), ("p_ue", sti - 1)]
                S.add("scalar",
                      lambda e, mslot=mslot: e.activation(out=alpha[:, mslot:mslot + 1],
                                                          in_=asub[:, mslot:mslot + 1],
                                                          func=AF.Exp),
                      waits=aw, inc="p_al")
            for ch in range(NCH2):
                w = [("p_ng", NG)]
                best = {}
                for gg in range(ch * GPC2, (ch + 1) * GPC2):
                    nm, vv = EVSEM[(sti, gg)]
                    best[nm] = max(best.get(nm, 0), vv)
                w += list(best.items())
                if sti >= 2:
                    w.append(("p_av", sti - 1))
                spb = (sti % 2) * (NKT // 4) * 256
                ssc2 = (sti % 2) * (NKT // 4) * 256
                S.add("scalar",
                      lambda e, ch=ch, mslot=mslot, spb=spb, ssc2=ssc2: e.activation(
                          out=p_sb[:, spb + ch * GPC2 * 512: spb + (ch + 1) * GPC2 * 512],
                          in_=sc_sb[:, ssc2 + ch * GPC2 * 512: ssc2 + (ch + 1) * GPC2 * 512],
                          func=AF.Exp, bias=negm[:, mslot:mslot + 1], scale=1.0),
                      waits=w, inc="p_ex")

        for it in range(NST + 2):
            if 1 <= it <= NST:
                emit_mred(it - 1)
                emit_alpha_exp(it - 1)
            if it < NST:
                sti = it
                u, st = divmod(sti, NSTG)
                qh, q2 = divmod(u, NQT)
                hk = qh // 2
                pk = hk % 2
                pq = qh % 2
                mslot = sti % 2
                if st == 0 and q2 == 0:
                    if qh + 2 < NQH:
                        load_q(qh + 2)
                    if qh % 2 == 0 and hk + 2 < NKV:
                        load_kv(hk + 2)
                nvec = 0
                for ge in range(NGE):
                    for j in range(2):
                        kt_rel = ge * 2 + j
                        kt = st * NKS + kt_rel
                        slot = kt_rel % 8
                        w = []
                        if j == 0:
                            if ge >= 4 or sti > 0:
                                psti, pg = (sti, ge - 4) if ge >= 4 else (sti - 1, NGE + ge - 4)
                                w.append(EVSEM[(psti, pg)])
                            w += [(f"p_kv{pk}", 16 * (1 + N_CORES) * (hk // 2 + 1)),
                                  (f"p_q{pq}", 16 * (qh // 2 + 1))]
                            if sti == 0 and ge == 0:
                                w.append(("p_dne", DNE_ALL))
                        S.add("tensor",
                              lambda e, slot=slot, kt=kt, pk=pk, pq=pq, q2=q2: e.matmul(
                                  quad[:, slot * 256:(slot + 1) * 256],
                                  kbuf[:, pk * NKT * 128 + kt * 128: pk * NKT * 128 + (kt + 1) * 128],
                                  qbuf[:, pq * S_CORE + q2 * 256: pq * S_CORE + (q2 + 1) * 256],
                                  start=True, stop=True),
                              waits=w, inc=("p_sc2" if j == 1 else None))
                    SC2 = S.val("p_sc2")
                    eng = grp_eng(sti, ge)
                    bank = (2 * ge % 8) // 2
                    ssc = (sti % 2) * (NKT // 4) * 256
                    ew = [("p_sc2", SC2), ("p_ex", (sti - 2) * NCH2 + ge // GPC2 + 1)]
                    if sti == 0 and ge == 0:
                        ew += [("p_gmm", GMM_ALL), ("p_umm", UMM_ALL)]
                    if eng == "vector":
                        S.add("vector",
                              lambda e, bank=bank, ge=ge, mslot=mslot, nv=nvec, ssc=ssc: e.tensor_scalar(
                                  out=sc_sb[:, ssc + ge * 512: ssc + (ge + 1) * 512],
                                  in0=quad[:, bank * 512:(bank + 1) * 512],
                                  scalar1=float(C_SCALE), scalar2=NEG_BIG,
                                  op0=ALU.mult, op1=ALU.max,
                                  accum_out=maxv[:, mslot * 16 + nv: mslot * 16 + nv + 1]),
                              waits=ew, inc="p_evv")
                        EVSEM[(sti, ge)] = ("p_evv", S.val("p_evv"))
                        nvec += 1
                    elif eng == "vector_mask":
                        mg = ge - (NGE - 4)
                        S.add("vector",
                              lambda e, bank=bank, ge=ge, mg=mg, q2=q2, ssc=ssc: e.scalar_tensor_tensor(
                                  out=sc_sb[:, ssc + ge * 512: ssc + (ge + 1) * 512],
                                  in0=quad[:, bank * 512:(bank + 1) * 512],
                                  scalar=float(C_SCALE),
                                  in1=mask_sb[:, q2 * 2048 + mg * 512: q2 * 2048 + (mg + 1) * 512],
                                  op0=ALU.mult, op1=ALU.add),
                              waits=ew, inc="p_evv")
                        EVSEM[(sti, ge)] = ("p_evv", S.val("p_evv"))
                    else:
                        S.add("scalar",
                              lambda e, bank=bank, ge=ge, ssc=ssc: e.mul(
                                  sc_sb[:, ssc + ge * 512: ssc + (ge + 1) * 512],
                                  quad[:, bank * 512:(bank + 1) * 512], float(C_SCALE)),
                              waits=ew, inc="p_evs")
                        EVSEM[(sti, ge)] = ("p_evs", S.val("p_evs"))
                    if ge == 1 and it >= 1:
                        emit_bc(it - 1)
                        emit_ngf(it - 1)
                NV = nvec
                assert NV > 0 or st > 0

                def mx_red(e, mslot=mslot, NV=NV, st=st, pu=u % 2):
                    e.drain()
                    if st == 0:
                        return e.tensor_reduce(stat[:, 90 + mslot:91 + mslot],
                                               maxv[:, mslot * 16: mslot * 16 + NV],
                                               AXX, ALU.max)
                    if NV == 0:
                        return e.tensor_scalar(out=stat[:, 90 + mslot:91 + mslot],
                                               in0=epsv[:, 0:1], scalar1=0.0,
                                               scalar2=NEG_BIG, op0=ALU.mult, op1=ALU.add)
                    return e.tensor_reduce(stat[:, 90 + mslot:91 + mslot],
                                           maxv[:, mslot * 16: mslot * 16 + NV],
                                           AXX, ALU.max)
                mrw = []
                if sti >= 2:
                    mrw.append(("p_mxT", sti - 1))
                S.add("vector", mx_red, waits=mrw, inc="p_mv1")
            if it >= NST and 1 <= it <= NST:
                emit_bc(it - 1)
                emit_ngf(it - 1)
            if 2 <= it <= NST + 1:
                emit_av_mm(it - 2)
                emit_av_vec(it - 2)
            if it < NST:
                emit_mxT(it)
        EX_ALL = S.val("p_ex")
        AVC_ALL = S.val("p_av")
        UF_ALL = S.val("p_uf")
        SCG_ALL = S.val("p_sc2")
        EV_ALL = S.val("p_evv")
        EVS_ALL = S.val("p_evs")
        UE_ALL = S.val("p_ue")

        US_W = [(f"p_us{i}", S.val(f"p_us{i}")) for i in range(2)]
        UE_ALL2 = S.val("p_ue")
        es3.close()
        es4 = contextlib.ExitStack()
        sb4 = lambda name, shape, dt: es4.enter_context(nc.sbuf_tensor(name, shape, dt))
        uT4 = sb4("uT4", [128, 2 * 2048], BF16)
        woT = sb4("woT", [128, 16 * 512], BF16)
        wonat = sb4("wonat", [128, 2 * HID], BF16)
        xrl = sb4("xrl", [128, 4 * 512], F32)
        osb = sb4("osb", [128, 4 * 512], F32)
        tmp4 = sb4("tmp4", [128, 4 * 512], F32)

        wor = wo_in.rearrange("(a p) n -> a p n", p=128)
        n_x = 0
        for hs in range(4):
            for osub in range(4):
                kpc = hs * 4 + osub
                pw = kpc % 2
                dst = wonat[:, pw * HID:(pw + 1) * HID]
                S.add("gpsimd",
                      lambda e, d=dst, kpc=kpc: e.dma_start(out=d, in_=wor[kpc]),
                      waits=[("p_ex", EX_ALL), ("p_av", AVC_ALL), ("p_ue", UE_ALL2),
                             ("p_sc2", SCG_ALL), ("p_woT", (kpc - 1) * 16)] + US_W,
                      inc=f"p_won{pw}", inc_n=16)
                PWON = S.val(f"p_won{pw}")
                for ic in range(16):
                    trn = kpc * 16 + ic
                    off = (trn % 2) * 1024 + ((trn // 2) % 4) * 128
                    S.add("tensor",
                          lambda e, d=dst, ic=ic, off=off: e.transpose(
                              pairb[:, off:off + 128],
                              d[:, ic * 128:(ic + 1) * 128], idb[:]),
                          waits=([(f"p_won{pw}", PWON), ("p_uf", UF_ALL),
                                  ("p_ue", UE_ALL), ("p_evs", EVS_ALL)] if ic == 0 else []) +
                                [("p_woTe", trn - 1)],
                          inc="p_woT")
                    S.add("vector",
                          lambda e, ic=ic, osub=osub, off=off: e.tensor_copy(
                              woT[:, ic * 512 + osub * 128: ic * 512 + (osub + 1) * 128],
                              pairb[:, off:off + 128]),
                          waits=[("p_woT", S.val("p_woT"))], inc="p_woTe")
            WOTE = S.val("p_woTe")
            for tt in range(NTT):
                bank = tt % 4
                slot = n_x % 4
                nuse = n_x // 4
                up4 = n_x % 2
                n_x += 1
                S.add("sync",
                      lambda e, slot=slot, tt=tt, hs=hs: e.dma_start(
                          out=xrl[:, slot * 512:(slot + 1) * 512],
                          in_=xr[tt][:, hs * 512:(hs + 1) * 512]),
                      waits=[(f"p_out{slot}", 16 * nuse), ("p_ex", EX_ALL),
                             ("p_ue", UE_ALL2)] + US_W,
                      inc=f"p_xr{slot}", inc_n=16)
                S.add("scalar",
                      lambda e, up4=up4, tt=tt: e.dma_start(
                          out=uT4[:, up4 * 2048:(up4 + 1) * 2048].rearrange(
                              "p (c t) -> p c t", c=16),
                          in_=uTd.rearrange("(c p) t -> p c t", p=128)[
                              :, :, tt * 128:(tt + 1) * 128]),
                      waits=US_W + [("p_wom", S.val("p_wom") - 1)],
                      inc=f"p_ut4{up4}", inc_n=16)
                PUT4 = S.val(f"p_ut4{up4}")
                for ic in range(16):
                    w = []
                    if ic == 0:
                        w = [("p_woTe", WOTE), ("p_cmb", S.val("p_cmb") - 3),
                             ("p_evv", EV_ALL), ("p_evs", EVS_ALL), ("p_uf", UF_ALL),
                             (f"p_ut4{up4}", PUT4)]
                    S.add("tensor",
                          lambda e, bank=bank, ic=ic, up4=up4: e.matmul(
                              bankq(bank),
                              uT4[:, up4 * 2048 + ic * 128: up4 * 2048 + (ic + 1) * 128],
                              woT[:, ic * 512:(ic + 1) * 512],
                              start=(ic == 0), stop=(ic == 15)),
                          waits=w, inc=("p_wom" if ic == 15 else None))
                WOM = S.val("p_wom")

                def combine(e, bank=bank, slot=slot, tt=tt, hs=hs):
                    e.tensor_tensor(out=tmp4[:, slot * 512:(slot + 1) * 512],
                                    in0=bankq(bank), in1=xrl[:, slot * 512:(slot + 1) * 512],
                                    op=ALU.add)
                    e.drain()
                    return e.tensor_tensor(
                        out=osb[:, slot * 512:(slot + 1) * 512],
                        in0=tmp4[:, slot * 512:(slot + 1) * 512],
                        in1=ffn_sb[:, tt * HID + hs * 512: tt * HID + (hs + 1) * 512],
                        op=ALU.add)
                S.add("vector", combine,
                      waits=[("p_wom", WOM), (f"p_xr{slot}", 16 * (nuse + 1)),
                             (f"p_out{slot}", 16 * nuse)],
                      inc="p_cmb")
                S.add("sync",
                      lambda e, slot=slot, tt=tt, hs=hs: e.dma_start(
                          out=out_d[tt * 128:(tt + 1) * 128, hs * 512:(hs + 1) * 512],
                          in_=osb[:, slot * 512:(slot + 1) * 512]),
                      waits=[("p_cmb", S.val("p_cmb"))], inc=f"p_out{slot}", inc_n=16)
        OUT_W = [(f"p_out{i}", S.val(f"p_out{i}")) for i in range(4)]
        S.add("sync", lambda e: e.nop(), waits=OUT_W)
        es4.close()

        S.validate()
        with contextlib.ExitStack() as es_sem:
            sems = {n: es_sem.enter_context(nc.semaphore(n)) for n in S.names}
            import os
            if os.environ.get("KDBG"):
                for n, h in sems.items():
                    print("SEMMAP", getattr(h, "index", getattr(h, "id", h)), n)
            with nc.Block() as block:
                block.sync(lambda e: S.emit("sync", e, sems))
                block.scalar(lambda e: S.emit("scalar", e, sems))
                block.vector(lambda e: S.emit("vector", e, sems))
                block.tensor(lambda e: S.emit("tensor", e, sems))
                block.gpsimd(lambda e: S.emit("gpsimd", e, sems))

    meta = dict(S_CORE=S_CORE)
    return nc, meta


def build_consts(S_CORE, qnw, knw, gsc, usc, dsc):
    S_TOT = N_CORES * S_CORE
    NTT = S_CORE // 128
    NKT = S_TOT // 128
    NOC = INTER // 128
    f32 = np.float32
    bf = ml_dtypes.bfloat16
    f8 = ml_dtypes.float8_e4m3fn
    c = {
        "ident_f": np.eye(128, dtype=f32),
        "ident_b": np.eye(128).astype(bf),
        "ones_r": np.ones((1, 128), f32),
        "qgain": np.ascontiguousarray(np.broadcast_to(np.asarray(qnw, f32), (128, 128))),
        "kgain": np.ascontiguousarray(np.broadcast_to(np.asarray(knw, f32), (128, 128))),
        "gscale": np.ascontiguousarray(np.asarray(gsc, f32).reshape(NOC, 128).T),
        "uscale": np.ascontiguousarray(np.asarray(usc, f32).reshape(NOC, 128).T),
        "dscale": np.ascontiguousarray(np.broadcast_to(np.asarray(dsc, f32), (128, HID))),
        "epsv": np.ascontiguousarray(np.broadcast_to(np.array([LN_EPS, RMS_EPS], f32), (128, 2))),
    }
    masks = []
    for core in range(N_CORES):
        m = np.zeros((128, NTT * 1024), f32)
        for qt in range(NTT):
            for gi in range(2):
                for j in range(4):
                    kt = NKT - 8 + gi * 4 + j
                    kgl = kt * 128 + np.arange(128)[:, None]
                    qgl = core * S_CORE + qt * 128 + np.arange(128)[None, :]
                    msk = (kgl >= S_TOT - WINDOW) & (kgl <= qgl)
                    m[:, qt * 1024 + gi * 512 + j * 128: qt * 1024 + gi * 512 + (j + 1) * 128] = \
                        np.where(msk, MASK_NEG, 0.0)
        masks.append(m.astype(f8))
    return c, masks


_CACHE = {}


def kernel(L0_attn_hidden_states, L0_attn_Wq, L0_attn_Wk, L0_attn_Wv, L0_attn_Wo,
           L0_attn_q_norm_w, L0_attn_k_norm_w, L0_ffn_gate_w, L0_ffn_up_w,
           L0_ffn_down_w, L0_ffn_gate_scale, L0_ffn_up_scale, L0_ffn_down_scale):
    x = np.asarray(L0_attn_hidden_states)
    B, L, H = x.shape
    S_CORE = L // N_CORES
    if S_CORE not in _CACHE:
        _CACHE[S_CORE] = build_program(S_CORE)
    nc, meta = _CACHE[S_CORE]
    consts, masks = build_consts(S_CORE, L0_attn_q_norm_w, L0_attn_k_norm_w,
                                 L0_ffn_gate_scale, L0_ffn_up_scale, L0_ffn_down_scale)
    shared = {
        "wq": np.ascontiguousarray(np.asarray(L0_attn_Wq, np.float32)),
        "wk": np.ascontiguousarray(np.asarray(L0_attn_Wk, np.float32)),
        "wv": np.ascontiguousarray(np.asarray(L0_attn_Wv, np.float32)),
        "wo": np.ascontiguousarray(np.asarray(L0_attn_Wo, np.float32)),
        "gate_w": np.ascontiguousarray(np.asarray(L0_ffn_gate_w, np.float32)),
        "up_w": np.ascontiguousarray(np.asarray(L0_ffn_up_w, np.float32)),
        "down_w": np.ascontiguousarray(np.asarray(L0_ffn_down_w, np.float32)),
    }
    shared.update(consts)
    xs = np.ascontiguousarray(x.reshape(L, H).astype(np.float32))
    in_maps = []
    for core in range(N_CORES):
        m = dict(shared)
        m["x"] = xs[core * S_CORE:(core + 1) * S_CORE]
        m["mask"] = masks[core]
        in_maps.append(m)
    res = run_bass_kernel_spmd(nc, in_maps, core_ids=list(range(N_CORES)))
    out = np.concatenate([res.results[c]["out"] for c in range(N_CORES)], axis=0)
    return out.reshape(B, L, H)


# version 10
# speedup vs baseline: 1.2091x; 1.0037x over previous
import math
import contextlib
import numpy as np
import ml_dtypes
import concourse.bass as bass
import concourse.mybir as mybir
from concourse.bass_utils import run_bass_kernel_spmd

F32 = mybir.dt.float32
BF16 = mybir.dt.bfloat16
F8 = mybir.dt.float8e4
AF = mybir.ActivationFunctionType
ALU = mybir.AluOpType
AXX = mybir.AxisListType.X

N_CORES = 8
HID = 2048
INTER = 6144
NQH = 16
NKV = 8
HD = 128
WINDOW = 1024
C_SCALE = 1.0 / math.sqrt(HD)
LN_EPS = 1e-5
RMS_EPS = 1e-6
MASK_NEG = -240.0
NEG_BIG = -3.0e38

ENGS = ("sync", "scalar", "vector", "tensor", "gpsimd")


class Sched:

    def __init__(self):
        self.steps = {e: [] for e in ENGS}
        self.ctr = {}
        self.names = []

    def sem(self, name):
        if name not in self.ctr:
            self.ctr[name] = 0
            self.names.append(name)
        return name

    def val(self, name):
        self.sem(name)
        return self.ctr[name]

    def add(self, eng, fn, waits=(), inc=None, inc_n=1, tag=None):
        w = tuple((self.sem(s), v) for (s, v) in waits if v > 0)
        if inc is not None:
            self.sem(inc)
            self.ctr[inc] += inc_n
        self.steps[eng].append((w, fn, inc, inc_n, tag))

    def validate(self):
        for e in ENGS:
            for (waits, _f, _i, _n, _t) in self.steps[e]:
                for (s, v) in waits:
                    assert v <= self.ctr[s], f"unsatisfiable wait {s}>={v} (max {self.ctr[s]})"

    def emit(self, eng_name, eng, sems):
        for (waits, fn, inc, inc_n, tag) in self.steps[eng_name]:
            for (s, v) in waits:
                eng.wait_ge(sems[s], v)
            inst = fn(eng)
            if tag and inst is not None:
                inst.annotate(tag)
            if inc is not None:
                assert inst is not None, f"step with inc={inc} returned None"
                inst.then_inc(sems[inc], inc_n)


def build_program(S_CORE):
    S_TOT = N_CORES * S_CORE
    NTT = S_CORE // 128
    NKT = S_TOT // 128
    NGRP = NKT // 4
    NU = NQH * NTT
    NCH = 4
    GPC = NGRP // NCH
    assert NGRP % NCH == 0 and NGRP >= 4
    NOC = INTER // 128
    NT5 = S_CORE // 512
    NQO = HID // 512
    NKO = (NKV * HD) // 512
    assert S_CORE % 512 == 0

    nc = bass.Bass("TRN2", target_bir_lowering=False, debug=False)
    ap = lambda t: t.ap()
    x_in = ap(nc.dram_tensor("x", [S_CORE, HID], F32, kind="ExternalInput"))
    wq_in = ap(nc.dram_tensor("wq", [HID, HID], F32, kind="ExternalInput"))
    wk_in = ap(nc.dram_tensor("wk", [NKV * HD, HID], F32, kind="ExternalInput"))
    wv_in = ap(nc.dram_tensor("wv", [NKV * HD, HID], F32, kind="ExternalInput"))
    wo_in = ap(nc.dram_tensor("wo", [HID, HID], F32, kind="ExternalInput"))
    gate_in = ap(nc.dram_tensor("gate_w", [HID, INTER], F32, kind="ExternalInput"))
    up_in = ap(nc.dram_tensor("up_w", [HID, INTER], F32, kind="ExternalInput"))
    down_in = ap(nc.dram_tensor("down_w", [INTER, HID], F32, kind="ExternalInput"))
    ident_f = ap(nc.dram_tensor("ident_f", [128, 128], F32, kind="ExternalInput"))
    ident_b = ap(nc.dram_tensor("ident_b", [128, 128], BF16, kind="ExternalInput"))
    ones_r = ap(nc.dram_tensor("ones_r", [1, 128], F32, kind="ExternalInput"))
    qg_in = ap(nc.dram_tensor("qgain", [128, 128], F32, kind="ExternalInput"))
    kg_in = ap(nc.dram_tensor("kgain", [128, 128], F32, kind="ExternalInput"))
    gs_in = ap(nc.dram_tensor("gscale", [128, NOC], F32, kind="ExternalInput"))
    us_in = ap(nc.dram_tensor("uscale", [128, NOC], F32, kind="ExternalInput"))
    ds_in = ap(nc.dram_tensor("dscale", [128, HID], F32, kind="ExternalInput"))
    mask_in = ap(nc.dram_tensor("mask", [128, NTT * 1024], F8, kind="ExternalInput"))
    eps_in = ap(nc.dram_tensor("epsv", [128, 2], F32, kind="ExternalInput"))
    out_d = ap(nc.dram_tensor("out", [S_CORE, HID], F32, kind="ExternalOutput"))
    qsc = ap(nc.dram_tensor("qscratch", [NQH * HD, S_CORE], BF16))
    AGC = max(S_CORE, NKV * HD)
    AGR = NKV * HD + S_CORE
    ag_i = ap(nc.dram_tensor("ag_in", [AGR, AGC], BF16))
    ag_o = ap(nc.dram_tensor("ag_out", [N_CORES * AGR, AGC], BF16))
    uTd = ap(nc.dram_tensor("uTd", [NQH * HD, S_CORE], BF16))

    S = Sched()
    es = contextlib.ExitStack()
    sbE = lambda name, shape, dt: es.enter_context(nc.sbuf_tensor(name, shape, dt))

    with es:
        idf = sbE("idf", [128, 128], F32)
        idb = sbE("idb", [128, 128], BF16)
        onr = sbE("onr", [1, 128], F32)
        qg = sbE("qg", [128, 128], F32)
        kg = sbE("kg", [128, 128], F32)
        gs = sbE("gs", [128, NOC], F32)
        us = sbE("us", [128, NOC], F32)
        mask_sb = sbE("mask_sb", [128, NTT * 1024], F8)
        epsv = sbE("epsv_sb", [128, 2], F32)
        stat = sbE("stat", [128, 96], F32)
        mred = sbE("mred", [1, 2], F32)
        negm = sbE("negm", [128, 2], F32)
        recs = sbE("recs", [128, 4], F32)
        uhat = sbE("uhat", [128, 2 * 256], BF16)
        mpos = sbE("mpos", [128, 2], F32)
        asub = sbE("asub", [128, 2], F32)
        alpha = sbE("alpha", [128, 2], F32)
        maxv = sbE("maxv", [128, 2 * 16], F32)
        ffn_sb = sbE("ffn_sb", [128, NTT * HID], BF16)
        quad = es.enter_context(nc.psum_tensor("quad", [128, 2048], F32))
        pair = es.enter_context(nc.psum_tensor("pair", [128, 1024], F32))
        b6 = es.enter_context(nc.psum_tensor("b6", [128, 512], F32))
        b7 = es.enter_context(nc.psum_tensor("b7", [128, 512], F32))

        b6b = b6[:].bitcast(BF16)
        b7b = b7[:].bitcast(BF16)
        pairb = pair[:].bitcast(BF16)

        def bankq(i):
            return quad[:, i * 512:(i + 1) * 512]

        for dst, src in [(idf[:], ident_f), (idb[:], ident_b), (onr[:], ones_r),
                         (qg[:], qg_in), (kg[:], kg_in), (gs[:], gs_in),
                         (us[:], us_in), (mask_sb[:], mask_in), (epsv[:], eps_in)]:
            S.add("sync", lambda e, d=dst, s=src: e.dma_start(out=d, in_=s),
                  inc="c_ld", inc_n=16)

        es12 = contextlib.ExitStack()
        sb12 = lambda name, shape, dt: es12.enter_context(nc.sbuf_tensor(name, shape, dt))
        x_db = sb12("x_db", [128, 2 * HID], F32)
        xn_tm = sb12("xn_tm", [128, HID], BF16)
        xn_fm = sb12("xn_fm", [128, 16 * S_CORE], BF16)
        wnat = sb12("wnat", [128, 2 * HID], BF16)
        wT = sb12("wT", [128, 16 * 512], BF16)
        qraw = sb12("qraw", [128, 2 * 512], F32)
        g8 = sb12("g8", [128, 2 * HID], F8)
        u8 = sb12("u8", [128, 2 * HID], F8)
        d8 = sb12("d8", [128, 4 * 512], F8)
        silu2 = sb12("silu2", [128, 2 * 512], BF16)
        fuse = sb12("fuse", [128, NOC * 512], BF16)
        ds_bc = sb12("ds_bc", [128, HID], F32)
        zbuf = sb12("zbuf", [128, 512], BF16)
        stg4 = sb12("stg4", [128, 4 * 512], BF16)

        S.add("sync", lambda e: e.dma_start(out=ds_bc[:], in_=ds_in), inc="c_ld", inc_n=16)
        CLD = S.val("c_ld")

        AGC = max(S_CORE, NKV * HD)
        if AGC > S_CORE:
            S.add("vector", lambda e: e.memset(zbuf[:], 0.0), inc="p_zm")
            for rb in range(NKV * HD // 128):
                S.add("sync",
                      lambda e, rb=rb: e.dma_start(
                          out=ag_i[rb * 128:(rb + 1) * 128, S_CORE:AGC], in_=zbuf[:, 0:AGC - S_CORE]),
                      waits=[("p_zm", 1)], inc="p_zp", inc_n=16)
            ZP_W = [("p_zp", S.val("p_zp"))]
        else:
            ZP_W = []

        xr = x_in.rearrange("(a p) n -> a p n", p=128)
        xn_c = xn_fm[:, :].rearrange("p (c t) -> p c t", c=16)

        for tt in range(NTT):
            par = tt % 2
            sl = lambda k, w=1, par=par: stat[:, par * 48 + k: par * 48 + k + w]
            xs = x_db[:, par * HID:(par + 1) * HID]
            S.add("sync", lambda e, xs=xs, tt=tt: e.dma_start(out=xs, in_=xr[tt]),
                  waits=[("c_ld", CLD), ("p_ln", tt - 1)], inc=f"p_x{par}", inc_n=16)
            S.add("scalar",
                  lambda e, xs=xs, d=sl(0): e.activation(out=xn_tm[:], in_=xs,
                                                         func=AF.Identity, accum_out=d),
                  waits=[(f"p_x{par}", 16 * (tt // 2 + 1)), ("p_tT", tt * 4)])
            S.add("scalar",
                  lambda e, xs=xs, d=sl(1): e.activation(out=fuse[:, 0:HID].bitcast(BF16) if False else fuse[:, 0:HID], in_=xs,
                                                         func=AF.Square, accum_out=d))
            S.add("scalar", lambda e: e.drain(), inc="p_st")
            PST = S.val("p_st")

            def ln_stats(e, sl=sl):
                e.tensor_scalar(out=sl(2), in0=sl(0), scalar1=1.0 / HID, scalar2=0.0,
                                op0=ALU.mult, op1=ALU.add)
                e.drain()
                e.tensor_tensor(out=sl(4), in0=sl(2), in1=sl(2), op=ALU.mult)
                e.drain()
                return e.scalar_tensor_tensor(out=sl(5), in0=sl(1), scalar=1.0 / HID,
                                              in1=sl(4), op0=ALU.mult, op1=ALU.subtract)
            S.add("vector", ln_stats, waits=[("p_st", PST)], inc="p_ve")
            S.add("scalar", lambda e, sl=sl: e.activation(out=sl(6), in_=sl(5), func=AF.Sqrt,
                                                          bias=epsv[:, 0:1]),
                  waits=[("p_ve", S.val("p_ve"))])
            S.add("scalar", lambda e: e.drain(), inc="p_sq")
            PSQ = S.val("p_sq")

            def ln_apply(e, xs=xs, sl=sl):
                e.reciprocal(sl(7), sl(6))
                e.drain()
                return e.tensor_scalar(out=xn_tm[:], in0=xs, scalar1=sl(2), scalar2=sl(7),
                                       op0=ALU.subtract, op1=ALU.mult)
            S.add("vector", ln_apply, waits=[("p_sq", PSQ), ("p_tT", tt * 4)], inc="p_ln")
            PLN = S.val("p_ln")
            for qtr in range(4):
                for j in range(4):
                    c = qtr * 4 + j
                    w = []
                    if j == 0:
                        w = [("p_xnT", tt * 4 + qtr)]
                        if qtr == 0:
                            w.append(("p_ln", PLN))
                    S.add("tensor",
                          lambda e, c=c, j=j: e.transpose(b6b[:, j * 128:(j + 1) * 128],
                                                          xn_tm[:, c * 128:(c + 1) * 128],
                                                          idb[:]),
                          waits=w, inc=("p_tT" if j == 3 else None))
                S.add("vector",
                      lambda e, tt=tt, qtr=qtr: e.tensor_copy(
                          xn_c[:, qtr * 4:(qtr + 1) * 4, tt * 128:(tt + 1) * 128],
                          b6b[:, 0:512].rearrange("p (c t) -> p c t", c=4)),
                      waits=[("p_tT", S.val("p_tT"))], inc="p_xnT")
        XND = S.val("p_xnT")

        o_specs = ([("q", i) for i in range(NQO)] + [("k", i) for i in range(NKO)]
                   + [("v", i) for i in range(NKO)])
        w_dram = {"q": wq_in, "k": wk_in, "v": wv_in}
        n_stg = 0
        n_qk = 0

        for oi, (kind, osl) in enumerate(o_specs):
            wd = w_dram[kind].rearrange("(a p) n -> a p n", p=128)
            for osub in range(4):
                kpc = oi * 4 + osub
                pw = kpc % 2
                dst = wnat[:, pw * HID:(pw + 1) * HID]
                S.add("gpsimd", lambda e, d=dst, s=wd[osl * 4 + osub]: e.dma_start(out=d, in_=s),
                      waits=[("p_wT", (kpc - 1) * 16), ("c_ld", CLD)],
                      inc=f"p_wn{pw}", inc_n=16)
                PWN = S.val(f"p_wn{pw}")
                for ic in range(16):
                    trn = kpc * 16 + ic
                    off = (trn % 2) * 1024 + ((trn // 2) % 4) * 128
                    S.add("tensor",
                          lambda e, d=dst, ic=ic, off=off: e.transpose(
                              pairb[:, off:off + 128],
                              d[:, ic * 128:(ic + 1) * 128], idb[:]),
                          waits=([(f"p_wn{pw}", PWN)] if ic == 0 else []) +
                                [("p_wTe", trn - 1)],
                          inc="p_wT")
                    S.add("vector",
                          lambda e, ic=ic, osub=osub, off=off: e.tensor_copy(
                              wT[:, ic * 512 + osub * 128: ic * 512 + (osub + 1) * 128],
                              pairb[:, off:off + 128]),
                          waits=[("p_wT", S.val("p_wT"))], inc="p_wTe")
            WTE = S.val("p_wTe")

            for tt in range(NTT):
                bank = tt % 4
                for ic in range(16):
                    w = []
                    if ic == 0:
                        w = [("p_xnT", XND), ("p_wTe", WTE),
                             ("p_pev", S.val("p_pev") - 3)]
                    S.add("tensor",
                          lambda e, bank=bank, ic=ic, tt=tt: e.matmul(
                              bankq(bank), xn_c[:, ic, tt * 128:(tt + 1) * 128],
                              wT[:, ic * 512:(ic + 1) * 512],
                              start=(ic == 0), stop=(ic == 15)),
                          waits=w, inc=("p_mmP" if ic == 15 else None))
                MMP = S.val("p_mmP")
                slot = n_stg % 4
                nuse = n_stg // 4
                n_stg += 1
                stg_s = stg4[:, slot * 512:(slot + 1) * 512]
                if kind == "v":
                    S.add("vector",
                          lambda e, bank=bank, d=stg_s: e.tensor_copy(d, bankq(bank)),
                          waits=[("p_mmP", MMP), (f"p_stg{slot}", 16 * nuse)],
                          inc="p_pev")
                    S.add("scalar",
                          lambda e, s=stg_s, tt=tt, osl=osl: e.dma_start(
                              out=ag_i[NKV * HD + tt * 128: NKV * HD + (tt + 1) * 128,
                                       osl * 512:(osl + 1) * 512], in_=s),
                          waits=[("p_pev", S.val("p_pev"))], inc=f"p_stg{slot}", inc_n=16)
                else:
                    n_qk += 1
                    parq = tt % 2
                    slp = lambda k, w=1, parq=parq: stat[:, parq * 48 + 16 + k: parq * 48 + 16 + k + w]
                    qr = qraw[:, parq * 512:(parq + 1) * 512]
                    S.add("vector",
                          lambda e, bank=bank, qr=qr: e.tensor_copy(qr, bankq(bank)),
                          waits=[("p_mmP", MMP), ("p_rms", n_qk - 2)],
                          inc="p_pev")
                    PEV = S.val("p_pev")
                    for h4 in range(4):
                        S.add("scalar",
                              lambda e, qr=qr, h4=h4, d=slp(h4): e.activation(
                                  out=xn_tm[:, h4 * 128:(h4 + 1) * 128],
                                  in_=qr[:, h4 * 128:(h4 + 1) * 128],
                                  func=AF.Square, accum_out=d),
                              waits=([("p_pev", PEV), ("p_tT", NTT * 4)] if h4 == 0 else []))
                    S.add("scalar", lambda e: e.drain(), inc="p_ss")
                    S.add("vector",
                          lambda e, slp=slp: e.tensor_scalar(
                              out=slp(4, 4), in0=slp(0, 4), scalar1=1.0 / HD,
                              scalar2=0.0, op0=ALU.mult, op1=ALU.add),
                          waits=[("p_ss", S.val("p_ss"))], inc="p_rv")
                    S.add("scalar",
                          lambda e, slp=slp: e.activation(out=slp(8, 4), in_=slp(4, 4),
                                                          func=AF.Sqrt, bias=epsv[:, 1:2]),
                          waits=[("p_rv", S.val("p_rv"))])
                    S.add("scalar", lambda e: e.drain(), inc="p_sq2")

                    def rms_nr(e, slp=slp):
                        e.reciprocal(slp(28, 4), slp(8, 4))
                        return e.drain()
                    S.add("vector", rms_nr, waits=[("p_sq2", S.val("p_sq2"))], inc="p_nr")
                    gt = qg if kind == "q" else kg
                    for h4 in range(4):
                        w = [("p_qT", S.val("p_qT"))] if h4 == 0 else []
                        S.add("vector",
                              lambda e, qr=qr, h4=h4, gt=gt, slp=slp: e.scalar_tensor_tensor(
                                  out=silu2[:, h4 * 128:(h4 + 1) * 128],
                                  in0=qr[:, h4 * 128:(h4 + 1) * 128],
                                  scalar=slp(28 + h4), in1=gt[:],
                                  op0=ALU.mult, op1=ALU.mult),
                              waits=w, inc=("p_rms" if h4 == 3 else None))
                    PRMS = S.val("p_rms")
                    for h4 in range(4):
                        S.add("tensor",
                              lambda e, h4=h4: e.transpose(
                                  b6b[:, h4 * 128:(h4 + 1) * 128],
                                  silu2[:, h4 * 128:(h4 + 1) * 128], idb[:]),
                              waits=([("p_rms", PRMS), ("p_qTe", S.val("p_qTe"))]
                                     if h4 == 0 else []),
                              inc=("p_qT" if h4 == 3 else None))
                    S.add("vector",
                          lambda e, d=stg_s: e.tensor_copy(d, b6b[:, 0:512]),
                          waits=[("p_qT", S.val("p_qT")), (f"p_stg{slot}", 16 * nuse)],
                          inc="p_qTe")
                    if kind == "q":
                        dstq = qsc.rearrange("(h p) t -> p h t", p=128)[
                            :, osl * 4:osl * 4 + 4, tt * 128:(tt + 1) * 128]
                    else:
                        dstq = ag_i[0:NKV * HD, 0:S_CORE].rearrange("(h p) t -> p h t", p=128)[
                            :, osl * 4:osl * 4 + 4, tt * 128:(tt + 1) * 128]
                    S.add("scalar",
                          lambda e, s=stg_s, d=dstq: e.dma_start(
                              out=d, in_=s.rearrange("p (h t) -> p h t", h=4)),
                          waits=[("p_qTe", S.val("p_qTe"))], inc=f"p_stg{slot}", inc_n=16)
        STG_W = [(f"p_stg{i}", S.val(f"p_stg{i}")) for i in range(4)]
        PEV_ALL = S.val("p_pev")
        QTE_ALL = S.val("p_qTe")

        S.add("gpsimd",
              lambda e: e.collective_compute("AllGather", ALU.bypass,
                                             replica_groups=[list(range(N_CORES))],
                                             ins=[ag_i.opt()], outs=[ag_o.opt()]),
              waits=STG_W + ZP_W, inc="p_ag")

        gcol = gate_in.rearrange("(c p) n -> p c n", p=128)
        ucol = up_in.rearrange("(c p) n -> p c n", p=128)
        dcol = down_in.rearrange("(c p) n -> p c n", p=128)
        NOCP = NOC // 2
        ALLB = [quad[:, i * 512:(i + 1) * 512] for i in range(4)] + \
               [pair[:, 0:512], pair[:, 512:1024], b6[:], b7[:]]

        for oph in range(2):
            for ocr in range(NOCP):
                oc = oph * NOCP + ocr
                j = oc
                pw = j % 2
                gds = g8[:, pw * HID:(pw + 1) * HID]
                uds = u8[:, pw * HID:(pw + 1) * HID]
                S.add("gpsimd",
                      lambda e, d=gds, oc=oc: e.dma_start(
                          out=d.rearrange("p (c n) -> p c n", c=16),
                          in_=gcol[:, :, oc * 128:(oc + 1) * 128]),
                      waits=[("p_gmm", (j - 2) * NT5 + NT5)], inc=f"p_g8{pw}", inc_n=16)
                S.add("gpsimd",
                      lambda e, d=uds, oc=oc: e.dma_start(
                          out=d.rearrange("p (c n) -> p c n", c=16),
                          in_=ucol[:, :, oc * 128:(oc + 1) * 128]),
                      waits=[("p_umm", (j - 2) * NT5 + NT5)], inc=f"p_u8{pw}", inc_n=16)
                PG8 = S.val(f"p_g8{pw}")
                PU8 = S.val(f"p_u8{pw}")
                for t5 in range(NT5):
                    j2 = j * NT5 + t5
                    pw2 = j2 % 2
                    gp = pair[:, 0:512] if pw2 == 0 else b6[:]
                    upp = pair[:, 512:1024] if pw2 == 0 else b7[:]
                    for ic in range(16):
                        w = []
                        if ic == 0:
                            w = [(f"p_g8{pw}", PG8), ("p_xnT", XND), ("p_si", j2 - 1)]
                            if j2 <= 1:
                                w += [("p_pev", PEV_ALL), ("p_qTe", QTE_ALL)]
                            if oph == 1 and ocr == 0 and t5 <= 1:
                                w += [("p_dne", oph * 16 * NT5)]
                        S.add("tensor",
                              lambda e, gp=gp, gds=gds, ic=ic, t5=t5: e.matmul(
                                  gp, gds[:, ic * 128:(ic + 1) * 128],
                                  xn_c[:, ic, t5 * 512:(t5 + 1) * 512],
                                  start=(ic == 0), stop=(ic == 15)),
                              waits=w, inc=("p_gmm" if ic == 15 else None))
                    for ic in range(16):
                        w = []
                        if ic == 0:
                            w = [("p_fu", j2 - 1), (f"p_u8{pw}", PU8)]
                            if oph == 1 and ocr == 0 and t5 <= 1:
                                w += [("p_dne", oph * 16 * NT5)]
                        S.add("tensor",
                              lambda e, upp=upp, uds=uds, ic=ic, t5=t5: e.matmul(
                                  upp, uds[:, ic * 128:(ic + 1) * 128],
                                  xn_c[:, ic, t5 * 512:(t5 + 1) * 512],
                                  start=(ic == 0), stop=(ic == 15)),
                              waits=w, inc=("p_umm" if ic == 15 else None))
                    ps = pw2
                    S.add("scalar",
                          lambda e, gp=gp, oc=oc, ps=ps: e.activation(
                              out=silu2[:, ps * 512:(ps + 1) * 512], in_=gp, func=AF.Silu,
                              scale=gs[:, oc:oc + 1]),
                          waits=[("p_gmm", S.val("p_gmm")), ("p_fu", j2 - 1)], inc="p_si")
                    fsl = (ocr * NT5 + t5)
                    S.add("vector",
                          lambda e, upp=upp, oc=oc, ps=ps, fsl=fsl: e.scalar_tensor_tensor(
                              out=fuse[:, fsl * 512:(fsl + 1) * 512], in0=upp,
                              scalar=us[:, oc:oc + 1], in1=silu2[:, ps * 512:(ps + 1) * 512],
                              op0=ALU.mult, op1=ALU.mult),
                          waits=[("p_umm", S.val("p_umm")), ("p_si", S.val("p_si"))] +
                                ([("p_dnm", oph * 4 * NOCP if ocr == 0 and t5 == 0 else 0)]
                                 if oph >= 1 else []),
                          inc="p_fu")
            FUD = S.val("p_fu")
            for hs in range(4):
                for ocr in range(NOCP):
                    oc = oph * NOCP + ocr
                    jd = (oph * 4 + hs) * NOCP + ocr
                    sd = jd % 4
                    dds = d8[:, sd * 512:(sd + 1) * 512]
                    S.add("gpsimd",
                          lambda e, d=dds, oc=oc, hs=hs: e.dma_start(
                              out=d, in_=dcol[:, oc, hs * 512:(hs + 1) * 512]),
                          waits=[("p_dnm", jd - 3), ("p_fu", FUD if ocr == 0 else 0)],
                          inc=f"p_d8{sd}", inc_n=16)
                    PD8 = S.val(f"p_d8{sd}")
                    for t5 in range(NT5):
                        for ts in range(4):
                            bk = t5 * 4 + ts
                            fsl = (ocr * NT5 + t5)
                            w = []
                            if ts == 0 and t5 == 0:
                                w = [(f"p_d8{sd}", PD8), ("p_fu", FUD)]
                                if ocr == 0:
                                    w.append(("p_dne", (oph * 4 + hs) * 4 * NT5))
                            S.add("tensor",
                                  lambda e, bk=bk, fsl=fsl, ts=ts, dds=dds, ocr=ocr: e.matmul(
                                      ALLB[bk], fuse[:, fsl * 512 + ts * 128: fsl * 512 + (ts + 1) * 128],
                                      dds, start=(ocr == 0), stop=(ocr == NOCP - 1)),
                                  waits=w, inc=("p_dnm" if (ts == 3 and t5 == NT5 - 1) else None),
                                  tag=f"dn{oph}.{hs}.{ocr}.{t5}.{ts}")
                DNM = S.val("p_dnm")
                for t5 in range(NT5):
                    for ts in range(4):
                        bk = t5 * 4 + ts
                        tt = t5 * 4 + ts
                        if oph == 0:
                            S.add("vector",
                                  lambda e, bk=bk, tt=tt, hs=hs: e.tensor_tensor(
                                      out=ffn_sb[:, tt * HID + hs * 512: tt * HID + (hs + 1) * 512],
                                      in0=ALLB[bk], in1=ds_bc[:, hs * 512:(hs + 1) * 512],
                                      op=ALU.mult),
                                  waits=[("p_dnm", DNM)], inc="p_dne")
                        else:
                            def ev2(e, bk=bk, tt=tt, hs=hs):
                                qtmp = qraw[:, (bk % 2) * 512:(bk % 2) * 512 + 512]
                                e.tensor_tensor(out=qtmp,
                                                in0=ALLB[bk], in1=ds_bc[:, hs * 512:(hs + 1) * 512],
                                                op=ALU.mult)
                                e.drain()
                                return e.tensor_tensor(
                                    out=ffn_sb[:, tt * HID + hs * 512: tt * HID + (hs + 1) * 512],
                                    in0=qtmp,
                                    in1=ffn_sb[:, tt * HID + hs * 512: tt * HID + (hs + 1) * 512],
                                    op=ALU.add)
                            S.add("vector", ev2, waits=[("p_dnm", DNM)], inc="p_dne")
        GMM_ALL = S.val("p_gmm")
        UMM_ALL = S.val("p_umm")
        DNE_ALL = S.val("p_dne")
        DNM_ALL = S.val("p_dnm")
        FU_ALL = S.val("p_fu")
        SI_ALL = S.val("p_si")

        es12.close()
        es3 = contextlib.ExitStack()
        sb3 = lambda name, shape, dt: es3.enter_context(nc.sbuf_tensor(name, shape, dt))
        sc_sb = sb3("sc_sb", [128, 2 * (NKT // 4) * 256], F32)
        p_sb = sb3("p_sb", [128, 2 * (NKT // 4) * 256], BF16)
        usb = sb3("usb", [128, 2 * 512], F32)
        EVSEM = {}
        kbuf = sb3("kbuf", [128, 2 * NKT * 128], BF16)
        vbuf = sb3("vbuf", [128, 2 * NKT * 129], BF16)
        qbuf = sb3("qbuf", [128, 2 * S_CORE], BF16)
        ustg = sb3("ustg", [128, 2 * 128], BF16)

        agv = ag_o.rearrange("(r f p) t -> r f p t", r=N_CORES, p=128)
        vview = vbuf[:, :].rearrange("p (b k) -> p b k", k=129)
        NQT = NTT // 2
        NSTG = 4
        NKS = NKT // NSTG
        NGE = NKS // 2
        NST = NQH * NQT * NSTG
        NCH2 = 2
        GPC2 = NGE // NCH2
        DNM_ALL = S.val("p_dnm")
        DNE_ALL = S.val("p_dne")
        FU_ALL = S.val("p_fu")
        GMM_ALL = S.val("p_gmm")
        UMM_ALL = S.val("p_umm")

        def grp_eng(st_idx, ge):
            u, st = divmod(st_idx, NSTG)
            masked = (st == NSTG - 1) and (ge >= NGE - 4)
            if masked:
                return "vector_mask"
            return "scalar" if ge >= NGE // 2 else "vector"

        S.add("vector", lambda e: e.memset(vview[:, :, 128:129], 1.0),
              waits=[("p_dne", DNE_ALL)], inc="p_on")

        def load_kv(h):
            pk = h % 2
            kd = kbuf[:, pk * NKT * 128:(pk + 1) * NKT * 128]
            w = [("p_ag", 1), ("p_dnm", DNM_ALL), ("p_on", 1)]
            if h >= 2:
                w.append(("p_sc2", 2 * (h - 1) * NTT // 2 * NSTG * NGE))
                w.append(("p_ue", NSTG * (2 * (h - 1) * (NTT // 2))))
            S.add("sync",
                  lambda e, kd=kd, h=h: e.dma_start(
                      out=kd.rearrange("p (r t) -> p r t", r=N_CORES),
                      in_=agv[:, h, :, 0:S_CORE].transpose([1, 0, 2])),
                  waits=w, inc=f"p_kv{pk}", inc_n=16)
            vdst = vbuf[:, :].rearrange("p (b2 r j k) -> p b2 r j k",
                                        b2=2, r=N_CORES, j=NTT)[:, pk, :, :, 0:128]
            for r in range(N_CORES):
                S.add("sync",
                      lambda e, h=h, vdst=vdst, r=r: e.dma_start(
                          out=vdst[:, r],
                          in_=agv[r, NKV:NKV + NTT, :, h * 128:(h + 1) * 128]
                              .transpose([1, 0, 2])),
                      inc=f"p_kv{pk}", inc_n=16)

        def load_q(qh):
            pq = qh % 2
            w = STG_W + [("p_ln", NTT), ("p_dnm", DNM_ALL)]
            if qh >= 2:
                w.append(("p_sc2", (qh - 1) * NTT // 2 * NSTG * NGE))
            S.add("sync",
                  lambda e, qh=qh, pq=pq: e.dma_start(
                      out=qbuf[:, pq * S_CORE:(pq + 1) * S_CORE],
                      in_=qsc[qh * 128:(qh + 1) * 128, :]),
                  waits=w, inc=f"p_q{pq}", inc_n=16)

        load_kv(0)
        load_kv(1)
        load_q(0)
        load_q(1)

        def emit_av_mm(st_idx):
            u, st = divmod(st_idx, NSTG)
            qh, q2 = divmod(u, NQT)
            hk = qh // 2
            pk = hk % 2
            pu = u % 2
            avb = pair[:, pu * 512:(pu + 1) * 512]
            QSO = (0, 136)
            for qs in range(2):
                for ch in range(NCH2):
                    for jj in range(GPC2 * 2):
                        kt_rel = ch * GPC2 * 2 + jj
                        kt = st * NKS + kt_rel
                        w = []
                        if jj == 0 and qs == 0:
                            w = [("p_ex", st_idx * NCH2 + ch + 1)]
                            if ch == 0:
                                if st == 0:
                                    w.append(("p_fu", FU_ALL))
                                    w.append(("p_ue", NSTG * (u - 1)))
                                else:
                                    w.append(("p_ue", NSTG * u + st))
                        spb = (st_idx % 2) * (NKT // 4) * 256
                        S.add("tensor",
                              lambda e, kt=kt, kt_rel=kt_rel, qs=qs, pk=pk, avb=avb, spb=spb: e.matmul(
                                  avb[:, QSO[qs]: QSO[qs] + 129],
                                  p_sb[:, spb + kt_rel * 256 + qs * 128: spb + kt_rel * 256 + (qs + 1) * 128],
                                  vview[:, pk * NKT + kt, :],
                                  start=(kt_rel == 0), stop=(kt_rel == NKS - 1)),
                              waits=w,
                              inc=("p_av" if (jj == GPC2 * 2 - 1 and qs == 1
                                              and ch == NCH2 - 1) else None),
                              tag=f"av{st_idx}.{qs}.{kt_rel}")

        def emit_av_vec(st_idx):
            u, st = divmod(st_idx, NSTG)
            qh, q2 = divmod(u, NQT)
            pu = u % 2
            avb = pair[:, pu * 512:(pu + 1) * 512]
            if st == 0:
                def park(e, pu=pu):
                    e.tensor_copy(usb[:, pu * 512 + 0: pu * 512 + 129], avb[:, 0:129])
                    return e.tensor_copy(usb[:, pu * 512 + 256: pu * 512 + 385], avb[:, 136:265])
                S.add("vector", park,
                      waits=[("p_av", st_idx + 1), ("p_ue", NSTG * u)], inc="p_ue")
                return
            last = (st == NSTG - 1)

            def comb(e, pu=pu, last=last, aslot=st_idx % 2):
                QSO2 = (0, 136)
                for qs in range(2):
                    e.scalar_tensor_tensor(
                        out=usb[:, pu * 512 + qs * 256: pu * 512 + qs * 256 + 129],
                        in0=usb[:, pu * 512 + qs * 256: pu * 512 + qs * 256 + 129],
                        scalar=alpha[:, aslot:aslot + 1],
                        in1=avb[:, QSO2[qs]: QSO2[qs] + 129],
                        op0=ALU.mult, op1=ALU.add)
                if not last:
                    return e.drain()
                e.drain()
                e.reciprocal(recs[:, pu * 2:pu * 2 + 1], usb[:, pu * 512 + 128: pu * 512 + 129])
                e.reciprocal(recs[:, pu * 2 + 1:pu * 2 + 2], usb[:, pu * 512 + 384: pu * 512 + 385])
                e.drain()
                e.tensor_scalar(out=uhat[:, pu * 256: pu * 256 + 128],
                                in0=usb[:, pu * 512: pu * 512 + 128],
                                scalar1=recs[:, pu * 2:pu * 2 + 1], scalar2=0.0,
                                op0=ALU.mult, op1=ALU.add)
                return e.tensor_scalar(out=uhat[:, pu * 256 + 128: pu * 256 + 256],
                                       in0=usb[:, pu * 512 + 256: pu * 512 + 384],
                                       scalar1=recs[:, pu * 2 + 1:pu * 2 + 2], scalar2=0.0,
                                       op0=ALU.mult, op1=ALU.add)
            S.add("vector", comb,
                  waits=[("p_av", st_idx + 1), ("p_al", 3 * u + st),
                         ("p_uT", 2 * (u - 1) + 2)], inc="p_ue", tag=f"comb{st_idx}")
            if not last:
                return
            for qs in range(2):
                n_u = 2 * u + qs
                up_ = n_u % 2
                ub = b6b
                S.add("tensor",
                      lambda e, ub=ub, pu=pu, qs=qs: e.transpose(
                          ub[:, 0:128], uhat[:, pu * 256 + qs * 128: pu * 256 + (qs + 1) * 128],
                          idb[:]),
                      waits=[("p_ue", S.val("p_ue")), ("p_uf", 2 * u + qs)],
                      inc="p_uT")
                S.add("vector",
                      lambda e, ub=ub, up_=up_: e.tensor_copy(
                          ustg[:, up_ * 128:(up_ + 1) * 128], ub[:, 0:128]),
                      waits=[("p_uT", S.val("p_uT")),
                             (f"p_us{up_}", 16 * (n_u // 2))], inc="p_uf")
                dst = uTd.rearrange("(c p) t -> p c t", p=128)[
                    :, qh, q2 * 256 + qs * 128: q2 * 256 + (qs + 1) * 128]
                S.add("scalar",
                      lambda e, dst=dst, up_=up_: e.dma_start(
                          out=dst, in_=ustg[:, up_ * 128:(up_ + 1) * 128]),
                      waits=[("p_uf", S.val("p_uf"))], inc=f"p_us{up_}", inc_n=16)

        def emit_mred(sti):
            mslot = sti % 2
            S.add("vector",
                  lambda e, mslot=mslot: e.tensor_reduce(
                      mred[:, mslot:mslot + 1], b7[0:1, mslot * 256: mslot * 256 + 128],
                      AXX, ALU.max),
                  waits=[("p_mxT", sti + 1), ("p_bc", sti - 1)], inc="p_mv2")

        def emit_bc(sti):
            mslot = sti % 2
            S.add("tensor",
                  lambda e, mslot=mslot: e.matmul(
                      b7[:, mslot * 256 + 128: mslot * 256 + 129], onr[:],
                      mred[:, mslot:mslot + 1], start=True, stop=True),
                  waits=[("p_mv2", sti + 1), ("p_ng", sti - 1)], inc="p_bc")

        def emit_mxT(sti):
            mslot = sti % 2
            S.add("tensor",
                  lambda e, mslot=mslot: e.transpose(
                      b7[0:1, mslot * 256: mslot * 256 + 128],
                      stat[:, 90 + mslot:91 + mslot], idf[:]),
                  waits=[("p_mv1", sti + 1), ("p_mv2", sti - 1)], inc="p_mxT")

        def emit_ngf(sti):
            u, st = divmod(sti, NSTG)
            mslot = sti % 2
            pu = u % 2
            ngw = [("p_bc", sti + 1)]
            if sti >= 2:
                ngw.append(("p_ex", (sti - 1) * NCH2))
            if st >= 1:
                prev = sti - 2 if (sti - 2) % NSTG != 0 else sti - 4
                if prev >= 1:
                    pa = 3 * (prev // NSTG) + (prev % NSTG)
                    if pa > 0:
                        ngw.append(("p_al", pa))

            def ngf(e, mslot=mslot, st=st, pu=pu):
                bcv = b7[:, mslot * 256 + 128: mslot * 256 + 129]
                if st == 0:
                    e.tensor_scalar(out=negm[:, mslot:mslot + 1], in0=bcv,
                                    scalar1=-1.0, scalar2=0.0, op0=ALU.mult, op1=ALU.add)
                    return e.tensor_copy(mpos[:, pu:pu + 1], bcv)
                e.tensor_tensor(out=stat[:, 94:95], in0=bcv,
                                in1=mpos[:, pu:pu + 1], op=ALU.max)
                e.drain()
                e.tensor_scalar(out=negm[:, mslot:mslot + 1], in0=stat[:, 94:95],
                                scalar1=-1.0, scalar2=0.0, op0=ALU.mult, op1=ALU.add)
                e.tensor_tensor(out=asub[:, mslot:mslot + 1], in0=mpos[:, pu:pu + 1],
                                in1=stat[:, 94:95], op=ALU.subtract)
                e.drain()
                return e.tensor_copy(mpos[:, pu:pu + 1], stat[:, 94:95])
            S.add("vector", ngf, waits=ngw, inc="p_ng")

        def emit_alpha_exp(sti):
            u, st = divmod(sti, NSTG)
            mslot = sti % 2
            NG = sti + 1
            if st >= 1:
                aw = [("p_ng", NG), ("p_ue", sti - 1)]
                S.add("scalar",
                      lambda e, mslot=mslot: e.activation(out=alpha[:, mslot:mslot + 1],
                                                          in_=asub[:, mslot:mslot + 1],
                                                          func=AF.Exp),
                      waits=aw, inc="p_al")
            for ch in range(NCH2):
                w = [("p_ng", NG)]
                best = {}
                for gg in range(ch * GPC2, (ch + 1) * GPC2):
                    nm, vv = EVSEM[(sti, gg)]
                    best[nm] = max(best.get(nm, 0), vv)
                w += list(best.items())
                if sti >= 2:
                    w.append(("p_av", sti - 1))
                spb = (sti % 2) * (NKT // 4) * 256
                ssc2 = (sti % 2) * (NKT // 4) * 256
                S.add("scalar",
                      lambda e, ch=ch, mslot=mslot, spb=spb, ssc2=ssc2: e.activation(
                          out=p_sb[:, spb + ch * GPC2 * 512: spb + (ch + 1) * GPC2 * 512],
                          in_=sc_sb[:, ssc2 + ch * GPC2 * 512: ssc2 + (ch + 1) * GPC2 * 512],
                          func=AF.Exp, bias=negm[:, mslot:mslot + 1], scale=1.0),
                      waits=w, inc="p_ex")

        for it in range(NST + 2):
            if 1 <= it <= NST:
                emit_mred(it - 1)
            if it < NST:
                sti = it
                u, st = divmod(sti, NSTG)
                qh, q2 = divmod(u, NQT)
                hk = qh // 2
                pk = hk % 2
                pq = qh % 2
                mslot = sti % 2
                if st == 0 and q2 == 0:
                    if qh + 2 < NQH:
                        load_q(qh + 2)
                    if qh % 2 == 0 and hk + 2 < NKV:
                        load_kv(hk + 2)
                nvec = 0
                for ge in range(NGE):
                    for j in range(2):
                        kt_rel = ge * 2 + j
                        kt = st * NKS + kt_rel
                        slot = kt_rel % 8
                        w = []
                        if j == 0:
                            if ge >= 4 or sti > 0:
                                psti, pg = (sti, ge - 4) if ge >= 4 else (sti - 1, NGE + ge - 4)
                                w.append(EVSEM[(psti, pg)])
                            w += [(f"p_kv{pk}", 16 * (1 + N_CORES) * (hk // 2 + 1)),
                                  (f"p_q{pq}", 16 * (qh // 2 + 1))]
                            if sti == 0 and ge == 0:
                                w.append(("p_dne", DNE_ALL))
                        S.add("tensor",
                              lambda e, slot=slot, kt=kt, pk=pk, pq=pq, q2=q2: e.matmul(
                                  quad[:, slot * 256:(slot + 1) * 256],
                                  kbuf[:, pk * NKT * 128 + kt * 128: pk * NKT * 128 + (kt + 1) * 128],
                                  qbuf[:, pq * S_CORE + q2 * 256: pq * S_CORE + (q2 + 1) * 256],
                                  start=True, stop=True),
                              waits=w, inc=("p_sc2" if j == 1 else None))
                    SC2 = S.val("p_sc2")
                    eng = grp_eng(sti, ge)
                    bank = (2 * ge % 8) // 2
                    ssc = (sti % 2) * (NKT // 4) * 256
                    ew = [("p_sc2", SC2), ("p_ex", (sti - 2) * NCH2 + ge // GPC2 + 1)]
                    if sti == 0 and ge == 0:
                        ew += [("p_gmm", GMM_ALL), ("p_umm", UMM_ALL)]
                    if eng == "vector":
                        S.add("vector",
                              lambda e, bank=bank, ge=ge, mslot=mslot, nv=nvec, ssc=ssc: e.tensor_scalar(
                                  out=sc_sb[:, ssc + ge * 512: ssc + (ge + 1) * 512],
                                  in0=quad[:, bank * 512:(bank + 1) * 512],
                                  scalar1=float(C_SCALE), scalar2=NEG_BIG,
                                  op0=ALU.mult, op1=ALU.max,
                                  accum_out=maxv[:, mslot * 16 + nv: mslot * 16 + nv + 1]),
                              waits=ew, inc="p_evv")
                        EVSEM[(sti, ge)] = ("p_evv", S.val("p_evv"))
                        nvec += 1
                    elif eng == "vector_mask":
                        mg = ge - (NGE - 4)
                        S.add("vector",
                              lambda e, bank=bank, ge=ge, mg=mg, q2=q2, ssc=ssc: e.scalar_tensor_tensor(
                                  out=sc_sb[:, ssc + ge * 512: ssc + (ge + 1) * 512],
                                  in0=quad[:, bank * 512:(bank + 1) * 512],
                                  scalar=float(C_SCALE),
                                  in1=mask_sb[:, q2 * 2048 + mg * 512: q2 * 2048 + (mg + 1) * 512],
                                  op0=ALU.mult, op1=ALU.add),
                              waits=ew, inc="p_evv")
                        EVSEM[(sti, ge)] = ("p_evv", S.val("p_evv"))
                    else:
                        S.add("scalar",
                              lambda e, bank=bank, ge=ge, ssc=ssc: e.mul(
                                  sc_sb[:, ssc + ge * 512: ssc + (ge + 1) * 512],
                                  quad[:, bank * 512:(bank + 1) * 512], float(C_SCALE)),
                              waits=ew, inc="p_evs")
                        EVSEM[(sti, ge)] = ("p_evs", S.val("p_evs"))
                    if ge == 1 and it >= 1:
                        emit_bc(it - 1)
                        emit_ngf(it - 1)
                NV = nvec
                assert NV > 0 or st > 0

                def mx_red(e, mslot=mslot, NV=NV, st=st, pu=u % 2):
                    e.drain()
                    if st == 0:
                        return e.tensor_reduce(stat[:, 90 + mslot:91 + mslot],
                                               maxv[:, mslot * 16: mslot * 16 + NV],
                                               AXX, ALU.max)
                    if NV == 0:
                        return e.tensor_scalar(out=stat[:, 90 + mslot:91 + mslot],
                                               in0=epsv[:, 0:1], scalar1=0.0,
                                               scalar2=NEG_BIG, op0=ALU.mult, op1=ALU.add)
                    return e.tensor_reduce(stat[:, 90 + mslot:91 + mslot],
                                           maxv[:, mslot * 16: mslot * 16 + NV],
                                           AXX, ALU.max)
                mrw = []
                if sti >= 2:
                    mrw.append(("p_mxT", sti - 1))
                S.add("vector", mx_red, waits=mrw, inc="p_mv1")
                if it >= 1:
                    emit_alpha_exp(it - 1)
            if it >= NST and 1 <= it <= NST:
                emit_bc(it - 1)
                emit_ngf(it - 1)
                emit_alpha_exp(it - 1)
            if 2 <= it <= NST + 1:
                emit_av_mm(it - 2)
                emit_av_vec(it - 2)
            if it < NST:
                emit_mxT(it)
        EX_ALL = S.val("p_ex")
        AVC_ALL = S.val("p_av")
        UF_ALL = S.val("p_uf")
        SCG_ALL = S.val("p_sc2")
        EV_ALL = S.val("p_evv")
        EVS_ALL = S.val("p_evs")
        UE_ALL = S.val("p_ue")

        US_W = [(f"p_us{i}", S.val(f"p_us{i}")) for i in range(2)]
        UE_ALL2 = S.val("p_ue")
        es3.close()
        es4 = contextlib.ExitStack()
        sb4 = lambda name, shape, dt: es4.enter_context(nc.sbuf_tensor(name, shape, dt))
        uT4 = sb4("uT4", [128, 2 * 2048], BF16)
        woT = sb4("woT", [128, 16 * 512], BF16)
        wonat = sb4("wonat", [128, 2 * HID], BF16)
        xrl = sb4("xrl", [128, 4 * 512], F32)
        osb = sb4("osb", [128, 4 * 512], F32)
        tmp4 = sb4("tmp4", [128, 4 * 512], F32)

        wor = wo_in.rearrange("(a p) n -> a p n", p=128)
        n_x = 0
        for hs in range(4):
            for osub in range(4):
                kpc = hs * 4 + osub
                pw = kpc % 2
                dst = wonat[:, pw * HID:(pw + 1) * HID]
                S.add("gpsimd",
                      lambda e, d=dst, kpc=kpc: e.dma_start(out=d, in_=wor[kpc]),
                      waits=[("p_ex", EX_ALL), ("p_av", AVC_ALL), ("p_ue", UE_ALL2),
                             ("p_sc2", SCG_ALL), ("p_woT", (kpc - 1) * 16)] + US_W,
                      inc=f"p_won{pw}", inc_n=16)
                PWON = S.val(f"p_won{pw}")
                for ic in range(16):
                    trn = kpc * 16 + ic
                    off = (trn % 2) * 1024 + ((trn // 2) % 4) * 128
                    S.add("tensor",
                          lambda e, d=dst, ic=ic, off=off: e.transpose(
                              pairb[:, off:off + 128],
                              d[:, ic * 128:(ic + 1) * 128], idb[:]),
                          waits=([(f"p_won{pw}", PWON), ("p_uf", UF_ALL),
                                  ("p_ue", UE_ALL), ("p_evs", EVS_ALL)] if ic == 0 else []) +
                                [("p_woTe", trn - 1)],
                          inc="p_woT")
                    S.add("vector",
                          lambda e, ic=ic, osub=osub, off=off: e.tensor_copy(
                              woT[:, ic * 512 + osub * 128: ic * 512 + (osub + 1) * 128],
                              pairb[:, off:off + 128]),
                          waits=[("p_woT", S.val("p_woT"))], inc="p_woTe")
            WOTE = S.val("p_woTe")
            for tt in range(NTT):
                bank = tt % 4
                slot = n_x % 4
                nuse = n_x // 4
                up4 = n_x % 2
                n_x += 1
                S.add("sync",
                      lambda e, slot=slot, tt=tt, hs=hs: e.dma_start(
                          out=xrl[:, slot * 512:(slot + 1) * 512],
                          in_=xr[tt][:, hs * 512:(hs + 1) * 512]),
                      waits=[(f"p_out{slot}", 16 * nuse), ("p_ex", EX_ALL),
                             ("p_ue", UE_ALL2)] + US_W,
                      inc=f"p_xr{slot}", inc_n=16)
                S.add("scalar",
                      lambda e, up4=up4, tt=tt: e.dma_start(
                          out=uT4[:, up4 * 2048:(up4 + 1) * 2048].rearrange(
                              "p (c t) -> p c t", c=16),
                          in_=uTd.rearrange("(c p) t -> p c t", p=128)[
                              :, :, tt * 128:(tt + 1) * 128]),
                      waits=US_W + [("p_wom", S.val("p_wom") - 1)],
                      inc=f"p_ut4{up4}", inc_n=16)
                PUT4 = S.val(f"p_ut4{up4}")
                for ic in range(16):
                    w = []
                    if ic == 0:
                        w = [("p_woTe", WOTE), ("p_cmb", S.val("p_cmb") - 3),
                             ("p_evv", EV_ALL), ("p_evs", EVS_ALL), ("p_uf", UF_ALL),
                             (f"p_ut4{up4}", PUT4)]
                    S.add("tensor",
                          lambda e, bank=bank, ic=ic, up4=up4: e.matmul(
                              bankq(bank),
                              uT4[:, up4 * 2048 + ic * 128: up4 * 2048 + (ic + 1) * 128],
                              woT[:, ic * 512:(ic + 1) * 512],
                              start=(ic == 0), stop=(ic == 15)),
                          waits=w, inc=("p_wom" if ic == 15 else None))
                WOM = S.val("p_wom")

                def combine(e, bank=bank, slot=slot, tt=tt, hs=hs):
                    e.tensor_tensor(out=tmp4[:, slot * 512:(slot + 1) * 512],
                                    in0=bankq(bank), in1=xrl[:, slot * 512:(slot + 1) * 512],
                                    op=ALU.add)
                    e.drain()
                    return e.tensor_tensor(
                        out=osb[:, slot * 512:(slot + 1) * 512],
                        in0=tmp4[:, slot * 512:(slot + 1) * 512],
                        in1=ffn_sb[:, tt * HID + hs * 512: tt * HID + (hs + 1) * 512],
                        op=ALU.add)
                S.add("vector", combine,
                      waits=[("p_wom", WOM), (f"p_xr{slot}", 16 * (nuse + 1)),
                             (f"p_out{slot}", 16 * nuse)],
                      inc="p_cmb")
                S.add("sync",
                      lambda e, slot=slot, tt=tt, hs=hs: e.dma_start(
                          out=out_d[tt * 128:(tt + 1) * 128, hs * 512:(hs + 1) * 512],
                          in_=osb[:, slot * 512:(slot + 1) * 512]),
                      waits=[("p_cmb", S.val("p_cmb"))], inc=f"p_out{slot}", inc_n=16)
        OUT_W = [(f"p_out{i}", S.val(f"p_out{i}")) for i in range(4)]
        S.add("sync", lambda e: e.nop(), waits=OUT_W)
        es4.close()

        S.validate()
        with contextlib.ExitStack() as es_sem:
            sems = {n: es_sem.enter_context(nc.semaphore(n)) for n in S.names}
            import os
            if os.environ.get("KDBG"):
                for n, h in sems.items():
                    print("SEMMAP", getattr(h, "index", getattr(h, "id", h)), n)
            with nc.Block() as block:
                block.sync(lambda e: S.emit("sync", e, sems))
                block.scalar(lambda e: S.emit("scalar", e, sems))
                block.vector(lambda e: S.emit("vector", e, sems))
                block.tensor(lambda e: S.emit("tensor", e, sems))
                block.gpsimd(lambda e: S.emit("gpsimd", e, sems))

    meta = dict(S_CORE=S_CORE)
    return nc, meta


def build_consts(S_CORE, qnw, knw, gsc, usc, dsc):
    S_TOT = N_CORES * S_CORE
    NTT = S_CORE // 128
    NKT = S_TOT // 128
    NOC = INTER // 128
    f32 = np.float32
    bf = ml_dtypes.bfloat16
    f8 = ml_dtypes.float8_e4m3fn
    c = {
        "ident_f": np.eye(128, dtype=f32),
        "ident_b": np.eye(128).astype(bf),
        "ones_r": np.ones((1, 128), f32),
        "qgain": np.ascontiguousarray(np.broadcast_to(np.asarray(qnw, f32), (128, 128))),
        "kgain": np.ascontiguousarray(np.broadcast_to(np.asarray(knw, f32), (128, 128))),
        "gscale": np.ascontiguousarray(np.asarray(gsc, f32).reshape(NOC, 128).T),
        "uscale": np.ascontiguousarray(np.asarray(usc, f32).reshape(NOC, 128).T),
        "dscale": np.ascontiguousarray(np.broadcast_to(np.asarray(dsc, f32), (128, HID))),
        "epsv": np.ascontiguousarray(np.broadcast_to(np.array([LN_EPS, RMS_EPS], f32), (128, 2))),
    }
    masks = []
    for core in range(N_CORES):
        m = np.zeros((128, NTT * 1024), f32)
        for qt in range(NTT):
            for gi in range(2):
                for j in range(4):
                    kt = NKT - 8 + gi * 4 + j
                    kgl = kt * 128 + np.arange(128)[:, None]
                    qgl = core * S_CORE + qt * 128 + np.arange(128)[None, :]
                    msk = (kgl >= S_TOT - WINDOW) & (kgl <= qgl)
                    m[:, qt * 1024 + gi * 512 + j * 128: qt * 1024 + gi * 512 + (j + 1) * 128] = \
                        np.where(msk, MASK_NEG, 0.0)
        masks.append(m.astype(f8))
    return c, masks


_CACHE = {}


def kernel(L0_attn_hidden_states, L0_attn_Wq, L0_attn_Wk, L0_attn_Wv, L0_attn_Wo,
           L0_attn_q_norm_w, L0_attn_k_norm_w, L0_ffn_gate_w, L0_ffn_up_w,
           L0_ffn_down_w, L0_ffn_gate_scale, L0_ffn_up_scale, L0_ffn_down_scale):
    x = np.asarray(L0_attn_hidden_states)
    B, L, H = x.shape
    S_CORE = L // N_CORES
    if S_CORE not in _CACHE:
        _CACHE[S_CORE] = build_program(S_CORE)
    nc, meta = _CACHE[S_CORE]
    consts, masks = build_consts(S_CORE, L0_attn_q_norm_w, L0_attn_k_norm_w,
                                 L0_ffn_gate_scale, L0_ffn_up_scale, L0_ffn_down_scale)
    shared = {
        "wq": np.ascontiguousarray(np.asarray(L0_attn_Wq, np.float32)),
        "wk": np.ascontiguousarray(np.asarray(L0_attn_Wk, np.float32)),
        "wv": np.ascontiguousarray(np.asarray(L0_attn_Wv, np.float32)),
        "wo": np.ascontiguousarray(np.asarray(L0_attn_Wo, np.float32)),
        "gate_w": np.ascontiguousarray(np.asarray(L0_ffn_gate_w, np.float32)),
        "up_w": np.ascontiguousarray(np.asarray(L0_ffn_up_w, np.float32)),
        "down_w": np.ascontiguousarray(np.asarray(L0_ffn_down_w, np.float32)),
    }
    shared.update(consts)
    xs = np.ascontiguousarray(x.reshape(L, H).astype(np.float32))
    in_maps = []
    for core in range(N_CORES):
        m = dict(shared)
        m["x"] = xs[core * S_CORE:(core + 1) * S_CORE]
        m["mask"] = masks[core]
        in_maps.append(m)
    res = run_bass_kernel_spmd(nc, in_maps, core_ids=list(range(N_CORES)))
    out = np.concatenate([res.results[c]["out"] for c in range(N_CORES)], axis=0)
    return out.reshape(B, L, H)


# version 12
# speedup vs baseline: 1.2721x; 1.0521x over previous
import math
import contextlib
import numpy as np
import ml_dtypes
import concourse.bass as bass
import concourse.mybir as mybir
from concourse.bass_utils import run_bass_kernel_spmd

F32 = mybir.dt.float32
BF16 = mybir.dt.bfloat16
F8 = mybir.dt.float8e4
AF = mybir.ActivationFunctionType
ALU = mybir.AluOpType
AXX = mybir.AxisListType.X

N_CORES = 8
HID = 2048
INTER = 6144
NQH = 16
NKV = 8
HD = 128
WINDOW = 1024
C_SCALE = 1.0 / math.sqrt(HD)
LN_EPS = 1e-5
RMS_EPS = 1e-6
MASK_NEG = -240.0
NEG_BIG = -3.0e38

ENGS = ("sync", "scalar", "vector", "tensor", "gpsimd")


class Sched:

    def __init__(self):
        self.steps = {e: [] for e in ENGS}
        self.ctr = {}
        self.names = []

    def sem(self, name):
        if name not in self.ctr:
            self.ctr[name] = 0
            self.names.append(name)
        return name

    def val(self, name):
        self.sem(name)
        return self.ctr[name]

    def add(self, eng, fn, waits=(), inc=None, inc_n=1, tag=None):
        w = tuple((self.sem(s), v) for (s, v) in waits if v > 0)
        if inc is not None:
            self.sem(inc)
            self.ctr[inc] += inc_n
        self.steps[eng].append((w, fn, inc, inc_n, tag))

    def validate(self):
        for e in ENGS:
            for (waits, _f, _i, _n, _t) in self.steps[e]:
                for (s, v) in waits:
                    assert v <= self.ctr[s], f"unsatisfiable wait {s}>={v} (max {self.ctr[s]})"

    def emit(self, eng_name, eng, sems):
        for (waits, fn, inc, inc_n, tag) in self.steps[eng_name]:
            for (s, v) in waits:
                eng.wait_ge(sems[s], v)
            inst = fn(eng)
            if tag and inst is not None:
                inst.annotate(tag)
            if inc is not None:
                assert inst is not None, f"step with inc={inc} returned None"
                inst.then_inc(sems[inc], inc_n)


def build_program(S_CORE):
    S_TOT = N_CORES * S_CORE
    NTT = S_CORE // 128
    NKT = S_TOT // 128
    NGRP = NKT // 4
    NU = NQH * NTT
    NCH = 4
    GPC = NGRP // NCH
    assert NGRP % NCH == 0 and NGRP >= 4
    NOC = INTER // 128
    NT5 = S_CORE // 512
    NQO = HID // 512
    NKO = (NKV * HD) // 512
    assert S_CORE % 512 == 0

    nc = bass.Bass("TRN2", target_bir_lowering=False, debug=False)
    ap = lambda t: t.ap()
    x_in = ap(nc.dram_tensor("x", [S_CORE, HID], F32, kind="ExternalInput"))
    wq_in = ap(nc.dram_tensor("wq", [HID, HID], F32, kind="ExternalInput"))
    wk_in = ap(nc.dram_tensor("wk", [NKV * HD, HID], F32, kind="ExternalInput"))
    wv_in = ap(nc.dram_tensor("wv", [NKV * HD, HID], F32, kind="ExternalInput"))
    wo_in = ap(nc.dram_tensor("wo", [HID, HID], F32, kind="ExternalInput"))
    gate_in = ap(nc.dram_tensor("gate_w", [HID, INTER], F32, kind="ExternalInput"))
    up_in = ap(nc.dram_tensor("up_w", [HID, INTER], F32, kind="ExternalInput"))
    down_in = ap(nc.dram_tensor("down_w", [INTER, HID], F32, kind="ExternalInput"))
    ident_f = ap(nc.dram_tensor("ident_f", [128, 128], F32, kind="ExternalInput"))
    ident_b = ap(nc.dram_tensor("ident_b", [128, 128], BF16, kind="ExternalInput"))
    ones_r = ap(nc.dram_tensor("ones_r", [1, 128], F32, kind="ExternalInput"))
    qg_in = ap(nc.dram_tensor("qgain", [128, 128], F32, kind="ExternalInput"))
    kg_in = ap(nc.dram_tensor("kgain", [128, 128], F32, kind="ExternalInput"))
    gs_in = ap(nc.dram_tensor("gscale", [128, NOC], F32, kind="ExternalInput"))
    us_in = ap(nc.dram_tensor("uscale", [128, NOC], F32, kind="ExternalInput"))
    ds_in = ap(nc.dram_tensor("dscale", [128, HID], F32, kind="ExternalInput"))
    mask_in = ap(nc.dram_tensor("mask", [128, NTT * 1024], F8, kind="ExternalInput"))
    eps_in = ap(nc.dram_tensor("epsv", [128, 2], F32, kind="ExternalInput"))
    out_d = ap(nc.dram_tensor("out", [S_CORE, HID], F32, kind="ExternalOutput"))
    qsc = ap(nc.dram_tensor("qscratch", [NQH * HD, S_CORE], BF16))
    AGC = max(S_CORE, NKV * HD)
    AGR = NKV * HD + S_CORE
    ag_i = ap(nc.dram_tensor("ag_in", [AGR, AGC], BF16))
    ag_o = ap(nc.dram_tensor("ag_out", [N_CORES * AGR, AGC], BF16))
    uTd = ap(nc.dram_tensor("uTd", [NQH * HD, S_CORE], BF16))

    S = Sched()
    es = contextlib.ExitStack()
    sbE = lambda name, shape, dt: es.enter_context(nc.sbuf_tensor(name, shape, dt))

    with es:
        idf = sbE("idf", [128, 128], F32)
        idb = sbE("idb", [128, 128], BF16)
        onr = sbE("onr", [1, 128], F32)
        qg = sbE("qg", [128, 128], F32)
        kg = sbE("kg", [128, 128], F32)
        gs = sbE("gs", [128, NOC], F32)
        us = sbE("us", [128, NOC], F32)
        mask_sb = sbE("mask_sb", [128, NTT * 1024], F8)
        epsv = sbE("epsv_sb", [128, 2], F32)
        stat = sbE("stat", [128, 96], F32)
        mred = sbE("mred", [1, 2], F32)
        negm = sbE("negm", [128, 2], F32)
        recs = sbE("recs", [128, 4], F32)
        uhat = sbE("uhat", [128, 2 * 256], BF16)
        mpos = sbE("mpos", [128, 2], F32)
        asub = sbE("asub", [128, 2], F32)
        alpha = sbE("alpha", [128, 2], F32)
        maxv = sbE("maxv", [128, 2 * 16], F32)
        ffn_sb = sbE("ffn_sb", [128, NTT * HID], BF16)
        quad = es.enter_context(nc.psum_tensor("quad", [128, 2048], F32))
        pair = es.enter_context(nc.psum_tensor("pair", [128, 1024], F32))
        b6 = es.enter_context(nc.psum_tensor("b6", [128, 512], F32))
        b7 = es.enter_context(nc.psum_tensor("b7", [128, 512], F32))

        b6b = b6[:].bitcast(BF16)
        b7b = b7[:].bitcast(BF16)
        pairb = pair[:].bitcast(BF16)

        def bankq(i):
            return quad[:, i * 512:(i + 1) * 512]

        for dst, src in [(idf[:], ident_f), (idb[:], ident_b), (onr[:], ones_r),
                         (qg[:], qg_in), (kg[:], kg_in), (gs[:], gs_in),
                         (us[:], us_in), (mask_sb[:], mask_in), (epsv[:], eps_in)]:
            S.add("sync", lambda e, d=dst, s=src: e.dma_start(out=d, in_=s),
                  inc="c_ld", inc_n=16)

        es12 = contextlib.ExitStack()
        sb12 = lambda name, shape, dt: es12.enter_context(nc.sbuf_tensor(name, shape, dt))
        x_db = sb12("x_db", [128, 2 * HID], F32)
        xn_tm = sb12("xn_tm", [128, HID], BF16)
        xn_fm = sb12("xn_fm", [128, 16 * S_CORE], BF16)
        wnat = sb12("wnat", [128, 2 * HID], BF16)
        wT = sb12("wT", [128, 16 * 512], BF16)
        qraw = sb12("qraw", [128, 2 * 512], F32)
        g8 = sb12("g8", [128, 2 * HID], F8)
        u8 = sb12("u8", [128, 2 * HID], F8)
        d8 = sb12("d8", [128, 4 * 512], F8)
        silu2 = sb12("silu2", [128, 2 * 512], BF16)
        fuse = sb12("fuse", [128, NOC * 512], BF16)
        ds_bc = sb12("ds_bc", [128, HID], F32)
        zbuf = sb12("zbuf", [128, 512], BF16)
        stg4 = sb12("stg4", [128, 4 * 512], BF16)

        S.add("sync", lambda e: e.dma_start(out=ds_bc[:], in_=ds_in), inc="c_ld", inc_n=16)
        CLD = S.val("c_ld")

        AGC = max(S_CORE, NKV * HD)
        if AGC > S_CORE:
            S.add("vector", lambda e: e.memset(zbuf[:], 0.0), inc="p_zm")
            for rb in range(NKV * HD // 128):
                S.add("sync",
                      lambda e, rb=rb: e.dma_start(
                          out=ag_i[rb * 128:(rb + 1) * 128, S_CORE:AGC], in_=zbuf[:, 0:AGC - S_CORE]),
                      waits=[("p_zm", 1)], inc="p_zp", inc_n=16)
            ZP_W = [("p_zp", S.val("p_zp"))]
        else:
            ZP_W = []

        xr = x_in.rearrange("(a p) n -> a p n", p=128)
        xn_c = xn_fm[:, :].rearrange("p (c t) -> p c t", c=16)

        for tt in range(NTT):
            par = tt % 2
            sl = lambda k, w=1, par=par: stat[:, par * 48 + k: par * 48 + k + w]
            xs = x_db[:, par * HID:(par + 1) * HID]
            S.add("sync", lambda e, xs=xs, tt=tt: e.dma_start(out=xs, in_=xr[tt]),
                  waits=[("c_ld", CLD), ("p_ln", tt - 1)], inc=f"p_x{par}", inc_n=16)
            S.add("scalar",
                  lambda e, xs=xs, d=sl(0): e.activation(out=xn_tm[:], in_=xs,
                                                         func=AF.Identity, accum_out=d),
                  waits=[(f"p_x{par}", 16 * (tt // 2 + 1)), ("p_tT", tt * 4)])
            S.add("scalar",
                  lambda e, xs=xs, d=sl(1): e.activation(out=fuse[:, 0:HID].bitcast(BF16) if False else fuse[:, 0:HID], in_=xs,
                                                         func=AF.Square, accum_out=d))
            S.add("scalar", lambda e: e.drain(), inc="p_st")
            PST = S.val("p_st")

            def ln_stats(e, sl=sl):
                e.tensor_scalar(out=sl(2), in0=sl(0), scalar1=1.0 / HID, scalar2=0.0,
                                op0=ALU.mult, op1=ALU.add)
                e.drain()
                e.tensor_tensor(out=sl(4), in0=sl(2), in1=sl(2), op=ALU.mult)
                e.drain()
                return e.scalar_tensor_tensor(out=sl(5), in0=sl(1), scalar=1.0 / HID,
                                              in1=sl(4), op0=ALU.mult, op1=ALU.subtract)
            S.add("vector", ln_stats, waits=[("p_st", PST)], inc="p_ve")
            S.add("scalar", lambda e, sl=sl: e.activation(out=sl(6), in_=sl(5), func=AF.Sqrt,
                                                          bias=epsv[:, 0:1]),
                  waits=[("p_ve", S.val("p_ve"))])
            S.add("scalar", lambda e: e.drain(), inc="p_sq")
            PSQ = S.val("p_sq")

            def ln_apply(e, xs=xs, sl=sl):
                e.reciprocal(sl(7), sl(6))
                e.drain()
                return e.tensor_scalar(out=xn_tm[:], in0=xs, scalar1=sl(2), scalar2=sl(7),
                                       op0=ALU.subtract, op1=ALU.mult)
            S.add("vector", ln_apply, waits=[("p_sq", PSQ), ("p_tT", tt * 4)], inc="p_ln")
            PLN = S.val("p_ln")
            for qtr in range(4):
                for j in range(4):
                    c = qtr * 4 + j
                    w = []
                    if j == 0:
                        w = [("p_xnT", tt * 4 + qtr)]
                        if qtr == 0:
                            w.append(("p_ln", PLN))
                    S.add("tensor",
                          lambda e, c=c, j=j: e.transpose(b6b[:, j * 128:(j + 1) * 128],
                                                          xn_tm[:, c * 128:(c + 1) * 128],
                                                          idb[:]),
                          waits=w, inc=("p_tT" if j == 3 else None))
                S.add("vector",
                      lambda e, tt=tt, qtr=qtr: e.tensor_copy(
                          xn_c[:, qtr * 4:(qtr + 1) * 4, tt * 128:(tt + 1) * 128],
                          b6b[:, 0:512].rearrange("p (c t) -> p c t", c=4)),
                      waits=[("p_tT", S.val("p_tT"))], inc="p_xnT")
        XND = S.val("p_xnT")

        o_specs = ([("q", i) for i in range(NQO)] + [("k", i) for i in range(NKO)]
                   + [("v", i) for i in range(NKO)])
        w_dram = {"q": wq_in, "k": wk_in, "v": wv_in}
        n_stg = 0
        n_qk = 0

        pend_B = []

        for oi, (kind, osl) in enumerate(o_specs):
            wd = w_dram[kind].rearrange("(a p) n -> a p n", p=128)
            for osub in range(4):
                kpc = oi * 4 + osub
                pw = kpc % 2
                dst = wnat[:, pw * HID:(pw + 1) * HID]
                S.add("gpsimd", lambda e, d=dst, s=wd[osl * 4 + osub]: e.dma_start(out=d, in_=s),
                      waits=[("p_wT", (kpc - 1) * 16), ("c_ld", CLD)],
                      inc=f"p_wn{pw}", inc_n=16)
                PWN = S.val(f"p_wn{pw}")
                for ic in range(16):
                    trn = kpc * 16 + ic
                    off = (trn % 2) * 1024 + ((trn // 2) % 4) * 128
                    S.add("tensor",
                          lambda e, d=dst, ic=ic, off=off: e.transpose(
                              pairb[:, off:off + 128],
                              d[:, ic * 128:(ic + 1) * 128], idb[:]),
                          waits=([(f"p_wn{pw}", PWN)] if ic == 0 else []) +
                                [("p_wTe", trn - 1)],
                          inc="p_wT")
                    S.add("vector",
                          lambda e, ic=ic, osub=osub, off=off: e.tensor_copy(
                              wT[:, ic * 512 + osub * 128: ic * 512 + (osub + 1) * 128],
                              pairb[:, off:off + 128]),
                          waits=[("p_wT", S.val("p_wT"))], inc="p_wTe")
            WTE = S.val("p_wTe")

            for tt in range(NTT):
                bank = tt % 4
                for ic in range(16):
                    w = []
                    if ic == 0:
                        w = [("p_xnT", XND), ("p_wTe", WTE),
                             ("p_pev", S.val("p_pev") - 3)]
                    S.add("tensor",
                          lambda e, bank=bank, ic=ic, tt=tt: e.matmul(
                              bankq(bank), xn_c[:, ic, tt * 128:(tt + 1) * 128],
                              wT[:, ic * 512:(ic + 1) * 512],
                              start=(ic == 0), stop=(ic == 15)),
                          waits=w, inc=("p_mmP" if ic == 15 else None))
                while pend_B:
                    pend_B.pop(0)()
                MMP = S.val("p_mmP")
                slot = n_stg % 4
                nuse = n_stg // 4
                n_stg += 1
                stg_s = stg4[:, slot * 512:(slot + 1) * 512]
                if kind == "v":
                    S.add("vector",
                          lambda e, bank=bank, d=stg_s: e.tensor_copy(d, bankq(bank)),
                          waits=[("p_mmP", MMP), (f"p_stg{slot}", 16 * nuse)],
                          inc="p_pev")
                    S.add("scalar",
                          lambda e, s=stg_s, tt=tt, osl=osl: e.dma_start(
                              out=ag_i[NKV * HD + tt * 128: NKV * HD + (tt + 1) * 128,
                                       osl * 512:(osl + 1) * 512], in_=s),
                          waits=[("p_pev", S.val("p_pev"))], inc=f"p_stg{slot}", inc_n=16)
                else:
                    n_qk += 1
                    parq = tt % 2
                    slp = lambda k, w=1, parq=parq: stat[:, parq * 48 + 16 + k: parq * 48 + 16 + k + w]
                    qr = qraw[:, parq * 512:(parq + 1) * 512]
                    S.add("vector",
                          lambda e, bank=bank, qr=qr: e.tensor_copy(qr, bankq(bank)),
                          waits=[("p_mmP", MMP), ("p_rms", n_qk - 2)],
                          inc="p_pev")
                    PEV = S.val("p_pev")
                    for h4 in range(4):
                        S.add("scalar",
                              lambda e, qr=qr, h4=h4, d=slp(h4): e.activation(
                                  out=xn_tm[:, h4 * 128:(h4 + 1) * 128],
                                  in_=qr[:, h4 * 128:(h4 + 1) * 128],
                                  func=AF.Square, accum_out=d),
                              waits=([("p_pev", PEV), ("p_tT", NTT * 4)] if h4 == 0 else []))
                    S.add("scalar", lambda e: e.drain(), inc="p_ss")
                    S.add("vector",
                          lambda e, slp=slp: e.tensor_scalar(
                              out=slp(4, 4), in0=slp(0, 4), scalar1=1.0 / HD,
                              scalar2=0.0, op0=ALU.mult, op1=ALU.add),
                          waits=[("p_ss", S.val("p_ss"))], inc="p_rv")
                    S.add("scalar",
                          lambda e, slp=slp: e.activation(out=slp(8, 4), in_=slp(4, 4),
                                                          func=AF.Sqrt, bias=epsv[:, 1:2]),
                          waits=[("p_rv", S.val("p_rv"))])
                    S.add("scalar", lambda e: e.drain(), inc="p_sq2")

                    def rms_nr(e, slp=slp):
                        e.reciprocal(slp(28, 4), slp(8, 4))
                        return e.drain()
                    S.add("vector", rms_nr, waits=[("p_sq2", S.val("p_sq2"))], inc="p_nr")
                    gt = qg if kind == "q" else kg
                    for h4 in range(4):
                        w = [("p_qT", S.val("p_qT"))] if h4 == 0 else []
                        S.add("vector",
                              lambda e, qr=qr, h4=h4, gt=gt, slp=slp: e.scalar_tensor_tensor(
                                  out=silu2[:, h4 * 128:(h4 + 1) * 128],
                                  in0=qr[:, h4 * 128:(h4 + 1) * 128],
                                  scalar=slp(28 + h4), in1=gt[:],
                                  op0=ALU.mult, op1=ALU.mult),
                              waits=w, inc=("p_rms" if h4 == 3 else None))
                    PRMS = S.val("p_rms")
                    if kind == "q":
                        dstq = qsc.rearrange("(h p) t -> p h t", p=128)[
                            :, osl * 4:osl * 4 + 4, tt * 128:(tt + 1) * 128]
                    else:
                        dstq = ag_i[0:NKV * HD, 0:S_CORE].rearrange("(h p) t -> p h t", p=128)[
                            :, osl * 4:osl * 4 + 4, tt * 128:(tt + 1) * 128]

                    def B(PRMS=PRMS, slot=slot, nuse=nuse, stg_s=stg_s, dstq=dstq):
                        for h4 in range(4):
                            S.add("tensor",
                                  lambda e, h4=h4: e.transpose(
                                      b6b[:, h4 * 128:(h4 + 1) * 128],
                                      silu2[:, h4 * 128:(h4 + 1) * 128], idb[:]),
                                  waits=([("p_rms", PRMS), ("p_qTe", S.val("p_qTe"))]
                                         if h4 == 0 else []),
                                  inc=("p_qT" if h4 == 3 else None))
                        S.add("vector",
                              lambda e, d=stg_s: e.tensor_copy(d, b6b[:, 0:512]),
                              waits=[("p_qT", S.val("p_qT")), (f"p_stg{slot}", 16 * nuse)],
                              inc="p_qTe")
                        S.add("scalar",
                              lambda e, s=stg_s, d=dstq: e.dma_start(
                                  out=d, in_=s.rearrange("p (h t) -> p h t", h=4)),
                              waits=[("p_qTe", S.val("p_qTe"))], inc=f"p_stg{slot}",
                              inc_n=16)
                    pend_B.append(B)
        while pend_B:
            pend_B.pop(0)()
        STG_W = [(f"p_stg{i}", S.val(f"p_stg{i}")) for i in range(4)]
        PEV_ALL = S.val("p_pev")
        QTE_ALL = S.val("p_qTe")

        S.add("gpsimd",
              lambda e: e.collective_compute("AllGather", ALU.bypass,
                                             replica_groups=[list(range(N_CORES))],
                                             ins=[ag_i.opt()], outs=[ag_o.opt()]),
              waits=STG_W + ZP_W, inc="p_ag")

        gcol = gate_in.rearrange("(c p) n -> p c n", p=128)
        ucol = up_in.rearrange("(c p) n -> p c n", p=128)
        dcol = down_in.rearrange("(c p) n -> p c n", p=128)
        NOCP = NOC // 2
        ALLB = [quad[:, i * 512:(i + 1) * 512] for i in range(4)] + \
               [pair[:, 0:512], pair[:, 512:1024], b6[:], b7[:]]

        for oph in range(2):
            for ocr in range(NOCP):
                oc = oph * NOCP + ocr
                j = oc
                pw = j % 2
                gds = g8[:, pw * HID:(pw + 1) * HID]
                uds = u8[:, pw * HID:(pw + 1) * HID]
                S.add("gpsimd",
                      lambda e, d=gds, oc=oc: e.dma_start(
                          out=d.rearrange("p (c n) -> p c n", c=16),
                          in_=gcol[:, :, oc * 128:(oc + 1) * 128]),
                      waits=[("p_gmm", (j - 2) * NT5 + NT5)], inc=f"p_g8{pw}", inc_n=16)
                S.add("gpsimd",
                      lambda e, d=uds, oc=oc: e.dma_start(
                          out=d.rearrange("p (c n) -> p c n", c=16),
                          in_=ucol[:, :, oc * 128:(oc + 1) * 128]),
                      waits=[("p_umm", (j - 2) * NT5 + NT5)], inc=f"p_u8{pw}", inc_n=16)
                PG8 = S.val(f"p_g8{pw}")
                PU8 = S.val(f"p_u8{pw}")
                for t5 in range(NT5):
                    j2 = j * NT5 + t5
                    pw2 = j2 % 2
                    gp = pair[:, 0:512] if pw2 == 0 else b6[:]
                    upp = pair[:, 512:1024] if pw2 == 0 else b7[:]
                    for ic in range(16):
                        w = []
                        if ic == 0:
                            w = [(f"p_g8{pw}", PG8), ("p_xnT", XND), ("p_si", j2 - 1)]
                            if j2 <= 1:
                                w += [("p_pev", PEV_ALL), ("p_qTe", QTE_ALL)]
                            if oph == 1 and ocr == 0 and t5 <= 1:
                                w += [("p_dne", oph * 16 * NT5)]
                        S.add("tensor",
                              lambda e, gp=gp, gds=gds, ic=ic, t5=t5: e.matmul(
                                  gp, gds[:, ic * 128:(ic + 1) * 128],
                                  xn_c[:, ic, t5 * 512:(t5 + 1) * 512],
                                  start=(ic == 0), stop=(ic == 15)),
                              waits=w, inc=("p_gmm" if ic == 15 else None))
                    for ic in range(16):
                        w = []
                        if ic == 0:
                            w = [("p_fu", j2 - 1), (f"p_u8{pw}", PU8)]
                            if oph == 1 and ocr == 0 and t5 <= 1:
                                w += [("p_dne", oph * 16 * NT5)]
                        S.add("tensor",
                              lambda e, upp=upp, uds=uds, ic=ic, t5=t5: e.matmul(
                                  upp, uds[:, ic * 128:(ic + 1) * 128],
                                  xn_c[:, ic, t5 * 512:(t5 + 1) * 512],
                                  start=(ic == 0), stop=(ic == 15)),
                              waits=w, inc=("p_umm" if ic == 15 else None))
                    ps = pw2
                    S.add("scalar",
                          lambda e, gp=gp, oc=oc, ps=ps: e.activation(
                              out=silu2[:, ps * 512:(ps + 1) * 512], in_=gp, func=AF.Silu,
                              scale=gs[:, oc:oc + 1]),
                          waits=[("p_gmm", S.val("p_gmm")), ("p_fu", j2 - 1)], inc="p_si")
                    fsl = (ocr * NT5 + t5)
                    S.add("vector",
                          lambda e, upp=upp, oc=oc, ps=ps, fsl=fsl: e.scalar_tensor_tensor(
                              out=fuse[:, fsl * 512:(fsl + 1) * 512], in0=upp,
                              scalar=us[:, oc:oc + 1], in1=silu2[:, ps * 512:(ps + 1) * 512],
                              op0=ALU.mult, op1=ALU.mult),
                          waits=[("p_umm", S.val("p_umm")), ("p_si", S.val("p_si"))] +
                                ([("p_dnm", oph * 4 * NOCP if ocr == 0 and t5 == 0 else 0)]
                                 if oph >= 1 else []),
                          inc="p_fu")
            FUD = S.val("p_fu")
            for hs in range(4):
                for ocr in range(NOCP):
                    oc = oph * NOCP + ocr
                    jd = (oph * 4 + hs) * NOCP + ocr
                    sd = jd % 4
                    dds = d8[:, sd * 512:(sd + 1) * 512]
                    S.add("gpsimd",
                          lambda e, d=dds, oc=oc, hs=hs: e.dma_start(
                              out=d, in_=dcol[:, oc, hs * 512:(hs + 1) * 512]),
                          waits=[("p_dnm", jd - 3), ("p_fu", FUD if ocr == 0 else 0)],
                          inc=f"p_d8{sd}", inc_n=16)
                    PD8 = S.val(f"p_d8{sd}")
                    for t5 in range(NT5):
                        for ts in range(4):
                            bk = t5 * 4 + ts
                            fsl = (ocr * NT5 + t5)
                            w = []
                            if ts == 0 and t5 == 0:
                                w = [(f"p_d8{sd}", PD8), ("p_fu", FUD)]
                                if ocr == 0:
                                    w.append(("p_dne", (oph * 4 + hs) * 4 * NT5))
                            S.add("tensor",
                                  lambda e, bk=bk, fsl=fsl, ts=ts, dds=dds, ocr=ocr: e.matmul(
                                      ALLB[bk], fuse[:, fsl * 512 + ts * 128: fsl * 512 + (ts + 1) * 128],
                                      dds, start=(ocr == 0), stop=(ocr == NOCP - 1)),
                                  waits=w, inc=("p_dnm" if (ts == 3 and t5 == NT5 - 1) else None),
                                  tag=f"dn{oph}.{hs}.{ocr}.{t5}.{ts}")
                DNM = S.val("p_dnm")
                for t5 in range(NT5):
                    for ts in range(4):
                        bk = t5 * 4 + ts
                        tt = t5 * 4 + ts
                        if oph == 0:
                            S.add("vector",
                                  lambda e, bk=bk, tt=tt, hs=hs: e.tensor_tensor(
                                      out=ffn_sb[:, tt * HID + hs * 512: tt * HID + (hs + 1) * 512],
                                      in0=ALLB[bk], in1=ds_bc[:, hs * 512:(hs + 1) * 512],
                                      op=ALU.mult),
                                  waits=[("p_dnm", DNM)], inc="p_dne")
                        else:
                            def ev2(e, bk=bk, tt=tt, hs=hs):
                                qtmp = qraw[:, (bk % 2) * 512:(bk % 2) * 512 + 512]
                                e.tensor_tensor(out=qtmp,
                                                in0=ALLB[bk], in1=ds_bc[:, hs * 512:(hs + 1) * 512],
                                                op=ALU.mult)
                                e.drain()
                                return e.tensor_tensor(
                                    out=ffn_sb[:, tt * HID + hs * 512: tt * HID + (hs + 1) * 512],
                                    in0=qtmp,
                                    in1=ffn_sb[:, tt * HID + hs * 512: tt * HID + (hs + 1) * 512],
                                    op=ALU.add)
                            S.add("vector", ev2, waits=[("p_dnm", DNM)], inc="p_dne")
        GMM_ALL = S.val("p_gmm")
        UMM_ALL = S.val("p_umm")
        DNE_ALL = S.val("p_dne")
        DNM_ALL = S.val("p_dnm")
        FU_ALL = S.val("p_fu")
        SI_ALL = S.val("p_si")

        es12.close()
        es3 = contextlib.ExitStack()
        sb3 = lambda name, shape, dt: es3.enter_context(nc.sbuf_tensor(name, shape, dt))
        sc_sb = sb3("sc_sb", [128, 2 * (NKT // 4) * 256], F32)
        p_sb = sb3("p_sb", [128, 2 * (NKT // 4) * 256], BF16)
        usb = sb3("usb", [128, 2 * 512], F32)
        EVSEM = {}
        kbuf = sb3("kbuf", [128, 2 * NKT * 128], BF16)
        vbuf = sb3("vbuf", [128, 2 * NKT * 129], BF16)
        qbuf = sb3("qbuf", [128, 2 * S_CORE], BF16)
        ustg = sb3("ustg", [128, 2 * 128], BF16)

        agv = ag_o.rearrange("(r f p) t -> r f p t", r=N_CORES, p=128)
        vview = vbuf[:, :].rearrange("p (b k) -> p b k", k=129)
        NQT = NTT // 2
        NSTG = 4
        NKS = NKT // NSTG
        NGE = NKS // 2
        NST = NQH * NQT * NSTG
        NCH2 = 2
        GPC2 = NGE // NCH2
        DNM_ALL = S.val("p_dnm")
        DNE_ALL = S.val("p_dne")
        FU_ALL = S.val("p_fu")
        GMM_ALL = S.val("p_gmm")
        UMM_ALL = S.val("p_umm")

        def grp_eng(st_idx, ge):
            u, st = divmod(st_idx, NSTG)
            masked = (st == NSTG - 1) and (ge >= NGE - 4)
            if masked:
                return "vector_mask"
            return "scalar" if ge in (NGE // 2, NGE // 2 + 1) else "vector"

        S.add("vector", lambda e: e.memset(vview[:, :, 128:129], 1.0),
              waits=[("p_dne", DNE_ALL)], inc="p_on")

        def load_kv(h):
            pk = h % 2
            kd = kbuf[:, pk * NKT * 128:(pk + 1) * NKT * 128]
            w = [("p_ag", 1), ("p_dnm", DNM_ALL), ("p_on", 1)]
            if h >= 2:
                w.append(("p_sc2", 2 * (h - 1) * NTT // 2 * NSTG * NGE))
                w.append(("p_ue", NSTG * (2 * (h - 1) * (NTT // 2))))
            S.add("sync",
                  lambda e, kd=kd, h=h: e.dma_start(
                      out=kd.rearrange("p (r t) -> p r t", r=N_CORES),
                      in_=agv[:, h, :, 0:S_CORE].transpose([1, 0, 2])),
                  waits=w, inc=f"p_kv{pk}", inc_n=16)
            vdst = vbuf[:, :].rearrange("p (b2 r j k) -> p b2 r j k",
                                        b2=2, r=N_CORES, j=NTT)[:, pk, :, :, 0:128]
            for r in range(N_CORES):
                S.add("sync",
                      lambda e, h=h, vdst=vdst, r=r: e.dma_start(
                          out=vdst[:, r],
                          in_=agv[r, NKV:NKV + NTT, :, h * 128:(h + 1) * 128]
                              .transpose([1, 0, 2])),
                      inc=f"p_kv{pk}", inc_n=16)

        def load_q(qh):
            pq = qh % 2
            w = STG_W + [("p_ln", NTT), ("p_dnm", DNM_ALL)]
            if qh >= 2:
                w.append(("p_sc2", (qh - 1) * NTT // 2 * NSTG * NGE))
            S.add("sync",
                  lambda e, qh=qh, pq=pq: e.dma_start(
                      out=qbuf[:, pq * S_CORE:(pq + 1) * S_CORE],
                      in_=qsc[qh * 128:(qh + 1) * 128, :]),
                  waits=w, inc=f"p_q{pq}", inc_n=16)

        load_kv(0)
        load_kv(1)
        load_q(0)
        load_q(1)

        def emit_av_mm(st_idx):
            u, st = divmod(st_idx, NSTG)
            qh, q2 = divmod(u, NQT)
            hk = qh // 2
            pk = hk % 2
            pu = u % 2
            avb = pair[:, pu * 512:(pu + 1) * 512]
            QSO = (0, 136)
            for qs in range(2):
                for ch in range(NCH2):
                    for jj in range(GPC2 * 2):
                        kt_rel = ch * GPC2 * 2 + jj
                        kt = st * NKS + kt_rel
                        w = []
                        if jj == 0 and qs == 0:
                            w = [("p_ex", st_idx * NCH2 + ch + 1)]
                            if ch == 0:
                                if st == 0:
                                    w.append(("p_fu", FU_ALL))
                                    w.append(("p_ue", NSTG * (u - 1)))
                                else:
                                    w.append(("p_ue", NSTG * u + st))
                        spb = (st_idx % 2) * (NKT // 4) * 256
                        S.add("tensor",
                              lambda e, kt=kt, kt_rel=kt_rel, qs=qs, pk=pk, avb=avb, spb=spb: e.matmul(
                                  avb[:, QSO[qs]: QSO[qs] + 129],
                                  p_sb[:, spb + kt_rel * 256 + qs * 128: spb + kt_rel * 256 + (qs + 1) * 128],
                                  vview[:, pk * NKT + kt, :],
                                  start=(kt_rel == 0), stop=(kt_rel == NKS - 1)),
                              waits=w,
                              inc=("p_av" if (jj == GPC2 * 2 - 1 and qs == 1
                                              and ch == NCH2 - 1) else None),
                              tag=f"av{st_idx}.{qs}.{kt_rel}")

        def emit_av_vec(st_idx):
            u, st = divmod(st_idx, NSTG)
            qh, q2 = divmod(u, NQT)
            pu = u % 2
            avb = pair[:, pu * 512:(pu + 1) * 512]
            if st == 0:
                def park(e, pu=pu):
                    e.tensor_copy(usb[:, pu * 512 + 0: pu * 512 + 129], avb[:, 0:129])
                    return e.tensor_copy(usb[:, pu * 512 + 256: pu * 512 + 385], avb[:, 136:265])
                S.add("vector", park,
                      waits=[("p_av", st_idx + 1), ("p_ue", NSTG * u)], inc="p_ue")
                return
            last = (st == NSTG - 1)

            def comb(e, pu=pu, last=last, aslot=st_idx % 2):
                QSO2 = (0, 136)
                for qs in range(2):
                    e.scalar_tensor_tensor(
                        out=usb[:, pu * 512 + qs * 256: pu * 512 + qs * 256 + 129],
                        in0=usb[:, pu * 512 + qs * 256: pu * 512 + qs * 256 + 129],
                        scalar=alpha[:, aslot:aslot + 1],
                        in1=avb[:, QSO2[qs]: QSO2[qs] + 129],
                        op0=ALU.mult, op1=ALU.add)
                if not last:
                    return e.drain()
                e.drain()
                e.reciprocal(recs[:, pu * 2:pu * 2 + 1], usb[:, pu * 512 + 128: pu * 512 + 129])
                e.reciprocal(recs[:, pu * 2 + 1:pu * 2 + 2], usb[:, pu * 512 + 384: pu * 512 + 385])
                e.drain()
                e.tensor_scalar(out=uhat[:, pu * 256: pu * 256 + 128],
                                in0=usb[:, pu * 512: pu * 512 + 128],
                                scalar1=recs[:, pu * 2:pu * 2 + 1], scalar2=0.0,
                                op0=ALU.mult, op1=ALU.add)
                return e.tensor_scalar(out=uhat[:, pu * 256 + 128: pu * 256 + 256],
                                       in0=usb[:, pu * 512 + 256: pu * 512 + 384],
                                       scalar1=recs[:, pu * 2 + 1:pu * 2 + 2], scalar2=0.0,
                                       op0=ALU.mult, op1=ALU.add)
            S.add("vector", comb,
                  waits=[("p_av", st_idx + 1), ("p_al", 3 * u + st),
                         ("p_uT", 2 * (u - 1) + 2)], inc="p_ue", tag=f"comb{st_idx}")
            if not last:
                return
            for qs in range(2):
                n_u = 2 * u + qs
                up_ = n_u % 2
                ub = b6b
                S.add("tensor",
                      lambda e, ub=ub, pu=pu, qs=qs: e.transpose(
                          ub[:, 0:128], uhat[:, pu * 256 + qs * 128: pu * 256 + (qs + 1) * 128],
                          idb[:]),
                      waits=[("p_ue", S.val("p_ue")), ("p_uf", 2 * u + qs)],
                      inc="p_uT")
                S.add("vector",
                      lambda e, ub=ub, up_=up_: e.tensor_copy(
                          ustg[:, up_ * 128:(up_ + 1) * 128], ub[:, 0:128]),
                      waits=[("p_uT", S.val("p_uT")),
                             (f"p_us{up_}", 16 * (n_u // 2))], inc="p_uf")
                dst = uTd.rearrange("(c p) t -> p c t", p=128)[
                    :, qh, q2 * 256 + qs * 128: q2 * 256 + (qs + 1) * 128]
                S.add("scalar",
                      lambda e, dst=dst, up_=up_: e.dma_start(
                          out=dst, in_=ustg[:, up_ * 128:(up_ + 1) * 128]),
                      waits=[("p_uf", S.val("p_uf"))], inc=f"p_us{up_}", inc_n=16)

        def emit_mred(sti):
            mslot = sti % 2
            S.add("vector",
                  lambda e, mslot=mslot: e.tensor_reduce(
                      mred[:, mslot:mslot + 1], b7[0:1, mslot * 256: mslot * 256 + 128],
                      AXX, ALU.max),
                  waits=[("p_mxT", sti + 1), ("p_bc", sti - 1)], inc="p_mv2")

        def emit_bc(sti):
            mslot = sti % 2
            S.add("tensor",
                  lambda e, mslot=mslot: e.matmul(
                      b7[:, mslot * 256 + 128: mslot * 256 + 129], onr[:],
                      mred[:, mslot:mslot + 1], start=True, stop=True),
                  waits=[("p_mv2", sti + 1), ("p_ng", sti - 1)], inc="p_bc")

        def emit_mxT(sti):
            mslot = sti % 2
            S.add("tensor",
                  lambda e, mslot=mslot: e.transpose(
                      b7[0:1, mslot * 256: mslot * 256 + 128],
                      stat[:, 90 + mslot:91 + mslot], idf[:]),
                  waits=[("p_mv1", sti + 1), ("p_mv2", sti - 1)], inc="p_mxT")

        def emit_ngf(sti):
            u, st = divmod(sti, NSTG)
            mslot = sti % 2
            pu = u % 2
            ngw = [("p_bc", sti + 1)]
            if sti >= 2:
                ngw.append(("p_ex", (sti - 1) * NCH2))
            if st >= 1:
                prev = sti - 2 if (sti - 2) % NSTG != 0 else sti - 4
                if prev >= 1:
                    pa = 3 * (prev // NSTG) + (prev % NSTG)
                    if pa > 0:
                        ngw.append(("p_al", pa))

            def ngf(e, mslot=mslot, st=st, pu=pu):
                bcv = b7[:, mslot * 256 + 128: mslot * 256 + 129]
                if st == 0:
                    e.tensor_scalar(out=negm[:, mslot:mslot + 1], in0=bcv,
                                    scalar1=-1.0, scalar2=0.0, op0=ALU.mult, op1=ALU.add)
                    return e.tensor_copy(mpos[:, pu:pu + 1], bcv)
                e.tensor_tensor(out=stat[:, 94:95], in0=bcv,
                                in1=mpos[:, pu:pu + 1], op=ALU.max)
                e.drain()
                e.tensor_scalar(out=negm[:, mslot:mslot + 1], in0=stat[:, 94:95],
                                scalar1=-1.0, scalar2=0.0, op0=ALU.mult, op1=ALU.add)
                e.tensor_tensor(out=asub[:, mslot:mslot + 1], in0=mpos[:, pu:pu + 1],
                                in1=stat[:, 94:95], op=ALU.subtract)
                e.drain()
                return e.tensor_copy(mpos[:, pu:pu + 1], stat[:, 94:95])
            S.add("vector", ngf, waits=ngw, inc="p_ng")

        def emit_alpha_exp(sti):
            u, st = divmod(sti, NSTG)
            mslot = sti % 2
            NG = sti + 1
            if st >= 1:
                aw = [("p_ng", NG), ("p_ue", sti - 1)]
                S.add("scalar",
                      lambda e, mslot=mslot: e.activation(out=alpha[:, mslot:mslot + 1],
                                                          in_=asub[:, mslot:mslot + 1],
                                                          func=AF.Exp),
                      waits=aw, inc="p_al")
            for ch in range(NCH2):
                w = [("p_ng", NG)]
                best = {}
                for gg in range(ch * GPC2, (ch + 1) * GPC2):
                    nm, vv = EVSEM[(sti, gg)]
                    best[nm] = max(best.get(nm, 0), vv)
                w += list(best.items())
                if sti >= 2:
                    w.append(("p_av", sti - 1))
                spb = (sti % 2) * (NKT // 4) * 256
                ssc2 = (sti % 2) * (NKT // 4) * 256
                S.add("scalar",
                      lambda e, ch=ch, mslot=mslot, spb=spb, ssc2=ssc2: e.activation(
                          out=p_sb[:, spb + ch * GPC2 * 512: spb + (ch + 1) * GPC2 * 512],
                          in_=sc_sb[:, ssc2 + ch * GPC2 * 512: ssc2 + (ch + 1) * GPC2 * 512],
                          func=AF.Exp, bias=negm[:, mslot:mslot + 1], scale=1.0),
                      waits=w, inc="p_ex")

        for it in range(NST + 2):
            if 1 <= it <= NST:
                emit_mred(it - 1)
            if it < NST:
                sti = it
                u, st = divmod(sti, NSTG)
                qh, q2 = divmod(u, NQT)
                hk = qh // 2
                pk = hk % 2
                pq = qh % 2
                mslot = sti % 2
                if st == 0 and q2 == 0:
                    if qh + 2 < NQH:
                        load_q(qh + 2)
                    if qh % 2 == 0 and hk + 2 < NKV:
                        load_kv(hk + 2)
                nvec = 0
                for ge in range(NGE):
                    for j in range(2):
                        kt_rel = ge * 2 + j
                        kt = st * NKS + kt_rel
                        slot = kt_rel % 8
                        w = []
                        if j == 0:
                            if ge >= 4 or sti > 0:
                                psti, pg = (sti, ge - 4) if ge >= 4 else (sti - 1, NGE + ge - 4)
                                w.append(EVSEM[(psti, pg)])
                            w += [(f"p_kv{pk}", 16 * (1 + N_CORES) * (hk // 2 + 1)),
                                  (f"p_q{pq}", 16 * (qh // 2 + 1))]
                            if sti == 0 and ge == 0:
                                w.append(("p_dne", DNE_ALL))
                        S.add("tensor",
                              lambda e, slot=slot, kt=kt, pk=pk, pq=pq, q2=q2: e.matmul(
                                  quad[:, slot * 256:(slot + 1) * 256],
                                  kbuf[:, pk * NKT * 128 + kt * 128: pk * NKT * 128 + (kt + 1) * 128],
                                  qbuf[:, pq * S_CORE + q2 * 256: pq * S_CORE + (q2 + 1) * 256],
                                  start=True, stop=True),
                              waits=w, inc=("p_sc2" if j == 1 else None))
                    SC2 = S.val("p_sc2")
                    eng = grp_eng(sti, ge)
                    bank = (2 * ge % 8) // 2
                    ssc = (sti % 2) * (NKT // 4) * 256
                    ew = [("p_sc2", SC2), ("p_ex", (sti - 2) * NCH2 + ge // GPC2 + 1)]
                    if sti == 0 and ge == 0:
                        ew += [("p_gmm", GMM_ALL), ("p_umm", UMM_ALL)]
                    if eng == "vector":
                        S.add("vector",
                              lambda e, bank=bank, ge=ge, mslot=mslot, nv=nvec, ssc=ssc: e.tensor_scalar(
                                  out=sc_sb[:, ssc + ge * 512: ssc + (ge + 1) * 512],
                                  in0=quad[:, bank * 512:(bank + 1) * 512],
                                  scalar1=float(C_SCALE), scalar2=NEG_BIG,
                                  op0=ALU.mult, op1=ALU.max,
                                  accum_out=maxv[:, mslot * 16 + nv: mslot * 16 + nv + 1]),
                              waits=ew, inc="p_evv")
                        EVSEM[(sti, ge)] = ("p_evv", S.val("p_evv"))
                        nvec += 1
                    elif eng == "vector_mask":
                        mg = ge - (NGE - 4)
                        S.add("vector",
                              lambda e, bank=bank, ge=ge, mg=mg, q2=q2, ssc=ssc: e.scalar_tensor_tensor(
                                  out=sc_sb[:, ssc + ge * 512: ssc + (ge + 1) * 512],
                                  in0=quad[:, bank * 512:(bank + 1) * 512],
                                  scalar=float(C_SCALE),
                                  in1=mask_sb[:, q2 * 2048 + mg * 512: q2 * 2048 + (mg + 1) * 512],
                                  op0=ALU.mult, op1=ALU.add),
                              waits=ew, inc="p_evv")
                        EVSEM[(sti, ge)] = ("p_evv", S.val("p_evv"))
                    else:
                        S.add("scalar",
                              lambda e, bank=bank, ge=ge, ssc=ssc: e.mul(
                                  sc_sb[:, ssc + ge * 512: ssc + (ge + 1) * 512],
                                  quad[:, bank * 512:(bank + 1) * 512], float(C_SCALE)),
                              waits=ew, inc="p_evs")
                        EVSEM[(sti, ge)] = ("p_evs", S.val("p_evs"))
                    if ge == 1 and it >= 1:
                        emit_bc(it - 1)
                        emit_ngf(it - 1)
                NV = nvec
                assert NV > 0 or st > 0

                def mx_red(e, mslot=mslot, NV=NV, st=st, pu=u % 2):
                    e.drain()
                    if st == 0:
                        return e.tensor_reduce(stat[:, 90 + mslot:91 + mslot],
                                               maxv[:, mslot * 16: mslot * 16 + NV],
                                               AXX, ALU.max)
                    if NV == 0:
                        return e.tensor_scalar(out=stat[:, 90 + mslot:91 + mslot],
                                               in0=epsv[:, 0:1], scalar1=0.0,
                                               scalar2=NEG_BIG, op0=ALU.mult, op1=ALU.add)
                    return e.tensor_reduce(stat[:, 90 + mslot:91 + mslot],
                                           maxv[:, mslot * 16: mslot * 16 + NV],
                                           AXX, ALU.max)
                mrw = []
                if sti >= 2:
                    mrw.append(("p_mxT", sti - 1))
                S.add("vector", mx_red, waits=mrw, inc="p_mv1")
                if it >= 1:
                    emit_alpha_exp(it - 1)
            if it >= NST and 1 <= it <= NST:
                emit_bc(it - 1)
                emit_ngf(it - 1)
                emit_alpha_exp(it - 1)
            if 2 <= it <= NST + 1:
                emit_av_mm(it - 2)
                emit_av_vec(it - 2)
            if it < NST:
                emit_mxT(it)
        EX_ALL = S.val("p_ex")
        AVC_ALL = S.val("p_av")
        UF_ALL = S.val("p_uf")
        SCG_ALL = S.val("p_sc2")
        EV_ALL = S.val("p_evv")
        EVS_ALL = S.val("p_evs")
        UE_ALL = S.val("p_ue")

        US_W = [(f"p_us{i}", S.val(f"p_us{i}")) for i in range(2)]
        UE_ALL2 = S.val("p_ue")
        es3.close()
        es4 = contextlib.ExitStack()
        sb4 = lambda name, shape, dt: es4.enter_context(nc.sbuf_tensor(name, shape, dt))
        uT4 = sb4("uT4", [128, 2 * 2048], BF16)
        woT = sb4("woT", [128, 16 * 512], BF16)
        wonat = sb4("wonat", [128, 2 * HID], BF16)
        xrl = sb4("xrl", [128, 4 * 512], F32)
        osb = sb4("osb", [128, 4 * 512], F32)
        tmp4 = sb4("tmp4", [128, 4 * 512], F32)

        wor = wo_in.rearrange("(a p) n -> a p n", p=128)
        n_x = 0
        for hs in range(4):
            for osub in range(4):
                kpc = hs * 4 + osub
                pw = kpc % 2
                dst = wonat[:, pw * HID:(pw + 1) * HID]
                S.add("gpsimd",
                      lambda e, d=dst, kpc=kpc: e.dma_start(out=d, in_=wor[kpc]),
                      waits=[("p_ex", EX_ALL), ("p_av", AVC_ALL), ("p_ue", UE_ALL2),
                             ("p_sc2", SCG_ALL), ("p_woT", (kpc - 1) * 16)] + US_W,
                      inc=f"p_won{pw}", inc_n=16)
                PWON = S.val(f"p_won{pw}")
                for ic in range(16):
                    trn = kpc * 16 + ic
                    off = (trn % 2) * 1024 + ((trn // 2) % 4) * 128
                    S.add("tensor",
                          lambda e, d=dst, ic=ic, off=off: e.transpose(
                              pairb[:, off:off + 128],
                              d[:, ic * 128:(ic + 1) * 128], idb[:]),
                          waits=([(f"p_won{pw}", PWON), ("p_uf", UF_ALL),
                                  ("p_ue", UE_ALL), ("p_evs", EVS_ALL)] if ic == 0 else []) +
                                [("p_woTe", trn - 1)],
                          inc="p_woT")
                    S.add("vector",
                          lambda e, ic=ic, osub=osub, off=off: e.tensor_copy(
                              woT[:, ic * 512 + osub * 128: ic * 512 + (osub + 1) * 128],
                              pairb[:, off:off + 128]),
                          waits=[("p_woT", S.val("p_woT"))], inc="p_woTe")
            WOTE = S.val("p_woTe")
            for tt in range(NTT):
                bank = tt % 4
                slot = n_x % 4
                nuse = n_x // 4
                up4 = n_x % 2
                n_x += 1
                S.add("gpsimd",
                      lambda e, slot=slot, tt=tt, hs=hs: e.dma_start(
                          out=xrl[:, slot * 512:(slot + 1) * 512],
                          in_=xr[tt][:, hs * 512:(hs + 1) * 512]),
                      waits=[(f"p_out{slot}", 16 * nuse), ("p_ex", EX_ALL),
                             ("p_ue", UE_ALL2)] + US_W,
                      inc=f"p_xr{slot}", inc_n=16)
                S.add("gpsimd",
                      lambda e, up4=up4, tt=tt: e.dma_start(
                          out=uT4[:, up4 * 2048:(up4 + 1) * 2048].rearrange(
                              "p (c t) -> p c t", c=16),
                          in_=uTd.rearrange("(c p) t -> p c t", p=128)[
                              :, :, tt * 128:(tt + 1) * 128]),
                      waits=US_W + [("p_wom", S.val("p_wom") - 1)],
                      inc=f"p_ut4{up4}", inc_n=16)
                PUT4 = S.val(f"p_ut4{up4}")
                for ic in range(16):
                    w = []
                    if ic == 0:
                        w = [("p_woTe", WOTE), ("p_cmb", S.val("p_cmb") - 3),
                             ("p_evv", EV_ALL), ("p_evs", EVS_ALL), ("p_uf", UF_ALL),
                             (f"p_ut4{up4}", PUT4)]
                    S.add("tensor",
                          lambda e, bank=bank, ic=ic, up4=up4: e.matmul(
                              bankq(bank),
                              uT4[:, up4 * 2048 + ic * 128: up4 * 2048 + (ic + 1) * 128],
                              woT[:, ic * 512:(ic + 1) * 512],
                              start=(ic == 0), stop=(ic == 15)),
                          waits=w, inc=("p_wom" if ic == 15 else None))
                WOM = S.val("p_wom")

                def combine(e, bank=bank, slot=slot, tt=tt, hs=hs):
                    e.tensor_tensor(out=tmp4[:, slot * 512:(slot + 1) * 512],
                                    in0=bankq(bank), in1=xrl[:, slot * 512:(slot + 1) * 512],
                                    op=ALU.add)
                    e.drain()
                    return e.tensor_tensor(
                        out=osb[:, slot * 512:(slot + 1) * 512],
                        in0=tmp4[:, slot * 512:(slot + 1) * 512],
                        in1=ffn_sb[:, tt * HID + hs * 512: tt * HID + (hs + 1) * 512],
                        op=ALU.add)
                S.add("vector", combine,
                      waits=[("p_wom", WOM), (f"p_xr{slot}", 16 * (nuse + 1)),
                             (f"p_out{slot}", 16 * nuse)],
                      inc="p_cmb")
                S.add("sync" if (hs * NTT + tt) % 2 == 0 else "scalar",
                      lambda e, slot=slot, tt=tt, hs=hs: e.dma_start(
                          out=out_d[tt * 128:(tt + 1) * 128, hs * 512:(hs + 1) * 512],
                          in_=osb[:, slot * 512:(slot + 1) * 512]),
                      waits=[("p_cmb", S.val("p_cmb"))], inc=f"p_out{slot}", inc_n=16)
        OUT_W = [(f"p_out{i}", S.val(f"p_out{i}")) for i in range(4)]
        S.add("sync", lambda e: e.nop(), waits=OUT_W)
        es4.close()

        S.validate()
        with contextlib.ExitStack() as es_sem:
            sems = {n: es_sem.enter_context(nc.semaphore(n)) for n in S.names}
            import os
            if os.environ.get("KDBG"):
                for n, h in sems.items():
                    print("SEMMAP", getattr(h, "index", getattr(h, "id", h)), n)
            with nc.Block() as block:
                block.sync(lambda e: S.emit("sync", e, sems))
                block.scalar(lambda e: S.emit("scalar", e, sems))
                block.vector(lambda e: S.emit("vector", e, sems))
                block.tensor(lambda e: S.emit("tensor", e, sems))
                block.gpsimd(lambda e: S.emit("gpsimd", e, sems))

    meta = dict(S_CORE=S_CORE)
    return nc, meta


def build_consts(S_CORE, qnw, knw, gsc, usc, dsc):
    S_TOT = N_CORES * S_CORE
    NTT = S_CORE // 128
    NKT = S_TOT // 128
    NOC = INTER // 128
    f32 = np.float32
    bf = ml_dtypes.bfloat16
    f8 = ml_dtypes.float8_e4m3fn
    c = {
        "ident_f": np.eye(128, dtype=f32),
        "ident_b": np.eye(128).astype(bf),
        "ones_r": np.ones((1, 128), f32),
        "qgain": np.ascontiguousarray(np.broadcast_to(np.asarray(qnw, f32), (128, 128))),
        "kgain": np.ascontiguousarray(np.broadcast_to(np.asarray(knw, f32), (128, 128))),
        "gscale": np.ascontiguousarray(np.asarray(gsc, f32).reshape(NOC, 128).T),
        "uscale": np.ascontiguousarray(np.asarray(usc, f32).reshape(NOC, 128).T),
        "dscale": np.ascontiguousarray(np.broadcast_to(np.asarray(dsc, f32), (128, HID))),
        "epsv": np.ascontiguousarray(np.broadcast_to(np.array([LN_EPS, RMS_EPS], f32), (128, 2))),
    }
    masks = []
    for core in range(N_CORES):
        m = np.zeros((128, NTT * 1024), f32)
        for qt in range(NTT):
            for gi in range(2):
                for j in range(4):
                    kt = NKT - 8 + gi * 4 + j
                    kgl = kt * 128 + np.arange(128)[:, None]
                    qgl = core * S_CORE + qt * 128 + np.arange(128)[None, :]
                    msk = (kgl >= S_TOT - WINDOW) & (kgl <= qgl)
                    m[:, qt * 1024 + gi * 512 + j * 128: qt * 1024 + gi * 512 + (j + 1) * 128] = \
                        np.where(msk, MASK_NEG, 0.0)
        masks.append(m.astype(f8))
    return c, masks


_CACHE = {}


def kernel(L0_attn_hidden_states, L0_attn_Wq, L0_attn_Wk, L0_attn_Wv, L0_attn_Wo,
           L0_attn_q_norm_w, L0_attn_k_norm_w, L0_ffn_gate_w, L0_ffn_up_w,
           L0_ffn_down_w, L0_ffn_gate_scale, L0_ffn_up_scale, L0_ffn_down_scale):
    x = np.asarray(L0_attn_hidden_states)
    B, L, H = x.shape
    S_CORE = L // N_CORES
    if S_CORE not in _CACHE:
        _CACHE[S_CORE] = build_program(S_CORE)
    nc, meta = _CACHE[S_CORE]
    consts, masks = build_consts(S_CORE, L0_attn_q_norm_w, L0_attn_k_norm_w,
                                 L0_ffn_gate_scale, L0_ffn_up_scale, L0_ffn_down_scale)
    shared = {
        "wq": np.ascontiguousarray(np.asarray(L0_attn_Wq, np.float32)),
        "wk": np.ascontiguousarray(np.asarray(L0_attn_Wk, np.float32)),
        "wv": np.ascontiguousarray(np.asarray(L0_attn_Wv, np.float32)),
        "wo": np.ascontiguousarray(np.asarray(L0_attn_Wo, np.float32)),
        "gate_w": np.ascontiguousarray(np.asarray(L0_ffn_gate_w, np.float32)),
        "up_w": np.ascontiguousarray(np.asarray(L0_ffn_up_w, np.float32)),
        "down_w": np.ascontiguousarray(np.asarray(L0_ffn_down_w, np.float32)),
    }
    shared.update(consts)
    xs = np.ascontiguousarray(x.reshape(L, H).astype(np.float32))
    in_maps = []
    for core in range(N_CORES):
        m = dict(shared)
        m["x"] = xs[core * S_CORE:(core + 1) * S_CORE]
        m["mask"] = masks[core]
        in_maps.append(m)
    res = run_bass_kernel_spmd(nc, in_maps, core_ids=list(range(N_CORES)))
    out = np.concatenate([res.results[c]["out"] for c in range(N_CORES)], axis=0)
    return out.reshape(B, L, H)


# version 13 (final)
# speedup vs baseline: 1.2971x; 1.0197x over previous
import math
import contextlib
import numpy as np
import ml_dtypes
import concourse.bass as bass
import concourse.mybir as mybir
from concourse.bass_utils import run_bass_kernel_spmd

F32 = mybir.dt.float32
BF16 = mybir.dt.bfloat16
F8 = mybir.dt.float8e4
AF = mybir.ActivationFunctionType
ALU = mybir.AluOpType
AXX = mybir.AxisListType.X

N_CORES = 8
HID = 2048
INTER = 6144
NQH = 16
NKV = 8
HD = 128
WINDOW = 1024
C_SCALE = 1.0 / math.sqrt(HD)
LN_EPS = 1e-5
RMS_EPS = 1e-6
MASK_NEG = -240.0
NEG_BIG = -3.0e38

ENGS = ("sync", "scalar", "vector", "tensor", "gpsimd")


class Sched:

    def __init__(self):
        self.steps = {e: [] for e in ENGS}
        self.ctr = {}
        self.names = []

    def sem(self, name):
        if name not in self.ctr:
            self.ctr[name] = 0
            self.names.append(name)
        return name

    def val(self, name):
        self.sem(name)
        return self.ctr[name]

    def add(self, eng, fn, waits=(), inc=None, inc_n=1, tag=None):
        w = tuple((self.sem(s), v) for (s, v) in waits if v > 0)
        if inc is not None:
            self.sem(inc)
            self.ctr[inc] += inc_n
        self.steps[eng].append((w, fn, inc, inc_n, tag))

    def validate(self):
        for e in ENGS:
            for (waits, _f, _i, _n, _t) in self.steps[e]:
                for (s, v) in waits:
                    assert v <= self.ctr[s], f"unsatisfiable wait {s}>={v} (max {self.ctr[s]})"

    def emit(self, eng_name, eng, sems):
        for (waits, fn, inc, inc_n, tag) in self.steps[eng_name]:
            for (s, v) in waits:
                eng.wait_ge(sems[s], v)
            inst = fn(eng)
            if tag and inst is not None:
                inst.annotate(tag)
            if inc is not None:
                assert inst is not None, f"step with inc={inc} returned None"
                inst.then_inc(sems[inc], inc_n)


def build_program(S_CORE):
    S_TOT = N_CORES * S_CORE
    NTT = S_CORE // 128
    NKT = S_TOT // 128
    NGRP = NKT // 4
    NU = NQH * NTT
    NCH = 4
    GPC = NGRP // NCH
    assert NGRP % NCH == 0 and NGRP >= 4
    NOC = INTER // 128
    NT5 = S_CORE // 512
    NQO = HID // 512
    NKO = (NKV * HD) // 512
    assert S_CORE % 512 == 0

    nc = bass.Bass("TRN2", target_bir_lowering=False, debug=False)
    ap = lambda t: t.ap()
    x_in = ap(nc.dram_tensor("x", [S_CORE, HID], F32, kind="ExternalInput"))
    wq_in = ap(nc.dram_tensor("wq", [HID, HID], F32, kind="ExternalInput"))
    wk_in = ap(nc.dram_tensor("wk", [NKV * HD, HID], F32, kind="ExternalInput"))
    wv_in = ap(nc.dram_tensor("wv", [NKV * HD, HID], F32, kind="ExternalInput"))
    wo_in = ap(nc.dram_tensor("wo", [HID, HID], F32, kind="ExternalInput"))
    gate_in = ap(nc.dram_tensor("gate_w", [HID, INTER], F32, kind="ExternalInput"))
    up_in = ap(nc.dram_tensor("up_w", [HID, INTER], F32, kind="ExternalInput"))
    down_in = ap(nc.dram_tensor("down_w", [INTER, HID], F32, kind="ExternalInput"))
    ident_f = ap(nc.dram_tensor("ident_f", [128, 128], F32, kind="ExternalInput"))
    ident_b = ap(nc.dram_tensor("ident_b", [128, 128], BF16, kind="ExternalInput"))
    ones_r = ap(nc.dram_tensor("ones_r", [1, 128], F32, kind="ExternalInput"))
    qg_in = ap(nc.dram_tensor("qgain", [128, 128], F32, kind="ExternalInput"))
    kg_in = ap(nc.dram_tensor("kgain", [128, 128], F32, kind="ExternalInput"))
    gs_in = ap(nc.dram_tensor("gscale", [128, NOC], F32, kind="ExternalInput"))
    us_in = ap(nc.dram_tensor("uscale", [128, NOC], F32, kind="ExternalInput"))
    ds_in = ap(nc.dram_tensor("dscale", [128, HID], F32, kind="ExternalInput"))
    mask_in = ap(nc.dram_tensor("mask", [128, NTT * 1024], F8, kind="ExternalInput"))
    eps_in = ap(nc.dram_tensor("epsv", [128, 2], F32, kind="ExternalInput"))
    out_d = ap(nc.dram_tensor("out", [S_CORE, HID], F32, kind="ExternalOutput"))
    qsc = ap(nc.dram_tensor("qscratch", [NQH * HD, S_CORE], BF16))
    AGC = max(S_CORE, NKV * HD)
    AGR = NKV * HD + S_CORE
    ag_i = ap(nc.dram_tensor("ag_in", [AGR, AGC], BF16))
    ag_o = ap(nc.dram_tensor("ag_out", [N_CORES * AGR, AGC], BF16))
    uTd = ap(nc.dram_tensor("uTd", [NQH * HD, S_CORE], BF16))

    S = Sched()
    es = contextlib.ExitStack()
    sbE = lambda name, shape, dt: es.enter_context(nc.sbuf_tensor(name, shape, dt))

    with es:
        idf = sbE("idf", [128, 128], F32)
        idb = sbE("idb", [128, 128], BF16)
        onr = sbE("onr", [1, 128], F32)
        qg = sbE("qg", [128, 128], F32)
        kg = sbE("kg", [128, 128], F32)
        gs = sbE("gs", [128, NOC], F32)
        us = sbE("us", [128, NOC], F32)
        mask_sb = sbE("mask_sb", [128, NTT * 1024], F8)
        epsv = sbE("epsv_sb", [128, 2], F32)
        stat = sbE("stat", [128, 96], F32)
        mred = sbE("mred", [1, 2], F32)
        negm = sbE("negm", [128, 2], F32)
        recs = sbE("recs", [128, 4], F32)
        uhat = sbE("uhat", [128, 2 * 256], BF16)
        mpos = sbE("mpos", [128, 2], F32)
        asub = sbE("asub", [128, 2], F32)
        alpha = sbE("alpha", [128, 2], F32)
        maxv = sbE("maxv", [128, 2 * 16], F32)
        ffn_sb = sbE("ffn_sb", [128, NTT * HID], BF16)
        quad = es.enter_context(nc.psum_tensor("quad", [128, 2048], F32))
        pair = es.enter_context(nc.psum_tensor("pair", [128, 1024], F32))
        b6 = es.enter_context(nc.psum_tensor("b6", [128, 512], F32))
        b7 = es.enter_context(nc.psum_tensor("b7", [128, 512], F32))

        b6b = b6[:].bitcast(BF16)
        b7b = b7[:].bitcast(BF16)
        pairb = pair[:].bitcast(BF16)

        def bankq(i):
            return quad[:, i * 512:(i + 1) * 512]

        for dst, src in [(idf[:], ident_f), (idb[:], ident_b), (onr[:], ones_r),
                         (qg[:], qg_in), (kg[:], kg_in), (gs[:], gs_in),
                         (us[:], us_in), (mask_sb[:], mask_in), (epsv[:], eps_in)]:
            S.add("sync", lambda e, d=dst, s=src: e.dma_start(out=d, in_=s),
                  inc="c_ld", inc_n=16)

        es12 = contextlib.ExitStack()
        sb12 = lambda name, shape, dt: es12.enter_context(nc.sbuf_tensor(name, shape, dt))
        x_db = sb12("x_db", [128, 2 * HID], F32)
        xn_tm = sb12("xn_tm", [128, HID], BF16)
        xn_fm = sb12("xn_fm", [128, 16 * S_CORE], BF16)
        wnat = sb12("wnat", [128, 2 * HID], BF16)
        wT = sb12("wT", [128, 16 * 512], BF16)
        qraw = sb12("qraw", [128, 2 * 512], F32)
        g8 = sb12("g8", [128, 2 * HID], F8)
        u8 = sb12("u8", [128, 2 * HID], F8)
        d8 = sb12("d8", [128, 4 * 512], F8)
        silu2 = sb12("silu2", [128, 2 * 512], BF16)
        fuse = sb12("fuse", [128, NOC * 512], BF16)
        ds_bc = sb12("ds_bc", [128, HID], F32)
        zbuf = sb12("zbuf", [128, 512], BF16)
        stg4 = sb12("stg4", [128, 4 * 512], BF16)

        S.add("sync", lambda e: e.dma_start(out=ds_bc[:], in_=ds_in), inc="c_ld", inc_n=16)
        CLD = S.val("c_ld")

        AGC = max(S_CORE, NKV * HD)
        if AGC > S_CORE:
            S.add("vector", lambda e: e.memset(zbuf[:], 0.0), inc="p_zm")
            for rb in range(NKV * HD // 128):
                S.add("sync",
                      lambda e, rb=rb: e.dma_start(
                          out=ag_i[rb * 128:(rb + 1) * 128, S_CORE:AGC], in_=zbuf[:, 0:AGC - S_CORE]),
                      waits=[("p_zm", 1)], inc="p_zp", inc_n=16)
            ZP_W = [("p_zp", S.val("p_zp"))]
        else:
            ZP_W = []

        xr = x_in.rearrange("(a p) n -> a p n", p=128)
        xn_c = xn_fm[:, :].rearrange("p (c t) -> p c t", c=16)

        for tt in range(NTT):
            par = tt % 2
            sl = lambda k, w=1, par=par: stat[:, par * 48 + k: par * 48 + k + w]
            xs = x_db[:, par * HID:(par + 1) * HID]
            S.add("sync", lambda e, xs=xs, tt=tt: e.dma_start(out=xs, in_=xr[tt]),
                  waits=[("c_ld", CLD), ("p_ln", tt - 1)], inc=f"p_x{par}", inc_n=16)
            S.add("scalar",
                  lambda e, xs=xs, d=sl(0): e.activation(out=xn_tm[:], in_=xs,
                                                         func=AF.Identity, accum_out=d),
                  waits=[(f"p_x{par}", 16 * (tt // 2 + 1)), ("p_tT", tt * 4)])
            S.add("scalar",
                  lambda e, xs=xs, d=sl(1): e.activation(out=fuse[:, 0:HID].bitcast(BF16) if False else fuse[:, 0:HID], in_=xs,
                                                         func=AF.Square, accum_out=d))
            S.add("scalar", lambda e: e.drain(), inc="p_st")
            PST = S.val("p_st")

            def ln_stats(e, sl=sl):
                e.tensor_scalar(out=sl(2), in0=sl(0), scalar1=1.0 / HID, scalar2=0.0,
                                op0=ALU.mult, op1=ALU.add)
                e.drain()
                e.tensor_tensor(out=sl(4), in0=sl(2), in1=sl(2), op=ALU.mult)
                e.drain()
                return e.scalar_tensor_tensor(out=sl(5), in0=sl(1), scalar=1.0 / HID,
                                              in1=sl(4), op0=ALU.mult, op1=ALU.subtract)
            S.add("vector", ln_stats, waits=[("p_st", PST)], inc="p_ve")
            S.add("scalar", lambda e, sl=sl: e.activation(out=sl(6), in_=sl(5), func=AF.Sqrt,
                                                          bias=epsv[:, 0:1]),
                  waits=[("p_ve", S.val("p_ve"))])
            S.add("scalar", lambda e: e.drain(), inc="p_sq")
            PSQ = S.val("p_sq")

            def ln_apply(e, xs=xs, sl=sl):
                e.reciprocal(sl(7), sl(6))
                e.drain()
                return e.tensor_scalar(out=xn_tm[:], in0=xs, scalar1=sl(2), scalar2=sl(7),
                                       op0=ALU.subtract, op1=ALU.mult)
            S.add("vector", ln_apply, waits=[("p_sq", PSQ), ("p_tT", tt * 4)], inc="p_ln")
            PLN = S.val("p_ln")
            for qtr in range(4):
                for j in range(4):
                    c = qtr * 4 + j
                    w = []
                    if j == 0:
                        w = [("p_xnT", tt * 4 + qtr)]
                        if qtr == 0:
                            w.append(("p_ln", PLN))
                    S.add("tensor",
                          lambda e, c=c, j=j: e.transpose(b6b[:, j * 128:(j + 1) * 128],
                                                          xn_tm[:, c * 128:(c + 1) * 128],
                                                          idb[:]),
                          waits=w, inc=("p_tT" if j == 3 else None))
                S.add("vector",
                      lambda e, tt=tt, qtr=qtr: e.tensor_copy(
                          xn_c[:, qtr * 4:(qtr + 1) * 4, tt * 128:(tt + 1) * 128],
                          b6b[:, 0:512].rearrange("p (c t) -> p c t", c=4)),
                      waits=[("p_tT", S.val("p_tT"))], inc="p_xnT")
        XND = S.val("p_xnT")

        o_specs = ([("q", i) for i in range(NQO)] + [("k", i) for i in range(NKO)]
                   + [("v", i) for i in range(NKO)])
        w_dram = {"q": wq_in, "k": wk_in, "v": wv_in}
        n_stg = 0
        n_qk = 0

        pend_B = []

        for oi, (kind, osl) in enumerate(o_specs):
            wd = w_dram[kind].rearrange("(a p) n -> a p n", p=128)
            for osub in range(4):
                kpc = oi * 4 + osub
                pw = kpc % 2
                dst = wnat[:, pw * HID:(pw + 1) * HID]
                S.add("gpsimd", lambda e, d=dst, s=wd[osl * 4 + osub]: e.dma_start(out=d, in_=s),
                      waits=[("p_wT", (kpc - 1) * 16), ("c_ld", CLD)],
                      inc=f"p_wn{pw}", inc_n=16)
                PWN = S.val(f"p_wn{pw}")
                for ic in range(16):
                    trn = kpc * 16 + ic
                    off = (trn % 2) * 1024 + ((trn // 2) % 4) * 128
                    S.add("tensor",
                          lambda e, d=dst, ic=ic, off=off: e.transpose(
                              pairb[:, off:off + 128],
                              d[:, ic * 128:(ic + 1) * 128], idb[:]),
                          waits=([(f"p_wn{pw}", PWN)] if ic == 0 else []) +
                                [("p_wTe", trn - 1)],
                          inc="p_wT")
                    S.add("vector",
                          lambda e, ic=ic, osub=osub, off=off: e.tensor_copy(
                              wT[:, ic * 512 + osub * 128: ic * 512 + (osub + 1) * 128],
                              pairb[:, off:off + 128]),
                          waits=[("p_wT", S.val("p_wT"))], inc="p_wTe")
            WTE = S.val("p_wTe")

            for tt in range(NTT):
                bank = tt % 4
                for ic in range(16):
                    w = []
                    if ic == 0:
                        w = [("p_xnT", XND), ("p_wTe", WTE),
                             ("p_pev", S.val("p_pev") - 3)]
                    S.add("tensor",
                          lambda e, bank=bank, ic=ic, tt=tt: e.matmul(
                              bankq(bank), xn_c[:, ic, tt * 128:(tt + 1) * 128],
                              wT[:, ic * 512:(ic + 1) * 512],
                              start=(ic == 0), stop=(ic == 15)),
                          waits=w, inc=("p_mmP" if ic == 15 else None))
                while pend_B:
                    pend_B.pop(0)()
                MMP = S.val("p_mmP")
                slot = n_stg % 4
                nuse = n_stg // 4
                n_stg += 1
                stg_s = stg4[:, slot * 512:(slot + 1) * 512]
                if kind == "v":
                    S.add("vector",
                          lambda e, bank=bank, d=stg_s: e.tensor_copy(d, bankq(bank)),
                          waits=[("p_mmP", MMP), (f"p_stg{slot}", 16 * nuse)],
                          inc="p_pev")
                    S.add("scalar",
                          lambda e, s=stg_s, tt=tt, osl=osl: e.dma_start(
                              out=ag_i[NKV * HD + tt * 128: NKV * HD + (tt + 1) * 128,
                                       osl * 512:(osl + 1) * 512], in_=s),
                          waits=[("p_pev", S.val("p_pev"))], inc=f"p_stg{slot}", inc_n=16)
                else:
                    n_qk += 1
                    parq = tt % 2
                    slp = lambda k, w=1, parq=parq: stat[:, parq * 48 + 16 + k: parq * 48 + 16 + k + w]
                    qr = qraw[:, parq * 512:(parq + 1) * 512]
                    S.add("vector",
                          lambda e, bank=bank, qr=qr: e.tensor_copy(qr, bankq(bank)),
                          waits=[("p_mmP", MMP), ("p_rms", n_qk - 2)],
                          inc="p_pev")
                    PEV = S.val("p_pev")
                    for h4 in range(4):
                        S.add("scalar",
                              lambda e, qr=qr, h4=h4, d=slp(h4): e.activation(
                                  out=xn_tm[:, h4 * 128:(h4 + 1) * 128],
                                  in_=qr[:, h4 * 128:(h4 + 1) * 128],
                                  func=AF.Square, accum_out=d),
                              waits=([("p_pev", PEV), ("p_tT", NTT * 4)] if h4 == 0 else []))
                    S.add("scalar", lambda e: e.drain(), inc="p_ss")
                    S.add("vector",
                          lambda e, slp=slp: e.tensor_scalar(
                              out=slp(4, 4), in0=slp(0, 4), scalar1=1.0 / HD,
                              scalar2=0.0, op0=ALU.mult, op1=ALU.add),
                          waits=[("p_ss", S.val("p_ss"))], inc="p_rv")
                    S.add("scalar",
                          lambda e, slp=slp: e.activation(out=slp(8, 4), in_=slp(4, 4),
                                                          func=AF.Sqrt, bias=epsv[:, 1:2]),
                          waits=[("p_rv", S.val("p_rv"))])
                    S.add("scalar", lambda e: e.drain(), inc="p_sq2")

                    def rms_nr(e, slp=slp):
                        e.reciprocal(slp(28, 4), slp(8, 4))
                        return e.drain()
                    S.add("vector", rms_nr, waits=[("p_sq2", S.val("p_sq2"))], inc="p_nr")
                    gt = qg if kind == "q" else kg
                    for h4 in range(4):
                        w = [("p_qT", S.val("p_qT"))] if h4 == 0 else []
                        S.add("vector",
                              lambda e, qr=qr, h4=h4, gt=gt, slp=slp: e.scalar_tensor_tensor(
                                  out=silu2[:, h4 * 128:(h4 + 1) * 128],
                                  in0=qr[:, h4 * 128:(h4 + 1) * 128],
                                  scalar=slp(28 + h4), in1=gt[:],
                                  op0=ALU.mult, op1=ALU.mult),
                              waits=w, inc=("p_rms" if h4 == 3 else None))
                    PRMS = S.val("p_rms")
                    if kind == "q":
                        dstq = qsc.rearrange("(h p) t -> p h t", p=128)[
                            :, osl * 4:osl * 4 + 4, tt * 128:(tt + 1) * 128]
                    else:
                        dstq = ag_i[0:NKV * HD, 0:S_CORE].rearrange("(h p) t -> p h t", p=128)[
                            :, osl * 4:osl * 4 + 4, tt * 128:(tt + 1) * 128]

                    def B(PRMS=PRMS, slot=slot, nuse=nuse, stg_s=stg_s, dstq=dstq):
                        for h4 in range(4):
                            S.add("tensor",
                                  lambda e, h4=h4: e.transpose(
                                      b6b[:, h4 * 128:(h4 + 1) * 128],
                                      silu2[:, h4 * 128:(h4 + 1) * 128], idb[:]),
                                  waits=([("p_rms", PRMS), ("p_qTe", S.val("p_qTe"))]
                                         if h4 == 0 else []),
                                  inc=("p_qT" if h4 == 3 else None))
                        S.add("vector",
                              lambda e, d=stg_s: e.tensor_copy(d, b6b[:, 0:512]),
                              waits=[("p_qT", S.val("p_qT")), (f"p_stg{slot}", 16 * nuse)],
                              inc="p_qTe")
                        S.add("scalar",
                              lambda e, s=stg_s, d=dstq: e.dma_start(
                                  out=d, in_=s.rearrange("p (h t) -> p h t", h=4)),
                              waits=[("p_qTe", S.val("p_qTe"))], inc=f"p_stg{slot}",
                              inc_n=16)
                    pend_B.append(B)
        while pend_B:
            pend_B.pop(0)()
        STG_W = [(f"p_stg{i}", S.val(f"p_stg{i}")) for i in range(4)]
        PEV_ALL = S.val("p_pev")
        QTE_ALL = S.val("p_qTe")

        S.add("gpsimd",
              lambda e: e.collective_compute("AllGather", ALU.bypass,
                                             replica_groups=[list(range(N_CORES))],
                                             ins=[ag_i.opt()], outs=[ag_o.opt()]),
              waits=STG_W + ZP_W, inc="p_ag")

        gcol = gate_in.rearrange("(c p) n -> p c n", p=128)
        ucol = up_in.rearrange("(c p) n -> p c n", p=128)
        dcol = down_in.rearrange("(c p) n -> p c n", p=128)
        NOCP = NOC // 2
        ALLB = [quad[:, i * 512:(i + 1) * 512] for i in range(4)] + \
               [pair[:, 0:512], pair[:, 512:1024], b6[:], b7[:]]

        for oph in range(2):
            for ocr in range(NOCP):
                oc = oph * NOCP + ocr
                j = oc
                pw = j % 2
                gds = g8[:, pw * HID:(pw + 1) * HID]
                uds = u8[:, pw * HID:(pw + 1) * HID]
                S.add("gpsimd",
                      lambda e, d=gds, oc=oc: e.dma_start(
                          out=d.rearrange("p (c n) -> p c n", c=16),
                          in_=gcol[:, :, oc * 128:(oc + 1) * 128]),
                      waits=[("p_gmm", (j - 2) * NT5 + NT5)], inc=f"p_g8{pw}", inc_n=16)
                S.add("gpsimd",
                      lambda e, d=uds, oc=oc: e.dma_start(
                          out=d.rearrange("p (c n) -> p c n", c=16),
                          in_=ucol[:, :, oc * 128:(oc + 1) * 128]),
                      waits=[("p_umm", (j - 2) * NT5 + NT5)], inc=f"p_u8{pw}", inc_n=16)
                PG8 = S.val(f"p_g8{pw}")
                PU8 = S.val(f"p_u8{pw}")
                for t5 in range(NT5):
                    j2 = j * NT5 + t5
                    pw2 = j2 % 2
                    gp = pair[:, 0:512] if pw2 == 0 else b6[:]
                    upp = pair[:, 512:1024] if pw2 == 0 else b7[:]
                    for ic in range(16):
                        w = []
                        if ic == 0:
                            w = [(f"p_g8{pw}", PG8), ("p_xnT", XND), ("p_si", j2 - 1)]
                            if j2 <= 1:
                                w += [("p_pev", PEV_ALL), ("p_qTe", QTE_ALL)]
                            if oph == 1 and ocr == 0 and t5 <= 1:
                                w += [("p_dne", oph * 16 * NT5)]
                        S.add("tensor",
                              lambda e, gp=gp, gds=gds, ic=ic, t5=t5: e.matmul(
                                  gp, gds[:, ic * 128:(ic + 1) * 128],
                                  xn_c[:, ic, t5 * 512:(t5 + 1) * 512],
                                  start=(ic == 0), stop=(ic == 15)),
                              waits=w, inc=("p_gmm" if ic == 15 else None))
                    for ic in range(16):
                        w = []
                        if ic == 0:
                            w = [("p_fu", j2 - 1), (f"p_u8{pw}", PU8)]
                            if oph == 1 and ocr == 0 and t5 <= 1:
                                w += [("p_dne", oph * 16 * NT5)]
                        S.add("tensor",
                              lambda e, upp=upp, uds=uds, ic=ic, t5=t5: e.matmul(
                                  upp, uds[:, ic * 128:(ic + 1) * 128],
                                  xn_c[:, ic, t5 * 512:(t5 + 1) * 512],
                                  start=(ic == 0), stop=(ic == 15)),
                              waits=w, inc=("p_umm" if ic == 15 else None))
                    ps = pw2
                    S.add("scalar",
                          lambda e, gp=gp, oc=oc, ps=ps: e.activation(
                              out=silu2[:, ps * 512:(ps + 1) * 512], in_=gp, func=AF.Silu,
                              scale=gs[:, oc:oc + 1]),
                          waits=[("p_gmm", S.val("p_gmm")), ("p_fu", j2 - 1)], inc="p_si")
                    fsl = (ocr * NT5 + t5)
                    S.add("vector",
                          lambda e, upp=upp, oc=oc, ps=ps, fsl=fsl: e.scalar_tensor_tensor(
                              out=fuse[:, fsl * 512:(fsl + 1) * 512], in0=upp,
                              scalar=us[:, oc:oc + 1], in1=silu2[:, ps * 512:(ps + 1) * 512],
                              op0=ALU.mult, op1=ALU.mult),
                          waits=[("p_umm", S.val("p_umm")), ("p_si", S.val("p_si"))] +
                                ([("p_dnm", oph * 4 * NOCP if ocr == 0 and t5 == 0 else 0)]
                                 if oph >= 1 else []),
                          inc="p_fu")
            FUD = S.val("p_fu")
            for hs in range(4):
                for ocr in range(NOCP):
                    oc = oph * NOCP + ocr
                    jd = (oph * 4 + hs) * NOCP + ocr
                    sd = jd % 4
                    dds = d8[:, sd * 512:(sd + 1) * 512]
                    S.add("gpsimd",
                          lambda e, d=dds, oc=oc, hs=hs: e.dma_start(
                              out=d, in_=dcol[:, oc, hs * 512:(hs + 1) * 512]),
                          waits=[("p_dnm", jd - 3), ("p_fu", FUD if ocr == 0 else 0)],
                          inc=f"p_d8{sd}", inc_n=16)
                    PD8 = S.val(f"p_d8{sd}")
                    for t5 in range(NT5):
                        for ts in range(4):
                            bk = t5 * 4 + ts
                            fsl = (ocr * NT5 + t5)
                            w = []
                            if ts == 0 and t5 == 0:
                                w = [(f"p_d8{sd}", PD8), ("p_fu", FUD)]
                                if ocr == 0:
                                    w.append(("p_dne", (oph * 4 + hs) * 4 * NT5))
                            S.add("tensor",
                                  lambda e, bk=bk, fsl=fsl, ts=ts, dds=dds, ocr=ocr: e.matmul(
                                      ALLB[bk], fuse[:, fsl * 512 + ts * 128: fsl * 512 + (ts + 1) * 128],
                                      dds, start=(ocr == 0), stop=(ocr == NOCP - 1)),
                                  waits=w, inc=("p_dnm" if (ts == 3 and t5 == NT5 - 1) else None),
                                  tag=f"dn{oph}.{hs}.{ocr}.{t5}.{ts}")
                DNM = S.val("p_dnm")
                for t5 in range(NT5):
                    for ts in range(4):
                        bk = t5 * 4 + ts
                        tt = t5 * 4 + ts
                        if oph == 0:
                            S.add("vector",
                                  lambda e, bk=bk, tt=tt, hs=hs: e.tensor_tensor(
                                      out=ffn_sb[:, tt * HID + hs * 512: tt * HID + (hs + 1) * 512],
                                      in0=ALLB[bk], in1=ds_bc[:, hs * 512:(hs + 1) * 512],
                                      op=ALU.mult),
                                  waits=[("p_dnm", DNM)], inc="p_dne")
                        else:
                            def ev2(e, bk=bk, tt=tt, hs=hs):
                                qtmp = qraw[:, (bk % 2) * 512:(bk % 2) * 512 + 512]
                                e.tensor_tensor(out=qtmp,
                                                in0=ALLB[bk], in1=ds_bc[:, hs * 512:(hs + 1) * 512],
                                                op=ALU.mult)
                                e.drain()
                                return e.tensor_tensor(
                                    out=ffn_sb[:, tt * HID + hs * 512: tt * HID + (hs + 1) * 512],
                                    in0=qtmp,
                                    in1=ffn_sb[:, tt * HID + hs * 512: tt * HID + (hs + 1) * 512],
                                    op=ALU.add)
                            S.add("vector", ev2, waits=[("p_dnm", DNM)], inc="p_dne")
        GMM_ALL = S.val("p_gmm")
        UMM_ALL = S.val("p_umm")
        DNE_ALL = S.val("p_dne")
        DNM_ALL = S.val("p_dnm")
        FU_ALL = S.val("p_fu")
        SI_ALL = S.val("p_si")

        es12.close()
        es3 = contextlib.ExitStack()
        sb3 = lambda name, shape, dt: es3.enter_context(nc.sbuf_tensor(name, shape, dt))
        sc_sb = sb3("sc_sb", [128, 2 * (NKT // 4) * 256], F32)
        p_sb = sb3("p_sb", [128, 2 * (NKT // 4) * 256], BF16)
        usb = sb3("usb", [128, 2 * 512], F32)
        EVSEM = {}
        kbuf = sb3("kbuf", [128, 2 * NKT * 128], BF16)
        vbuf = sb3("vbuf", [128, 2 * NKT * 129], BF16)
        qbuf = sb3("qbuf", [128, 2 * S_CORE], BF16)
        ustg = sb3("ustg", [128, 2 * 128], BF16)

        agv = ag_o.rearrange("(r f p) t -> r f p t", r=N_CORES, p=128)
        vview = vbuf[:, :].rearrange("p (b k) -> p b k", k=129)
        NQT = NTT // 2
        NSTG = 4
        NKS = NKT // NSTG
        NGE = NKS // 2
        NST = NQH * NQT * NSTG
        NCH2 = 2
        GPC2 = NGE // NCH2
        DNM_ALL = S.val("p_dnm")
        DNE_ALL = S.val("p_dne")
        FU_ALL = S.val("p_fu")
        GMM_ALL = S.val("p_gmm")
        UMM_ALL = S.val("p_umm")

        def grp_eng(st_idx, ge):
            u, st = divmod(st_idx, NSTG)
            masked = (st == NSTG - 1) and (ge >= NGE - 4)
            if masked:
                return "vector_mask"
            if st == NSTG - 1:
                return "scalar"
            return "scalar" if ge in (NGE // 2, NGE // 2 + 1) else "vector"

        S.add("vector", lambda e: e.memset(vview[:, :, 128:129], 1.0),
              waits=[("p_dne", DNE_ALL)], inc="p_on")

        def load_kv(h):
            pk = h % 2
            kd = kbuf[:, pk * NKT * 128:(pk + 1) * NKT * 128]
            w = [("p_ag", 1), ("p_dnm", DNM_ALL), ("p_on", 1)]
            if h >= 2:
                w.append(("p_sc2", 2 * (h - 1) * NTT // 2 * NSTG * NGE))
                w.append(("p_ue", NSTG * (2 * (h - 1) * (NTT // 2))))
            S.add("sync",
                  lambda e, kd=kd, h=h: e.dma_start(
                      out=kd.rearrange("p (r t) -> p r t", r=N_CORES),
                      in_=agv[:, h, :, 0:S_CORE].transpose([1, 0, 2])),
                  waits=w, inc=f"p_kv{pk}", inc_n=16)
            vdst = vbuf[:, :].rearrange("p (b2 r j k) -> p b2 r j k",
                                        b2=2, r=N_CORES, j=NTT)[:, pk, :, :, 0:128]
            for r in range(N_CORES):
                S.add("sync",
                      lambda e, h=h, vdst=vdst, r=r: e.dma_start(
                          out=vdst[:, r],
                          in_=agv[r, NKV:NKV + NTT, :, h * 128:(h + 1) * 128]
                              .transpose([1, 0, 2])),
                      inc=f"p_kv{pk}", inc_n=16)

        def load_q(qh):
            pq = qh % 2
            w = STG_W + [("p_ln", NTT), ("p_dnm", DNM_ALL)]
            if qh >= 2:
                w.append(("p_sc2", (qh - 1) * NTT // 2 * NSTG * NGE))
            S.add("sync",
                  lambda e, qh=qh, pq=pq: e.dma_start(
                      out=qbuf[:, pq * S_CORE:(pq + 1) * S_CORE],
                      in_=qsc[qh * 128:(qh + 1) * 128, :]),
                  waits=w, inc=f"p_q{pq}", inc_n=16)

        load_kv(0)
        load_kv(1)
        load_q(0)
        load_q(1)

        def emit_av_mm(st_idx):
            u, st = divmod(st_idx, NSTG)
            qh, q2 = divmod(u, NQT)
            hk = qh // 2
            pk = hk % 2
            pu = u % 2
            avb = pair[:, pu * 512:(pu + 1) * 512]
            QSO = (0, 136)
            for qs in range(2):
                for ch in range(NCH2):
                    for jj in range(GPC2 * 2):
                        kt_rel = ch * GPC2 * 2 + jj
                        kt = st * NKS + kt_rel
                        w = []
                        if jj == 0 and qs == 0:
                            w = [("p_ex", st_idx * NCH2 + ch + 1)]
                            if ch == 0:
                                if st == 0:
                                    w.append(("p_fu", FU_ALL))
                                    w.append(("p_ue", NSTG * (u - 1)))
                                else:
                                    w.append(("p_ue", NSTG * u + st))
                        spb = (st_idx % 2) * (NKT // 4) * 256
                        S.add("tensor",
                              lambda e, kt=kt, kt_rel=kt_rel, qs=qs, pk=pk, avb=avb, spb=spb: e.matmul(
                                  avb[:, QSO[qs]: QSO[qs] + 129],
                                  p_sb[:, spb + kt_rel * 256 + qs * 128: spb + kt_rel * 256 + (qs + 1) * 128],
                                  vview[:, pk * NKT + kt, :],
                                  start=(kt_rel == 0), stop=(kt_rel == NKS - 1)),
                              waits=w,
                              inc=("p_av" if (jj == GPC2 * 2 - 1 and qs == 1
                                              and ch == NCH2 - 1) else None),
                              tag=f"av{st_idx}.{qs}.{kt_rel}")

        def emit_av_vec(st_idx):
            u, st = divmod(st_idx, NSTG)
            qh, q2 = divmod(u, NQT)
            pu = u % 2
            avb = pair[:, pu * 512:(pu + 1) * 512]
            if st == 0:
                def park(e, pu=pu):
                    e.tensor_copy(usb[:, pu * 512 + 0: pu * 512 + 129], avb[:, 0:129])
                    return e.tensor_copy(usb[:, pu * 512 + 256: pu * 512 + 385], avb[:, 136:265])
                S.add("vector", park,
                      waits=[("p_av", st_idx + 1), ("p_ue", NSTG * u)], inc="p_ue")
                return
            last = (st == NSTG - 1)

            def comb(e, pu=pu, last=last, aslot=st_idx % 2):
                QSO2 = (0, 136)
                for qs in range(2):
                    e.scalar_tensor_tensor(
                        out=usb[:, pu * 512 + qs * 256: pu * 512 + qs * 256 + 129],
                        in0=usb[:, pu * 512 + qs * 256: pu * 512 + qs * 256 + 129],
                        scalar=alpha[:, aslot:aslot + 1],
                        in1=avb[:, QSO2[qs]: QSO2[qs] + 129],
                        op0=ALU.mult, op1=ALU.add)
                if not last:
                    return e.drain()
                e.drain()
                e.reciprocal(recs[:, pu * 2:pu * 2 + 1], usb[:, pu * 512 + 128: pu * 512 + 129])
                e.reciprocal(recs[:, pu * 2 + 1:pu * 2 + 2], usb[:, pu * 512 + 384: pu * 512 + 385])
                e.drain()
                e.tensor_scalar(out=uhat[:, pu * 256: pu * 256 + 128],
                                in0=usb[:, pu * 512: pu * 512 + 128],
                                scalar1=recs[:, pu * 2:pu * 2 + 1], scalar2=0.0,
                                op0=ALU.mult, op1=ALU.add)
                return e.tensor_scalar(out=uhat[:, pu * 256 + 128: pu * 256 + 256],
                                       in0=usb[:, pu * 512 + 256: pu * 512 + 384],
                                       scalar1=recs[:, pu * 2 + 1:pu * 2 + 2], scalar2=0.0,
                                       op0=ALU.mult, op1=ALU.add)
            S.add("vector", comb,
                  waits=[("p_av", st_idx + 1), ("p_al", 3 * u + st),
                         ("p_uT", 2 * (u - 1) + 2)], inc="p_ue", tag=f"comb{st_idx}")
            if not last:
                return
            for qs in range(2):
                n_u = 2 * u + qs
                up_ = n_u % 2
                ub = b6b
                S.add("tensor",
                      lambda e, ub=ub, pu=pu, qs=qs: e.transpose(
                          ub[:, 0:128], uhat[:, pu * 256 + qs * 128: pu * 256 + (qs + 1) * 128],
                          idb[:]),
                      waits=[("p_ue", S.val("p_ue")), ("p_uf", 2 * u + qs)],
                      inc="p_uT")
                S.add("vector",
                      lambda e, ub=ub, up_=up_: e.tensor_copy(
                          ustg[:, up_ * 128:(up_ + 1) * 128], ub[:, 0:128]),
                      waits=[("p_uT", S.val("p_uT")),
                             (f"p_us{up_}", 16 * (n_u // 2))], inc="p_uf")
                dst = uTd.rearrange("(c p) t -> p c t", p=128)[
                    :, qh, q2 * 256 + qs * 128: q2 * 256 + (qs + 1) * 128]
                S.add("scalar",
                      lambda e, dst=dst, up_=up_: e.dma_start(
                          out=dst, in_=ustg[:, up_ * 128:(up_ + 1) * 128]),
                      waits=[("p_uf", S.val("p_uf"))], inc=f"p_us{up_}", inc_n=16)

        def emit_mred(sti):
            mslot = sti % 2
            S.add("vector",
                  lambda e, mslot=mslot: e.tensor_reduce(
                      mred[:, mslot:mslot + 1], b7[0:1, mslot * 256: mslot * 256 + 128],
                      AXX, ALU.max),
                  waits=[("p_mxT", sti + 1), ("p_bc", sti - 1)], inc="p_mv2")

        def emit_bc(sti):
            mslot = sti % 2
            S.add("tensor",
                  lambda e, mslot=mslot: e.matmul(
                      b7[:, mslot * 256 + 128: mslot * 256 + 129], onr[:],
                      mred[:, mslot:mslot + 1], start=True, stop=True),
                  waits=[("p_mv2", sti + 1), ("p_ng", sti - 1)], inc="p_bc")

        def emit_mxT(sti):
            mslot = sti % 2
            S.add("tensor",
                  lambda e, mslot=mslot: e.transpose(
                      b7[0:1, mslot * 256: mslot * 256 + 128],
                      stat[:, 90 + mslot:91 + mslot], idf[:]),
                  waits=[("p_mv1", sti + 1), ("p_mv2", sti - 1)], inc="p_mxT")

        def emit_ngf(sti):
            u, st = divmod(sti, NSTG)
            mslot = sti % 2
            pu = u % 2
            ngw = [("p_bc", sti + 1)]
            if sti >= 2:
                ngw.append(("p_ex", (sti - 1) * NCH2))
            if st >= 1:
                prev = sti - 2 if (sti - 2) % NSTG != 0 else sti - 4
                if prev >= 1:
                    pa = 3 * (prev // NSTG) + (prev % NSTG)
                    if pa > 0:
                        ngw.append(("p_al", pa))

            def ngf(e, mslot=mslot, st=st, pu=pu):
                bcv = b7[:, mslot * 256 + 128: mslot * 256 + 129]
                if st == 0:
                    e.tensor_scalar(out=negm[:, mslot:mslot + 1], in0=bcv,
                                    scalar1=-1.0, scalar2=0.0, op0=ALU.mult, op1=ALU.add)
                    return e.tensor_copy(mpos[:, pu:pu + 1], bcv)
                e.tensor_tensor(out=stat[:, 94:95], in0=bcv,
                                in1=mpos[:, pu:pu + 1], op=ALU.max)
                e.drain()
                e.tensor_scalar(out=negm[:, mslot:mslot + 1], in0=stat[:, 94:95],
                                scalar1=-1.0, scalar2=0.0, op0=ALU.mult, op1=ALU.add)
                e.tensor_tensor(out=asub[:, mslot:mslot + 1], in0=mpos[:, pu:pu + 1],
                                in1=stat[:, 94:95], op=ALU.subtract)
                e.drain()
                return e.tensor_copy(mpos[:, pu:pu + 1], stat[:, 94:95])
            S.add("vector", ngf, waits=ngw, inc="p_ng")

        def emit_alpha_exp(sti):
            u, st = divmod(sti, NSTG)
            mslot = sti % 2
            NG = sti + 1
            if st >= 1:
                aw = [("p_ng", NG), ("p_ue", sti - 1)]
                S.add("scalar",
                      lambda e, mslot=mslot: e.activation(out=alpha[:, mslot:mslot + 1],
                                                          in_=asub[:, mslot:mslot + 1],
                                                          func=AF.Exp),
                      waits=aw, inc="p_al")
            for ch in range(NCH2):
                w = [("p_ng", NG)]
                best = {}
                for gg in range(ch * GPC2, (ch + 1) * GPC2):
                    nm, vv = EVSEM[(sti, gg)]
                    best[nm] = max(best.get(nm, 0), vv)
                w += list(best.items())
                if sti >= 2:
                    w.append(("p_av", sti - 1))
                spb = (sti % 2) * (NKT // 4) * 256
                ssc2 = (sti % 2) * (NKT // 4) * 256
                S.add("scalar",
                      lambda e, ch=ch, mslot=mslot, spb=spb, ssc2=ssc2: e.activation(
                          out=p_sb[:, spb + ch * GPC2 * 512: spb + (ch + 1) * GPC2 * 512],
                          in_=sc_sb[:, ssc2 + ch * GPC2 * 512: ssc2 + (ch + 1) * GPC2 * 512],
                          func=AF.Exp, bias=negm[:, mslot:mslot + 1], scale=1.0),
                      waits=w, inc="p_ex")

        for it in range(NST + 2):
            if 1 <= it <= NST:
                emit_mred(it - 1)
            if it < NST:
                sti = it
                u, st = divmod(sti, NSTG)
                qh, q2 = divmod(u, NQT)
                hk = qh // 2
                pk = hk % 2
                pq = qh % 2
                mslot = sti % 2
                if st == 0 and q2 == 0:
                    if qh + 2 < NQH:
                        load_q(qh + 2)
                    if qh % 2 == 0 and hk + 2 < NKV:
                        load_kv(hk + 2)
                nvec = 0
                for ge in range(NGE):
                    for j in range(2):
                        kt_rel = ge * 2 + j
                        kt = st * NKS + kt_rel
                        slot = kt_rel % 8
                        w = []
                        if j == 0:
                            if ge >= 4 or sti > 0:
                                psti, pg = (sti, ge - 4) if ge >= 4 else (sti - 1, NGE + ge - 4)
                                w.append(EVSEM[(psti, pg)])
                            w += [(f"p_kv{pk}", 16 * (1 + N_CORES) * (hk // 2 + 1)),
                                  (f"p_q{pq}", 16 * (qh // 2 + 1))]
                            if sti == 0 and ge == 0:
                                w.append(("p_dne", DNE_ALL))
                        S.add("tensor",
                              lambda e, slot=slot, kt=kt, pk=pk, pq=pq, q2=q2: e.matmul(
                                  quad[:, slot * 256:(slot + 1) * 256],
                                  kbuf[:, pk * NKT * 128 + kt * 128: pk * NKT * 128 + (kt + 1) * 128],
                                  qbuf[:, pq * S_CORE + q2 * 256: pq * S_CORE + (q2 + 1) * 256],
                                  start=True, stop=True),
                              waits=w, inc=("p_sc2" if j == 1 else None))
                    SC2 = S.val("p_sc2")
                    eng = grp_eng(sti, ge)
                    bank = (2 * ge % 8) // 2
                    ssc = (sti % 2) * (NKT // 4) * 256
                    ew = [("p_sc2", SC2), ("p_ex", (sti - 2) * NCH2 + ge // GPC2 + 1)]
                    if sti == 0 and ge == 0:
                        ew += [("p_gmm", GMM_ALL), ("p_umm", UMM_ALL)]
                    if eng == "vector":
                        S.add("vector",
                              lambda e, bank=bank, ge=ge, mslot=mslot, nv=nvec, ssc=ssc: e.tensor_scalar(
                                  out=sc_sb[:, ssc + ge * 512: ssc + (ge + 1) * 512],
                                  in0=quad[:, bank * 512:(bank + 1) * 512],
                                  scalar1=float(C_SCALE), scalar2=NEG_BIG,
                                  op0=ALU.mult, op1=ALU.max,
                                  accum_out=maxv[:, mslot * 16 + nv: mslot * 16 + nv + 1]),
                              waits=ew, inc="p_evv")
                        EVSEM[(sti, ge)] = ("p_evv", S.val("p_evv"))
                        nvec += 1
                    elif eng == "vector_mask":
                        mg = ge - (NGE - 4)
                        S.add("vector",
                              lambda e, bank=bank, ge=ge, mg=mg, q2=q2, ssc=ssc: e.scalar_tensor_tensor(
                                  out=sc_sb[:, ssc + ge * 512: ssc + (ge + 1) * 512],
                                  in0=quad[:, bank * 512:(bank + 1) * 512],
                                  scalar=float(C_SCALE),
                                  in1=mask_sb[:, q2 * 2048 + mg * 512: q2 * 2048 + (mg + 1) * 512],
                                  op0=ALU.mult, op1=ALU.add),
                              waits=ew, inc="p_evv")
                        EVSEM[(sti, ge)] = ("p_evv", S.val("p_evv"))
                    else:
                        S.add("scalar",
                              lambda e, bank=bank, ge=ge, ssc=ssc: e.mul(
                                  sc_sb[:, ssc + ge * 512: ssc + (ge + 1) * 512],
                                  quad[:, bank * 512:(bank + 1) * 512], float(C_SCALE)),
                              waits=ew, inc="p_evs")
                        EVSEM[(sti, ge)] = ("p_evs", S.val("p_evs"))
                    if ge == 1 and it >= 1:
                        emit_bc(it - 1)
                        emit_ngf(it - 1)
                NV = nvec
                assert NV > 0 or st > 0

                def mx_red(e, mslot=mslot, NV=NV, st=st, pu=u % 2):
                    e.drain()
                    if st == 0:
                        return e.tensor_reduce(stat[:, 90 + mslot:91 + mslot],
                                               maxv[:, mslot * 16: mslot * 16 + NV],
                                               AXX, ALU.max)
                    if NV == 0:
                        return e.tensor_scalar(out=stat[:, 90 + mslot:91 + mslot],
                                               in0=epsv[:, 0:1], scalar1=0.0,
                                               scalar2=NEG_BIG, op0=ALU.mult, op1=ALU.add)
                    return e.tensor_reduce(stat[:, 90 + mslot:91 + mslot],
                                           maxv[:, mslot * 16: mslot * 16 + NV],
                                           AXX, ALU.max)
                mrw = []
                if sti >= 2:
                    mrw.append(("p_mxT", sti - 1))
                S.add("vector", mx_red, waits=mrw, inc="p_mv1")
                if it >= 1:
                    emit_alpha_exp(it - 1)
            if it >= NST and 1 <= it <= NST:
                emit_bc(it - 1)
                emit_ngf(it - 1)
                emit_alpha_exp(it - 1)
            if 2 <= it <= NST + 1:
                emit_av_mm(it - 2)
                emit_av_vec(it - 2)
            if it < NST:
                emit_mxT(it)
        EX_ALL = S.val("p_ex")
        AVC_ALL = S.val("p_av")
        UF_ALL = S.val("p_uf")
        SCG_ALL = S.val("p_sc2")
        EV_ALL = S.val("p_evv")
        EVS_ALL = S.val("p_evs")
        UE_ALL = S.val("p_ue")

        US_W = [(f"p_us{i}", S.val(f"p_us{i}")) for i in range(2)]
        UE_ALL2 = S.val("p_ue")
        es3.close()
        es4 = contextlib.ExitStack()
        sb4 = lambda name, shape, dt: es4.enter_context(nc.sbuf_tensor(name, shape, dt))
        uT4 = sb4("uT4", [128, 2 * 2048], BF16)
        woT = sb4("woT", [128, 16 * 512], BF16)
        wonat = sb4("wonat", [128, 2 * HID], BF16)
        xrl = sb4("xrl", [128, 4 * 512], F32)
        osb = sb4("osb", [128, 4 * 512], F32)
        tmp4 = sb4("tmp4", [128, 4 * 512], F32)

        wor = wo_in.rearrange("(a p) n -> a p n", p=128)
        n_x = 0
        for hs in range(4):
            for osub in range(4):
                kpc = hs * 4 + osub
                pw = kpc % 2
                dst = wonat[:, pw * HID:(pw + 1) * HID]
                S.add("gpsimd",
                      lambda e, d=dst, kpc=kpc: e.dma_start(out=d, in_=wor[kpc]),
                      waits=[("p_ex", EX_ALL), ("p_av", AVC_ALL), ("p_ue", UE_ALL2),
                             ("p_sc2", SCG_ALL), ("p_woT", (kpc - 1) * 16)] + US_W,
                      inc=f"p_won{pw}", inc_n=16)
                PWON = S.val(f"p_won{pw}")
                for ic in range(16):
                    trn = kpc * 16 + ic
                    off = (trn % 2) * 1024 + ((trn // 2) % 4) * 128
                    S.add("tensor",
                          lambda e, d=dst, ic=ic, off=off: e.transpose(
                              pairb[:, off:off + 128],
                              d[:, ic * 128:(ic + 1) * 128], idb[:]),
                          waits=([(f"p_won{pw}", PWON), ("p_uf", UF_ALL),
                                  ("p_ue", UE_ALL), ("p_evs", EVS_ALL)] if ic == 0 else []) +
                                [("p_woTe", trn - 1)],
                          inc="p_woT")
                    S.add("vector",
                          lambda e, ic=ic, osub=osub, off=off: e.tensor_copy(
                              woT[:, ic * 512 + osub * 128: ic * 512 + (osub + 1) * 128],
                              pairb[:, off:off + 128]),
                          waits=[("p_woT", S.val("p_woT"))], inc="p_woTe")
            WOTE = S.val("p_woTe")
            for tt in range(NTT):
                bank = tt % 4
                slot = n_x % 4
                nuse = n_x // 4
                up4 = n_x % 2
                n_x += 1
                S.add("gpsimd",
                      lambda e, slot=slot, tt=tt, hs=hs: e.dma_start(
                          out=xrl[:, slot * 512:(slot + 1) * 512],
                          in_=xr[tt][:, hs * 512:(hs + 1) * 512]),
                      waits=[(f"p_out{slot}", 16 * nuse), ("p_ex", EX_ALL),
                             ("p_ue", UE_ALL2)] + US_W,
                      inc=f"p_xr{slot}", inc_n=16)
                S.add("gpsimd",
                      lambda e, up4=up4, tt=tt: e.dma_start(
                          out=uT4[:, up4 * 2048:(up4 + 1) * 2048].rearrange(
                              "p (c t) -> p c t", c=16),
                          in_=uTd.rearrange("(c p) t -> p c t", p=128)[
                              :, :, tt * 128:(tt + 1) * 128]),
                      waits=US_W + [("p_wom", S.val("p_wom") - 1)],
                      inc=f"p_ut4{up4}", inc_n=16)
                PUT4 = S.val(f"p_ut4{up4}")
                for ic in range(16):
                    w = []
                    if ic == 0:
                        w = [("p_woTe", WOTE), ("p_cmb", S.val("p_cmb") - 3),
                             ("p_evv", EV_ALL), ("p_evs", EVS_ALL), ("p_uf", UF_ALL),
                             (f"p_ut4{up4}", PUT4)]
                    S.add("tensor",
                          lambda e, bank=bank, ic=ic, up4=up4: e.matmul(
                              bankq(bank),
                              uT4[:, up4 * 2048 + ic * 128: up4 * 2048 + (ic + 1) * 128],
                              woT[:, ic * 512:(ic + 1) * 512],
                              start=(ic == 0), stop=(ic == 15)),
                          waits=w, inc=("p_wom" if ic == 15 else None))
                WOM = S.val("p_wom")

                def combine(e, bank=bank, slot=slot, tt=tt, hs=hs):
                    e.tensor_tensor(out=tmp4[:, slot * 512:(slot + 1) * 512],
                                    in0=bankq(bank), in1=xrl[:, slot * 512:(slot + 1) * 512],
                                    op=ALU.add)
                    e.drain()
                    return e.tensor_tensor(
                        out=osb[:, slot * 512:(slot + 1) * 512],
                        in0=tmp4[:, slot * 512:(slot + 1) * 512],
                        in1=ffn_sb[:, tt * HID + hs * 512: tt * HID + (hs + 1) * 512],
                        op=ALU.add)
                S.add("vector", combine,
                      waits=[("p_wom", WOM), (f"p_xr{slot}", 16 * (nuse + 1)),
                             (f"p_out{slot}", 16 * nuse)],
                      inc="p_cmb")
                S.add("sync" if (hs * NTT + tt) % 2 == 0 else "scalar",
                      lambda e, slot=slot, tt=tt, hs=hs: e.dma_start(
                          out=out_d[tt * 128:(tt + 1) * 128, hs * 512:(hs + 1) * 512],
                          in_=osb[:, slot * 512:(slot + 1) * 512]),
                      waits=[("p_cmb", S.val("p_cmb"))], inc=f"p_out{slot}", inc_n=16)
        OUT_W = [(f"p_out{i}", S.val(f"p_out{i}")) for i in range(4)]
        S.add("sync", lambda e: e.nop(), waits=OUT_W)
        es4.close()

        S.validate()
        with contextlib.ExitStack() as es_sem:
            sems = {n: es_sem.enter_context(nc.semaphore(n)) for n in S.names}
            import os
            if os.environ.get("KDBG"):
                for n, h in sems.items():
                    print("SEMMAP", getattr(h, "index", getattr(h, "id", h)), n)
            with nc.Block() as block:
                block.sync(lambda e: S.emit("sync", e, sems))
                block.scalar(lambda e: S.emit("scalar", e, sems))
                block.vector(lambda e: S.emit("vector", e, sems))
                block.tensor(lambda e: S.emit("tensor", e, sems))
                block.gpsimd(lambda e: S.emit("gpsimd", e, sems))

    meta = dict(S_CORE=S_CORE)
    return nc, meta


def build_consts(S_CORE, qnw, knw, gsc, usc, dsc):
    S_TOT = N_CORES * S_CORE
    NTT = S_CORE // 128
    NKT = S_TOT // 128
    NOC = INTER // 128
    f32 = np.float32
    bf = ml_dtypes.bfloat16
    f8 = ml_dtypes.float8_e4m3fn
    c = {
        "ident_f": np.eye(128, dtype=f32),
        "ident_b": np.eye(128).astype(bf),
        "ones_r": np.ones((1, 128), f32),
        "qgain": np.ascontiguousarray(np.broadcast_to(np.asarray(qnw, f32), (128, 128))),
        "kgain": np.ascontiguousarray(np.broadcast_to(np.asarray(knw, f32), (128, 128))),
        "gscale": np.ascontiguousarray(np.asarray(gsc, f32).reshape(NOC, 128).T),
        "uscale": np.ascontiguousarray(np.asarray(usc, f32).reshape(NOC, 128).T),
        "dscale": np.ascontiguousarray(np.broadcast_to(np.asarray(dsc, f32), (128, HID))),
        "epsv": np.ascontiguousarray(np.broadcast_to(np.array([LN_EPS, RMS_EPS], f32), (128, 2))),
    }
    masks = []
    for core in range(N_CORES):
        m = np.zeros((128, NTT * 1024), f32)
        for qt in range(NTT):
            for gi in range(2):
                for j in range(4):
                    kt = NKT - 8 + gi * 4 + j
                    kgl = kt * 128 + np.arange(128)[:, None]
                    qgl = core * S_CORE + qt * 128 + np.arange(128)[None, :]
                    msk = (kgl >= S_TOT - WINDOW) & (kgl <= qgl)
                    m[:, qt * 1024 + gi * 512 + j * 128: qt * 1024 + gi * 512 + (j + 1) * 128] = \
                        np.where(msk, MASK_NEG, 0.0)
        masks.append(m.astype(f8))
    return c, masks


_CACHE = {}


def kernel(L0_attn_hidden_states, L0_attn_Wq, L0_attn_Wk, L0_attn_Wv, L0_attn_Wo,
           L0_attn_q_norm_w, L0_attn_k_norm_w, L0_ffn_gate_w, L0_ffn_up_w,
           L0_ffn_down_w, L0_ffn_gate_scale, L0_ffn_up_scale, L0_ffn_down_scale):
    x = np.asarray(L0_attn_hidden_states)
    B, L, H = x.shape
    S_CORE = L // N_CORES
    if S_CORE not in _CACHE:
        _CACHE[S_CORE] = build_program(S_CORE)
    nc, meta = _CACHE[S_CORE]
    consts, masks = build_consts(S_CORE, L0_attn_q_norm_w, L0_attn_k_norm_w,
                                 L0_ffn_gate_scale, L0_ffn_up_scale, L0_ffn_down_scale)
    shared = {
        "wq": np.ascontiguousarray(np.asarray(L0_attn_Wq, np.float32)),
        "wk": np.ascontiguousarray(np.asarray(L0_attn_Wk, np.float32)),
        "wv": np.ascontiguousarray(np.asarray(L0_attn_Wv, np.float32)),
        "wo": np.ascontiguousarray(np.asarray(L0_attn_Wo, np.float32)),
        "gate_w": np.ascontiguousarray(np.asarray(L0_ffn_gate_w, np.float32)),
        "up_w": np.ascontiguousarray(np.asarray(L0_ffn_up_w, np.float32)),
        "down_w": np.ascontiguousarray(np.asarray(L0_ffn_down_w, np.float32)),
    }
    shared.update(consts)
    xs = np.ascontiguousarray(x.reshape(L, H).astype(np.float32))
    in_maps = []
    for core in range(N_CORES):
        m = dict(shared)
        m["x"] = xs[core * S_CORE:(core + 1) * S_CORE]
        m["mask"] = masks[core]
        in_maps.append(m)
    res = run_bass_kernel_spmd(nc, in_maps, core_ids=list(range(N_CORES)))
    out = np.concatenate([res.results[c]["out"] for c in range(N_CORES)], axis=0)
    return out.reshape(B, L, H)
